# version 24
# baseline (speedup 1.0000x reference)
import numpy as np
import concourse.bass as bass
import concourse.mybir as mybir
from concourse.bass_utils import run_bass_kernel_spmd
from concourse.bass_core_rust import IndirectOffsetOnAxis
import ml_dtypes
from contextlib import ExitStack

F32 = mybir.dt.float32
BF16 = mybir.dt.bfloat16
I32 = mybir.dt.int32
AF = mybir.ActivationFunctionType
ALU = mybir.AluOpType
AX = mybir.AxisListType

D = 1024
C = 8
NE = 8
DFF = 2048
MHC = 16
INV2PI = 0.15915494309189535
LN2EPS = 1.024e-2

N_CORES = 8
S_FULL = 16384
MOE_CAP = 640



class Buf:
    __slots__ = ("name", "w", "readers")

    def __init__(self, name):
        self.name = name
        self.w = None
        self.readers = []


class DSem:

    def __init__(self, name):
        self.name = name
        self.n = 0
        self.sem = None


class KQ:
    def __init__(self, kb, bname, fifo=False, dma=False, auto_inc=False):
        self.kb = kb
        self.bname = bname
        self.fifo = fifo
        self.dma = dma
        self.auto_inc = auto_inc
        self.n = 0
        self.sem = None
        self._emit_n = 0
        self.seen = {}

    def _merge(self, d):
        waits = {}
        for (q, v) in d:
            if q is self and self.fifo:
                continue
            if self.seen.get(id(q), -1) >= v:
                continue
            if waits.get(id(q), (None, -1))[1] < v:
                waits[id(q)] = (q, v)
        for k, (q, v) in waits.items():
            self.seen[k] = v
        return list(waits.values())

    def op(self, method, *args, reads=(), writes=(), deps=(), inc=False,
           dsem=None, **kw):
        d = []
        for b in reads:
            if b.w is not None:
                d.append(b.w)
        for b in writes:
            if b.w is not None:
                d.append(b.w)
            d.extend(b.readers)
        d.extend(deps)
        if self.dma:
            assert dsem is not None, "DMA ops need a dsem"
            if dsem.n > 0:
                d.append((dsem, dsem.n))
        waits = self._merge(d)
        if self.dma:
            self.kb.rec.append((self, method, args, kw, waits, (dsem, 16)))
            dsem.n += 16
            t = (dsem, dsem.n)
        else:
            myinc = 1 if (self.auto_inc or inc) else 0
            self.kb.rec.append((self, method, args, kw, waits, myinc))
            if myinc:
                self.n += myinc
            t = (self, self.n)
        for b in reads:
            b.readers.append(t)
        for b in writes:
            b.w = t
            b.readers = []
        return t


class KB:
    def __init__(self, nc):
        self.nc = nc
        self.rec = []
        self.pe = KQ(self, "tensor", fifo=True)
        self.dve = KQ(self, "vector", auto_inc=True)
        self.act = KQ(self, "scalar", auto_inc=True)
        self.io = KQ(self, "sync", dma=True)
        self.gio = KQ(self, "gpsimd", dma=True)
        self.pool = KQ(self, "gpsimd")
        self.queues = [self.pe, self.dve, self.act, self.io, self.gio,
                       self.pool]
        self.dsems = []

    def dsem(self, name):
        d = DSem(name)
        self.dsems.append(d)
        return d

    def _all_tags(self):
        tags = [(q, q.n) for q in self.queues if not q.dma and q.n > 0]
        tags += [(d, d.n) for d in self.dsems if d.n > 0]
        return tags

    def barrier(self, skip=()):
        tags = self._all_tags()
        for q in self.queues:
            if q in skip:
                continue
            deps = [t for t in tags if t[0] is not q]
            deps = [(dq, dv) for (dq, dv) in deps
                    if q.seen.get(id(dq), -1) < dv]
            for (dq, dv) in deps:
                q.seen[id(dq)] = dv
            if deps:
                self.rec.append((q, None, None, None, deps, 0))

    def mm_group(self, mms, reads=(), writes=(), deps=()):
        d = []
        for b in reads:
            if b.w is not None:
                d.append(b.w)
        for b in writes:
            if b.w is not None:
                d.append(b.w)
            d.extend(b.readers)
        d.extend(deps)
        waits = self.pe._merge(d)
        for i, (method, args, kw) in enumerate(mms):
            w = waits if i == 0 else []
            inc = 1 if i == len(mms) - 1 else 0
            self.rec.append((self.pe, method, args, kw, w, inc))
        self.pe.n += 1
        t = (self.pe, self.pe.n)
        for b in reads:
            b.readers.append(t)
        for b in writes:
            b.w = t
            b.readers = []
        return t

    def finish(self):
        tags = self._all_tags()
        self.rec.append((self.io, None, None, None, tags, 0))

    def flush(self, block):
        per_q = {}
        for r in self.rec:
            per_q.setdefault(r[0].bname, []).append(r)
        for chunk in per_q.values():
            bname = chunk[0][0].bname

            def emit(eng, chunk=chunk):
                for (q, method, args, kw, waits, inc) in chunk:
                    if method is None:
                        for (dq, dv) in waits:
                            eng.wait_ge(dq.sem, dv)
                        continue
                    allw = list(waits)
                    for (dq, dv) in allw[1:]:
                        eng.wait_ge(dq.sem, dv)
                    ins = getattr(eng, method)(*args, **kw)
                    for (dq, dv) in allw[:1]:
                        ins._wait_ge(dq.sem, dv)
                    if inc:
                        if isinstance(inc, tuple):
                            ins.then_inc(inc[0].sem, inc[1])
                        else:
                            ins.then_inc(q.sem, inc)
                            q._emit_n += inc
            getattr(block, bname)(emit)



def build_program(n_tok, dbg=None):
    assert n_tok % 128 == 0
    NT = n_tok // 128
    NBLK = MOE_CAP // 128
    nc = bass.Bass("TRN2", target_bir_lowering=False)

    ap = {}
    ap["x"] = nc.dram_tensor("x", [n_tok, D], F32, kind="ExternalInput").ap()
    for nm in ["Wk", "Wv", "Wq", "Wkp", "Wqp", "Wo"]:
        ap[nm] = nc.dram_tensor(nm, [D, D], F32, kind="ExternalInput").ap()
    for nm in ["bk", "bv", "bq", "bkp", "bqp", "ps", "lng", "lnb", "bo"]:
        ap[nm] = nc.dram_tensor(nm, [D], F32, kind="ExternalInput").ap()
    ap["gw"] = nc.dram_tensor("gw", [D, NE], F32, kind="ExternalInput").ap()
    ap["w1"] = nc.dram_tensor("w1", [NE, D, DFF], F32, kind="ExternalInput").ap()
    ap["w2"] = nc.dram_tensor("w2", [NE, DFF, D], F32, kind="ExternalInput").ap()
    out = nc.dram_tensor("out", [n_tok, D], F32, kind="ExternalOutput").ap()
    out_flat = out.rearrange("a b -> (a b)")

    ident_c = nc.inline_tensor(np.eye(128, dtype=ml_dtypes.bfloat16), "identc")
    u = np.zeros((128, 128), dtype=np.float32)
    for blk in range(2):
        for a_ in range(64):
            u[blk * 64 + a_, blk * 64 + a_:blk * 64 + 64] = 1.0
    cumu_c = nc.inline_tensor(u.astype(ml_dtypes.bfloat16), "cumuc")
    ones1_c = nc.inline_tensor(np.ones((1, 128), dtype=ml_dtypes.bfloat16), "ones1c")
    onescb_c = nc.inline_tensor(np.ones((128, 1), dtype=ml_dtypes.bfloat16), "onescbc")
    ones1f_c = nc.inline_tensor(np.ones((1, 128), dtype=np.float32), "ones1fc")
    ones11f_c = nc.inline_tensor(np.ones((1, 1), dtype=np.float32), "ones11fc")
    iota8c = nc.inline_tensor(np.arange(8, dtype=np.float32).reshape(8, 1), "iota8c")
    ones8c = nc.inline_tensor(np.ones((8, 1), dtype=np.float32), "ones8c")
    iotac = nc.inline_tensor(
        (np.arange(128)[:, None] + 128 * np.arange(16)[None, :]).astype(np.int32),
        "iotac")

    xn_dram = nc.dram_tensor("xn_dram", [n_tok, D + 8], BF16, kind="Internal").ap()
    idx_dram = nc.dram_tensor("idx_dram", [8 * MOE_CAP, 1], I32, kind="Internal").ap()
    y_dram = nc.dram_tensor("y_dram", [n_tok, D], F32, kind="Internal").ap()
    s_dram = nc.dram_tensor("s_dram", [n_tok, D], F32, kind="Internal").ap()

    kb = KB(nc)
    pe, dve, act, io, gio = kb.pe, kb.dve, kb.act, kb.io, kb.gio

    with ExitStack() as S0:
        E0 = S0.enter_context

        def mk(E, name, shape, dtype):
            t = E(nc.sbuf_tensor(name, shape, dtype))
            return t, Buf(name)

        ident_sb, b_ident = mk(E0, "ident_sb", [128, 128], BF16)
        cumu_sb, b_cumu = mk(E0, "cumu_sb", [128, 128], BF16)
        ones1_sb, b_ones1 = mk(E0, "ones1_sb", [1, 128], BF16)
        onescb_sb, b_onescb = mk(E0, "onescb_sb", [128, 1], BF16)
        ones1f_sb, b_ones1f = mk(E0, "ones1f_sb", [1, 128], F32)
        ones11f_sb, b_ones11f = mk(E0, "ones11f_sb", [1, 1], F32)
        cpool, b_cpool = mk(E0, "cpool", [128, 3], F32)
        idxsb, b_idx = mk(E0, "idxsb", [128, NBLK * 8], I32)

        Wv_sb2, b_wv2 = mk(E0, "Wv_sb", [128, C * D], BF16)
        Wv3 = Wv_sb2.rearrange("p (c n) -> p c n", c=C)
        Wkp_c, b_wkpc = mk(E0, "Wkp_c", [128, C * D], BF16)
        Wkp3 = Wkp_c.rearrange("p (c n) -> p c n", c=C)
        Wqp_c, b_wqpc = mk(E0, "Wqp_c", [128, C * D], BF16)
        Wqp3 = Wqp_c.rearrange("p (c n) -> p c n", c=C)
        Wo_sb2, b_wo2 = mk(E0, "Wo_sb", [128, C * D], BF16)
        Wo3 = Wo_sb2.rearrange("p (c n) -> p c n", c=C)
        bvr_t, b_r65 = mk(E0, "bvr_t", [1, 3 * D], BF16)
        bv_row = bvr_t[0:1, 0:D]
        bkp_row = bvr_t[0:1, D:2 * D]
        bqp_row = bvr_t[0:1, 2 * D:3 * D]
        bor_t, b_r33 = mk(E0, "bor_t", [1, 2 * D], BF16)
        bo_row = bor_t[0:1, 0:D]
        cs_row = bor_t[0:1, D:2 * D]
        psb_col, b_psbc = mk(E0, "psb_col", [128, C], F32)
        psb_bc, b_psbb = mk(E0, "psb_bc", [128, D], F32)

        PS1 = E0(nc.psum_tensor("PS1", [128, 1024], F32))
        PS2 = E0(nc.psum_tensor("PS2", [128, 1024], F32))
        PS3 = E0(nc.psum_tensor("PS3", [128, 1024], F32))
        PS4 = E0(nc.psum_tensor("PS4", [128, 512], F32))
        PSB = E0(nc.psum_tensor("PSB", [128, 1024], BF16))
        bP1a, bP1b, bP2a, bP2b, bP3a, bP3b, bP4, bPB = (
            Buf(x_) for x_ in ["PS1a", "PS1b", "PS2a", "PS2b",
                               "PS3a", "PS3b", "PS4", "PSB"])

        for q, nm in [(pe, "s_pe"), (dve, "s_dve"), (act, "s_act"),
                      (kb.pool, "s_pool")]:
            q.sem = E0(nc.semaphore(nm))
        ds_cst = kb.dsem("cst")
        ds_cs2 = kb.dsem("cs2")
        ds_cs3 = kb.dsem("cs3")
        ds_xt = [kb.dsem("xtA"), kb.dsem("xtB")]
        ds_st = [kb.dsem("stA"), kb.dsem("stB")]
        ds_wa = kb.dsem("wa")
        ds_wp = kb.dsem("wp")
        ds_wv = kb.dsem("wv")
        ds_rb = kb.dsem("rb")
        ds_w1 = [kb.dsem("w1A"), kb.dsem("w1B")]
        ds_w2 = [kb.dsem("w2A"), kb.dsem("w2B")]
        ds_dbg = kb.dsem("dbg")
        ds_xnd = kb.dsem("xnd")
        ds_sc4 = [kb.dsem("sc%d" % i) for i in range(4)]
        ds_sc = ds_sc4[0]
        ds_xg = [kb.dsem("xgA"), kb.dsem("xgB")]
        ds_ys = kb.dsem("ys")
        ds_ysb = kb.dsem("ysb")
        ds_ep = [[kb.dsem("ep%d_%d" % (t_, i)) for i in range(4)]
                 for t_ in range(4)]
        ds_yz = kb.dsem("yz")
        ds_yr = [kb.dsem("yrA"), kb.dsem("yrB")]
        ds_xr = [kb.dsem("xrA"), kb.dsem("xrB")]
        ds_ss = kb.dsem("ss")
        ds_fs = [kb.dsem("fsA"), kb.dsem("fsB")]
        for d_ in kb.dsems:
            d_.sem = E0(nc.semaphore("sd_" + d_.name))

        rb_tok = nc.gpsimd.alloc_register("rb_tok")
        rb_slot = nc.gpsimd.alloc_register("rb_slot")
        kb.pool.op("reg_mov", rb_tok, n_tok - 1)
        kb.pool.op("reg_mov", rb_slot, 8 * MOE_CAP - 1)

        io.op("dma_start", out=ident_sb[:], in_=ident_c.ap(), writes=[b_ident],
              dsem=ds_cst)
        io.op("dma_start", out=cumu_sb[:], in_=cumu_c.ap(), writes=[b_cumu],
              dsem=ds_cs2)
        io.op("dma_start", out=ones1_sb[:], in_=ones1_c.ap(), writes=[b_ones1],
              dsem=ds_cs3)
        io.op("dma_start", out=onescb_sb[:], in_=onescb_c.ap(), writes=[b_onescb],
              dsem=ds_cst)
        io.op("dma_start", out=ones1f_sb[:], in_=ones1f_c.ap(), writes=[b_ones1f],
              dsem=ds_cs2)
        io.op("dma_start", out=ones11f_sb[:], in_=ones11f_c.ap(),
              writes=[b_ones11f], dsem=ds_cs3)
        dve.op("memset", cpool[:, 0:1], -np.pi, writes=[b_cpool])
        dve.op("memset", cpool[:, 1:2], LN2EPS, writes=[b_cpool])
        dve.op("memset", cpool[:, 2:3], -np.pi / 2, writes=[b_cpool])
        NEG_PI = cpool[:, 0:1]
        EPS_B = cpool[0:1, 1:2]
        NEG_HPI = cpool[:, 2:3]

        b_xnd = Buf("xn_dram")
        b_idxd = Buf("idx_dram")
        b_idxd4 = [Buf("idx_dram%d" % i) for i in range(4)]
        b_yd = Buf("y_dram")
        b_sd = [Buf("s_dram%d" % j) for j in range(n_tok // 128)]

        with ExitStack() as SA:
            EA = SA.enter_context
            rows_b, b_rowsb = mk(EA, "rows_b", [1, 3 * D], BF16)
            r_bkp, r_bqp, r_bo = \
                (rows_b[0:1, i * D:(i + 1) * D] for i in range(3))
            ps_row, b_psr = mk(EA, "ps_row", [1, D], F32)
            cols8, b_cols8 = mk(EA, "cols8", [8, 4 * 128], F32)
            c_bk, c_bq, c_lnb, c_lng = \
                (cols8[0:8, i * 128:(i + 1) * 128] for i in range(4))
            wtmp, b_wtmp = mk(EA, "wtmp", [128, C * D], BF16)
            wp, b_wp = mk(EA, "wp", [128, C * D], BF16)
            wtmp3 = wtmp.rearrange("p (c n) -> p c n", c=C)
            wp3 = wp.rearrange("p (c n) -> p c n", c=C)
            colb_s, b_colb = mk(EA, "colb_s", [128, 8], BF16)
            colf_s, b_colf = mk(EA, "colf_s", [128, 8], F32)
            rtmp, b_rtmp = mk(EA, "rtmp", [1, D], F32)
            c8b_s, b_c8b = mk(EA, "c8b_s", [8, 128], BF16)
            xtile, _ = mk(EA, "xtile", [128, 2 * D], F32)
            b_xt2 = [Buf("xtA"), Buf("xtB")]
            sq_s, b_sq = mk(EA, "sq_s", [128, D], F32)
            st_s, _ = mk(EA, "st_s", [128, 16], F32)
            b_st2 = [Buf("stA"), Buf("stB")]
            xnt_s, _ = mk(EA, "xnt_s", [128, 2 * D], BF16)
            b_xnt2p = [Buf("xntpA"), Buf("xntpB")]
            xn_rt, _ = mk(EA, "xn_rt", [128, 2 * C * 128], BF16)
            xn_rt3 = [xn_rt[:, 0:C * 128].rearrange("p (c t) -> p c t", c=C),
                      xn_rt[:, C * 128:].rearrange("p (c t) -> p c t", c=C)]
            b_xrt2 = [Buf("xrtA"), Buf("xrtB")]
            c_rows, b_crows = mk(EA, "c_rows", [8, n_tok], BF16)
            gwb_s, b_gwb = mk(EA, "gwb_s", [128, C * NE], BF16)
            gw_sb, b_gw = mk(EA, "gw_sb", [128, C * NE], F32)
            iota_sb, b_iot = mk(EA, "iota_sb", [128, 16], I32)
            iota8_sb, b_io8 = mk(EA, "iota8_sb", [8, 1], F32)
            ones8_sb, b_on8 = mk(EA, "ones8_sb", [8, 1], F32)
            poscol, b_pc = mk(EA, "poscol", [128, 32], I32)
            lg_s, b_lg = mk(EA, "lg_s", [128, NT * 8], F32)
            mx_s, b_mx = mk(EA, "mx_s", [128, 4 * NT], F32)
            eq_s, _ = mk(EA, "eq_s", [128, 2 * 16], F32)
            b_eq2 = [Buf("eqA"), Buf("eqB")]
            eqb_s, _ = mk(EA, "eqb_s", [128, 16], BF16)
            b_eqb2 = [Buf("eqbA"), Buf("eqbB")]
            ct_s, _ = mk(EA, "ct_s", [128, 16], BF16)
            b_ct2 = [Buf("ctA"), Buf("ctB")]
            eq1em, b_e1m = mk(EA, "eq1em", [8, n_tok], BF16)
            mask_s, b_mask = mk(EA, "mask_s", [8, n_tok], F32)
            incl_s, b_incl = mk(EA, "incl_s", [8, n_tok], F32)
            eq2em, b_e2m = mk(EA, "eq2em", [8, n_tok], F32)
            prod_s, b_prod = mk(EA, "prod_s", [8, n_tok], F32)
            posg, b_posg = mk(EA, "posg", [1, 2 * n_tok], F32)
            carry_s, b_carry = mk(EA, "carry_s", [8, 1], F32)

            for j0 in range(min(2, NT)):
                io.op("dma_start", out=xtile[:, j0 * D:(j0 + 1) * D],
                      in_=ap["x"][j0 * 128:(j0 + 1) * 128, :],
                      writes=[b_xt2[j0]], dsem=ds_xt[j0])

            dve.op("memset", idxsb[:], 10 ** 6, writes=[b_idx])
            gio.op("dma_start",
                   out=idx_dram.rearrange("(b p e) o -> p b (e o)",
                                          p=128, b=NBLK),
                   in_=idxsb.rearrange("p (b e) -> p b e", b=NBLK),
                   reads=[b_idx], writes=[b_idxd] + b_idxd4, dsem=ds_sc)
            gio.op("dma_start", out=wtmp3,
                   in_=ap["Wk"].rearrange("(c p) n -> p c n", p=128),
                   writes=[b_wtmp], dsem=ds_wa)
            gio.op("dma_start", out=wp3,
                   in_=ap["Wkp"].rearrange("(c p) n -> p c n", p=128),
                   writes=[b_wp], dsem=ds_wp)
            for i, nm in enumerate(["bkp", "bqp", "bo"]):
                gio.op("dma_start", out=rows_b[0:1, i * D:(i + 1) * D],
                       in_=ap[nm].rearrange("(o n) -> o n", o=1),
                       writes=[b_rowsb], dsem=ds_rb)
            gio.op("dma_start", out=Wv3,
                   in_=ap["Wv"].rearrange("(c p) n -> p c n", p=128),
                   writes=[b_wv2], dsem=ds_wv)
            gio.op("dma_start", out=bv_row,
                   in_=ap["bv"].rearrange("(o n) -> o n", o=1),
                   writes=[b_r65], dsem=ds_wv)
            io.op("dma_start", out=gw_sb.rearrange("p (c e) -> p c e", c=C),
                  in_=ap["gw"].rearrange("(c p) e -> p c e", p=128),
                  writes=[b_gw], dsem=ds_cst)
            io.op("dma_start", out=iota_sb[:], in_=iotac.ap(), writes=[b_iot],
                  dsem=ds_cs2)
            io.op("dma_start", out=iota8_sb[:], in_=iota8c.ap(), writes=[b_io8],
                  dsem=ds_cs3)
            io.op("dma_start", out=ones8_sb[:], in_=ones8c.ap(), writes=[b_on8],
                  dsem=ds_cs2)
            io.op("dma_start", out=ps_row[:],
                  in_=ap["ps"].rearrange("(o n) -> o n", o=1),
                  writes=[b_psr], dsem=ds_cst)
            for i, nm in enumerate(["bk", "bq", "lnb", "lng"]):
                io.op("dma_start", out=cols8[0:8, i * 128:(i + 1) * 128],
                      in_=ap[nm].rearrange("(c p) -> c p", c=8),
                      writes=[b_cols8], dsem=[ds_cst, ds_cs2, ds_cs3][i % 3])
            dve.op("tensor_copy", out=gwb_s[:], in_=gw_sb[:], reads=[b_gw],
                   writes=[b_gwb])
            gwb3 = gwb_s.rearrange("p (c e) -> p c e", c=C)

            prep_units = []

            def transpose_inplace_units(w3, bw):
                for ci in range(C):
                    def tci(ci=ci, w3=w3, bw=bw):
                        for cf in range(ci, C):
                            mms = [("transpose",
                                    (PSB[:, 0:128],
                                     w3[:, ci, cf * 128:(cf + 1) * 128],
                                     ident_sb[:]), {})]
                            if cf != ci:
                                mms.append(("transpose",
                                            (PSB[:, 128:256],
                                             w3[:, cf, ci * 128:(ci + 1) * 128],
                                             ident_sb[:]), {}))
                            kb.mm_group(mms, reads=[bw, b_ident], writes=[bPB])
                            dve.op("tensor_copy",
                                   out=w3[:, cf, ci * 128:(ci + 1) * 128],
                                   in_=PSB[:, 0:128], reads=[bPB], writes=[bw])
                            if cf != ci:
                                dve.op("tensor_copy",
                                       out=w3[:, ci, cf * 128:(cf + 1) * 128],
                                       in_=PSB[:, 128:256],
                                       reads=[bPB], writes=[bw])
                    prep_units.append(tci)

            def colsum_unit(brow_ap, out_row_bf, b_orow, plain=False):
                def cu():
                    for h in range(2):
                        kb.mm_group(
                            [("matmul", (PS3[0:1, 0:512], colb_s[:, kc:kc + 1],
                                         wp3[:, kc, h * 512:(h + 1) * 512]),
                              dict(start=(kc == 0), stop=(kc == C - 1)))
                             for kc in range(C)],
                            reads=[b_colb, b_wp], writes=[bP3a])
                        if plain:
                            dve.op("tensor_copy",
                                   out=rtmp[0:1, h * 512:(h + 1) * 512],
                                   in_=PS3[0:1, 0:512], reads=[bP3a],
                                   writes=[b_rtmp])
                        else:
                            dve.op("tensor_tensor",
                                   out=rtmp[0:1, h * 512:(h + 1) * 512],
                                   in0=PS3[0:1, 0:512],
                                   in1=brow_ap[0:1, h * 512:(h + 1) * 512],
                                   op=ALU.add, reads=[bP3a, b_rowsb],
                                   writes=[b_rtmp])
                    dve.op("tensor_copy", out=out_row_bf, in_=rtmp[:],
                           reads=[b_rtmp], writes=[b_orow])
                prep_units.append(cu)

            def colb_unit(bcol_ap, also_f32=False):
                def cb():
                    dve.op("tensor_copy", out=c8b_s[:], in_=bcol_ap,
                           reads=[b_cols8], writes=[b_c8b])
                    kb.mm_group([("transpose", (PSB[:, 0:8], c8b_s[:],
                                                ident_sb[0:8, 0:8]), {})],
                                reads=[b_c8b, b_ident], writes=[bPB])
                    dve.op("tensor_copy", out=colb_s[:], in_=PSB[:, 0:8],
                           reads=[bPB], writes=[b_colb])
                    if also_f32:
                        dve.op("tensor_copy", out=colf_s[:], in_=PSB[:, 0:8],
                               reads=[bPB], writes=[b_colf])
                prep_units.append(cb)

            def combine_units(dst3, b_dst, w_name, wp_name, bcol_ap, brow_ap,
                              out_row_bf, b_orow, preloaded):
                if not preloaded:
                    def ld():
                        gio.op("dma_start", out=wtmp3,
                               in_=ap[w_name].rearrange(
                                   "(c p) n -> p c n", p=128),
                               writes=[b_wtmp], dsem=ds_wa)
                        gio.op("dma_start", out=wp3,
                               in_=ap[wp_name].rearrange(
                                   "(c p) n -> p c n", p=128),
                               writes=[b_wp], dsem=ds_wp)
                    prep_units.append(ld)
                transpose_inplace_units(wtmp3, b_wtmp)
                colb_unit(bcol_ap)
                for mc in range(C):
                    def dmc(mc=mc, dst3=dst3, b_dst=b_dst):
                        for h in range(2):
                            kb.mm_group(
                                [("matmul", (PS3[:, h * 512:(h + 1) * 512],
                                             wtmp3[:, kc,
                                                   mc * 128:(mc + 1) * 128],
                                             wp3[:, kc,
                                                 h * 512:(h + 1) * 512]),
                                  dict(start=(kc == 0), stop=(kc == C - 1)))
                                 for kc in range(C)],
                                reads=[b_wtmp, b_wp],
                                writes=[bP3a if h == 0 else bP3b])
                        dve.op("tensor_copy", out=dst3[:, mc, :],
                               in_=PS3[:, 0:1024], reads=[bP3a, bP3b],
                               writes=[b_dst])
                    prep_units.append(dmc)
                colsum_unit(brow_ap, out_row_bf, b_orow)

            combine_units(Wkp3, b_wkpc, "Wk", "Wkp", c_bk, r_bkp,
                          bkp_row, b_r65, True)
            combine_units(Wqp3, b_wqpc, "Wq", "Wqp", c_bq, r_bqp,
                          bqp_row, b_r65, False)

            def wo_ld():
                gio.op("dma_start", out=wp3,
                       in_=ap["Wo"].rearrange("(c p) n -> p c n", p=128),
                       writes=[b_wp], dsem=ds_wp)
            prep_units.append(wo_ld)
            colb_unit(c_lnb)
            colsum_unit(r_bo, bo_row, b_r33)
            colb_unit(c_lng, also_f32=True)
            colsum_unit(None, cs_row, b_r33, plain=True)

            def wo_scale():
                for c in range(C):
                    dve.op("tensor_scalar", out=Wo3[:, c, :], in0=wp3[:, c, :],
                           scalar1=colf_s[:, c:c + 1], scalar2=None,
                           op0=ALU.mult, reads=[b_wp, b_colf], writes=[b_wo2])
            prep_units.append(wo_scale)

            def psb_unit():
                for c in range(C):
                    kb.mm_group(
                        [("matmul", (PS4[:, 8 + c:9 + c],
                                     ps_row[0:1, c * 128:(c + 1) * 128],
                                     ones11f_sb[:]),
                          dict(start=True, stop=True))],
                        reads=[b_psr, b_ones11f], writes=[bP4])
                dve.op("tensor_scalar", out=psb_col[:], in0=PS4[:, 8:8 + C],
                       scalar1=INV2PI, scalar2=None, op0=ALU.mult,
                       reads=[bP4], writes=[b_psbc])
                for h in range(2):
                    kb.mm_group(
                        [("matmul", (PS3[:, h * 512:(h + 1) * 512],
                                     ones1f_sb[:],
                                     ps_row[0:1, h * 512:(h + 1) * 512]),
                          dict(start=True, stop=True))],
                        reads=[b_ones1f, b_psr],
                        writes=[bP3a if h == 0 else bP3b])
                dve.op("tensor_scalar", out=psb_bc[:], in0=PS3[:, 0:1024],
                       scalar1=INV2PI, scalar2=None, op0=ALU.mult,
                       reads=[bP3a, bP3b], writes=[b_psbb])
            prep_units.append(psb_unit)

            pp_i = [0]

            def prep_pump(k=1):
                n = 0
                while pp_i[0] < len(prep_units) and n < k:
                    prep_units[pp_i[0]]()
                    pp_i[0] += 1
                    n += 1

            def p0a(j):
                par = j % 2
                bx = b_xt2[par]
                bst = b_st2[par]
                sb = par * 8
                xt = xtile[:, par * D:(par + 1) * D]
                if j >= 2:
                    io.op("dma_start", out=xt,
                          in_=ap["x"][j * 128:(j + 1) * 128, :],
                          writes=[bx], dsem=ds_xt[par])
                dve.op("reduce_sum", out=st_s[:, sb:sb + 1], in_=xt, axis=AX.X,
                       reads=[bx], writes=[bst])
                act.op("activation", out=sq_s[:], in_=xt, func=AF.Square,
                       accum_out=st_s[:, sb + 1:sb + 2], reads=[bx],
                       writes=[b_sq, bst])
                dve.op("tensor_scalar", out=st_s[:, sb + 2:sb + 3],
                       in0=st_s[:, sb:sb + 1],
                       scalar1=1.0 / D, scalar2=None, op0=ALU.mult,
                       reads=[bst], writes=[bst])
                dve.op("tensor_scalar", out=st_s[:, sb + 3:sb + 4],
                       in0=st_s[:, sb + 1:sb + 2],
                       scalar1=1.0 / D, scalar2=None, op0=ALU.mult,
                       reads=[bst], writes=[bst])
                dve.op("tensor_tensor", out=st_s[:, sb + 4:sb + 5],
                       in0=st_s[:, sb + 2:sb + 3], in1=st_s[:, sb + 2:sb + 3],
                       op=ALU.mult, reads=[bst], writes=[bst])
                dve.op("scalar_tensor_tensor", out=st_s[:, sb + 4:sb + 5],
                       in0=st_s[:, sb + 4:sb + 5],
                       scalar=-1.0, in1=st_s[:, sb + 3:sb + 4],
                       op0=ALU.mult, op1=ALU.add, reads=[bst], writes=[bst])
                dve.op("tensor_scalar", out=st_s[:, sb + 4:sb + 5],
                       in0=st_s[:, sb + 4:sb + 5],
                       scalar1=1e-5, scalar2=None, op0=ALU.add,
                       reads=[bst], writes=[bst])
                act.op("activation", out=st_s[:, sb + 5:sb + 6],
                       in_=st_s[:, sb + 4:sb + 5],
                       func=AF.Sqrt, reads=[bst], writes=[bst])

            def p0b(j):
                par = j % 2
                bx = b_xt2[par]
                bst = b_st2[par]
                sb = par * 8
                xt = xtile[:, par * D:(par + 1) * D]
                xnt = xnt_s[:, par * D:(par + 1) * D]
                dve.op("reciprocal", out=st_s[:, sb + 6:sb + 7],
                       in_=st_s[:, sb + 5:sb + 6], reads=[bst], writes=[bst])
                dve.op("tensor_scalar", out=xnt, in0=xt,
                       scalar1=st_s[:, sb + 2:sb + 3],
                       scalar2=st_s[:, sb + 6:sb + 7],
                       op0=ALU.subtract, op1=ALU.mult,
                       reads=[bx, bst], writes=[b_xnt2p[par]])
                io.op("dma_start", out=xn_dram[j * 128:(j + 1) * 128, 0:D],
                      in_=xnt, reads=[b_xnt2p[par]], writes=[b_xnd],
                      dsem=ds_xnd)
                kb.mm_group(
                    [("transpose", (PSB[:, c * 128:(c + 1) * 128],
                                    xnt[:, c * 128:(c + 1) * 128],
                                    ident_sb[:]), {}) for c in range(C)],
                    reads=[b_xnt2p[par], b_ident], writes=[bPB])
                dve.op("tensor_copy", out=xn_rt3[par],
                       in_=PSB.rearrange("p (c t) -> p c t", c=C),
                       reads=[bPB], writes=[b_xrt2[par]])

            def rt1(j):
                par = j % 2
                beq = b_eq2[par]
                eb = par * 16
                kb.mm_group(
                    [("matmul", (PS4[:, 0:8], xn_rt3[par][:, kc, :],
                                 gwb3[:, kc, :]),
                      dict(start=(kc == 0), stop=(kc == C - 1)))
                     for kc in range(C)],
                    reads=[b_xrt2[par], b_gwb], writes=[bP4])
                dve.op("tensor_copy", out=lg_s[:, j * 8:(j + 1) * 8],
                       in_=PS4[:, 0:8], reads=[bP4], writes=[b_lg])
                dve.op("reduce_max", out=mx_s[:, j:j + 1],
                       in_=lg_s[:, j * 8:(j + 1) * 8], axis=AX.X,
                       reads=[b_lg], writes=[b_mx])
                dve.op("tensor_scalar", out=eq_s[:, eb:eb + 8],
                       in0=lg_s[:, j * 8:(j + 1) * 8],
                       scalar1=mx_s[:, j:j + 1], scalar2=None,
                       op0=ALU.is_equal, reads=[b_lg, b_mx], writes=[beq])
                dve.op("tensor_copy", out=eqb_s[:, par * 8:par * 8 + 8],
                       in_=eq_s[:, eb:eb + 8],
                       reads=[beq], writes=[b_eqb2[par]])
                kb.mm_group([("transpose", (PSB[0:8, 0:128],
                                            eqb_s[:, par * 8:par * 8 + 8],
                                            ident_sb[:]), {})],
                            reads=[b_eqb2[par], b_ident], writes=[bPB])
                dve.op("tensor_copy",
                       out=eq1em[0:8, j * 128:(j + 1) * 128],
                       in_=PSB[0:8, 0:128], reads=[bPB], writes=[b_e1m])
                dve.op("scalar_tensor_tensor", out=eq_s[:, eb + 8:eb + 16],
                       in0=eq_s[:, eb:eb + 8], scalar=-1e30,
                       in1=lg_s[:, j * 8:(j + 1) * 8],
                       op0=ALU.mult, op1=ALU.add,
                       reads=[beq, b_lg], writes=[beq])
                dve.op("reduce_max", out=mx_s[:, NT + j:NT + j + 1],
                       in_=eq_s[:, eb + 8:eb + 16], axis=AX.X, reads=[beq],
                       writes=[b_mx])

            p0a(0)
            if NT > 1:
                p0a(1)
            for j in range(NT):
                p0b(j)
                rt1(j)
                if j + 2 < NT:
                    p0a(j + 2)
                prep_pump(2)

            PCS = min(512, n_tok)
            TPP = PCS // 128

            def idx_piece(p):
                cols = slice(p * PCS, (p + 1) * PCS)
                dve.op("tensor_scalar", out=mask_s[:, cols],
                       in0=c_rows[:, cols],
                       scalar1=0.0, scalar2=None, op0=ALU.is_gt,
                       reads=[b_crows], writes=[b_mask])
                dve.op("tensor_tensor_scan", out=incl_s[:, cols],
                       data0=mask_s[:, cols], data1=mask_s[:, cols],
                       initial=0.0, op0=ALU.add, op1=ALU.max,
                       reads=[b_mask], writes=[b_incl])
                if p > 0:
                    dve.op("tensor_scalar", out=incl_s[:, cols],
                           in0=incl_s[:, cols], scalar1=carry_s[:],
                           scalar2=None, op0=ALU.add,
                           reads=[b_incl, b_carry], writes=[b_incl])
                dve.op("tensor_copy", out=carry_s[:],
                       in_=incl_s[:, (p + 1) * PCS - 1:(p + 1) * PCS],
                       reads=[b_incl], writes=[b_carry])
                dve.op("tensor_tensor", out=incl_s[:, cols],
                       in0=incl_s[:, cols],
                       in1=mask_s[:, cols], op=ALU.subtract,
                       reads=[b_incl, b_mask], writes=[b_incl])
                dve.op("tensor_scalar", out=incl_s[:, cols],
                       in0=incl_s[:, cols],
                       scalar1=8.0, scalar2=iota8_sb[:], op0=ALU.mult,
                       op1=ALU.add, reads=[b_incl, b_io8], writes=[b_incl])
                dve.op("tensor_tensor", out=eq2em[:, cols],
                       in0=mask_s[:, cols],
                       in1=eq1em[:, cols], op=ALU.subtract,
                       reads=[b_mask, b_e1m], writes=[b_e2m])
                for choice in range(2):
                    em = eq1em if choice == 0 else eq2em
                    bem = b_e1m if choice == 0 else b_e2m
                    dve.op("tensor_tensor", out=prod_s[:, cols],
                           in0=incl_s[:, cols], in1=em[:, cols],
                           op=ALU.mult, reads=[b_incl, bem], writes=[b_prod])
                    pb = [bP1a, bP1b, bP2a, bP2b][p % 4]
                    base = [0, 512, 1024, 1536][p % 4]
                    pps = (PS1 if base < 1024 else PS2)[0:1,
                            base % 1024:base % 1024 + PCS]
                    kb.mm_group(
                        [("matmul", (pps, ones8_sb[:], prod_s[0:8, cols]),
                          dict(start=True, stop=True))],
                        reads=[b_on8, b_prod], writes=[pb])
                    dve.op("tensor_copy",
                           out=posg[0:1, choice * n_tok + p * PCS:
                                    choice * n_tok + (p + 1) * PCS],
                           in_=pps, reads=[pb], writes=[b_posg])
                    for jj in range(TPP):
                        j2 = p * TPP + jj
                        k = choice * NT + j2
                        kb.mm_group(
                            [("matmul",
                              (PS4[:, k:k + 1],
                               posg[0:1, choice * n_tok + j2 * 128:
                                    choice * n_tok + (j2 + 1) * 128],
                               ones11f_sb[:]),
                              dict(start=True, stop=True))],
                            reads=[b_posg, b_ones11f], writes=[bP4])
                        dve.op("tensor_copy",
                               out=poscol[:, k:k + 1],
                               in_=PS4[:, k:k + 1],
                               reads=[bP4], writes=[b_pc])
                        gio.op("indirect_dma_start", out=idx_dram,
                               out_offset=IndirectOffsetOnAxis(
                                   poscol[:, k:k + 1], 0),
                               in_=iota_sb[:, (k % NT):(k % NT) + 1],
                               in_offset=None, bounds_check=rb_slot,
                               oob_is_err=False,
                               reads=[b_pc, b_iot],
                               writes=[b_idxd4[k % 4]],
                               dsem=ds_sc4[k % 4])

            dve.op("tensor_tensor", out=mx_s[:, 2 * NT:3 * NT],
                   in0=mx_s[:, 0:NT], in1=mx_s[:, NT:2 * NT],
                   op=ALU.subtract, reads=[b_mx], writes=[b_mx])
            act.op("activation", out=mx_s[:, 2 * NT:3 * NT],
                   in_=mx_s[:, 2 * NT:3 * NT], func=AF.Sigmoid,
                   reads=[b_mx], writes=[b_mx])
            dve.op("tensor_scalar", out=mx_s[:, 3 * NT:4 * NT],
                   in0=mx_s[:, 2 * NT:3 * NT], scalar1=-1.0, scalar2=1.0,
                   op0=ALU.mult, op1=ALU.add, reads=[b_mx], writes=[b_mx])
            for j in range(NT):
                par = j % 2
                beq = b_eq2[par]
                eb = par * 16
                bct = b_ct2[par]
                ct = ct_s[:, par * 8:par * 8 + 8]
                dve.op("tensor_scalar", out=eq_s[:, eb:eb + 8],
                       in0=lg_s[:, j * 8:(j + 1) * 8],
                       scalar1=mx_s[:, j:j + 1],
                       scalar2=mx_s[:, 2 * NT + j:2 * NT + j + 1],
                       op0=ALU.is_equal, op1=ALU.mult,
                       reads=[b_lg, b_mx], writes=[beq])
                dve.op("tensor_scalar", out=eq_s[:, eb + 8:eb + 16],
                       in0=lg_s[:, j * 8:(j + 1) * 8],
                       scalar1=mx_s[:, NT + j:NT + j + 1],
                       scalar2=mx_s[:, 3 * NT + j:3 * NT + j + 1],
                       op0=ALU.is_equal, op1=ALU.mult,
                       reads=[b_lg, b_mx], writes=[beq])
                dve.op("tensor_tensor", out=ct, in0=eq_s[:, eb:eb + 8],
                       in1=eq_s[:, eb + 8:eb + 16], op=ALU.add, reads=[beq],
                       writes=[bct])
                kb.mm_group([("transpose", (PSB[0:8, 0:128], ct,
                                            ident_sb[:]), {})],
                            reads=[bct, b_ident], writes=[bPB])
                dve.op("tensor_copy",
                       out=c_rows[0:8, j * 128:(j + 1) * 128],
                       in_=PSB[0:8, 0:128], reads=[bPB], writes=[b_crows])
                io.op("dma_start",
                      out=xn_dram[j * 128:(j + 1) * 128, D:D + 8],
                      in_=ct, reads=[bct], writes=[b_xnd],
                      dsem=ds_xnd)
                prep_pump(1)
                if (j + 1) % TPP == 0:
                    idx_piece(j // TPP)

            dve.op("memset", sq_s[:], 0.0, writes=[b_sq])
            for j in range(NT):
                io.op("dma_start", out=y_dram[j * 128:(j + 1) * 128, :],
                      in_=sq_s[:], reads=[b_sq], writes=[b_yd], dsem=ds_yz)

            if dbg == "c":
                gio.op("dma_start",
                       out=out_flat[0:8 * n_tok].rearrange(
                           "(p t) -> p t", p=8),
                       in_=c_rows[:], reads=[b_crows], dsem=ds_dbg)
                kb.finish()
                with nc.Block() as block:
                    kb.flush(block)
                return nc

            gio.op("dma_start",
                   out=idxsb.rearrange("p (b e) -> p b e", b=NBLK),
                   in_=idx_dram.rearrange("(b p e) o -> p b (e o)",
                                          p=128, b=NBLK),
                   reads=b_idxd4, writes=[b_idx], dsem=ds_sc)

            while pp_i[0] < len(prep_units):
                prep_units[pp_i[0]]()
                pp_i[0] += 1

        kb.barrier()

        with ExitStack() as SB:
            EB = SB.enter_context
            xg_tok, b_xg = mk(EB, "xg_tok", [128, NBLK * (D + 8)], BF16)
            xg3 = xg_tok.rearrange("p (b n) -> p b n", b=NBLK)
            xg_fm, b_xgf = mk(EB, "xg_fm", [128, C * MOE_CAP], BF16)
            xf3 = xg_fm.rearrange("p (c t) -> p c t", c=C)
            yb3 = xf3
            h_sb, b_h = mk(EB, "h_sb", [128, MHC * MOE_CAP], BF16)
            h3 = h_sb.rearrange("p (m t) -> p m t", m=MHC)
            ysc_s, _ = mk(EB, "ysc_s", [128, 2 * D], F32)
            b_ysc2 = [Buf("yscA"), Buf("yscB")]
            ccol_s, _ = mk(EB, "ccol_s", [128, 2 * NBLK], F32)
            b_cc2 = [Buf("ccA"), Buf("ccB")]
            w1a, b_w1a = mk(EB, "w1a", [128, C * 256], BF16)
            w1b, b_w1b = mk(EB, "w1b", [128, C * 256], BF16)
            w2a, b_w2a = mk(EB, "w2a", [128, MHC * 128], BF16)
            w2b, b_w2b = mk(EB, "w2b", [128, MHC * 128], BF16)
            w1bufs = [(w1a.rearrange("p (c n) -> p c n", c=C), b_w1a),
                      (w1b.rearrange("p (c n) -> p c n", c=C), b_w1b)]
            w2bufs = [(w2a.rearrange("p (m n) -> p m n", m=MHC), b_w2a),
                      (w2b.rearrange("p (m n) -> p m n", m=MHC), b_w2b)]
            NP = [(0, 512), (512, MOE_CAP - 512)]
            tA_s, b_tA = mk(EB, "tA_s", [128, D], F32)
            tA3 = tA_s.rearrange("p (c t) -> p c t", c=C)
            tB_s, b_tB = mk(EB, "tB_s", [128, D], F32)
            ni_s, b_ni = mk(EB, "ni_s", [128, D], I32)
            sin2_s, _bs = mk(EB, "sin2_s", [128, 2 * D], BF16)
            cos2_s, _bc = mk(EB, "cos2_s", [128, 2 * D], BF16)
            sink2_s, _sk = mk(EB, "sink2_s", [128, 2 * D], BF16)
            cosk2_s, _ck = mk(EB, "cosk2_s", [128, 2 * D], BF16)
            sink_p = [sink2_s[:, 0:D], sink2_s[:, D:2 * D]]
            cosk_p = [cosk2_s[:, 0:D], cosk2_s[:, D:2 * D]]
            b_sink_p = [Buf("sinkA"), Buf("sinkB")]
            b_cosk_p = [Buf("coskA"), Buf("coskB")]
            sin_p = [sin2_s[:, 0:D], sin2_s[:, D:2 * D]]
            cos_p = [cos2_s[:, 0:D], cos2_s[:, D:2 * D]]
            b_sin_p = [Buf("sinA"), Buf("sinB")]
            b_cos_p = [Buf("cosA"), Buf("cosB")]
            vret_s, b_vret = mk(EB, "vret_s", [128, D], BF16)
            mrq_s, b_mrq = mk(EB, "mrq_s", [128, D], BF16)
            memi_s, b_memi = mk(EB, "memi_s", [128, D], BF16)
            rs_s, b_rs = mk(EB, "rs_s", [128, D], BF16)
            xn_t, _ = mk(EB, "xn_t", [128, 2 * C * 128], BF16)
            xn_t3 = [xn_t[:, 0:C * 128].rearrange("p (c t) -> p c t", c=C),
                     xn_t[:, C * 128:].rearrange("p (c t) -> p c t", c=C)]
            b_xnt2 = [Buf("xntA"), Buf("xntB")]
            xnt_tok, _ = mk(EB, "xnt_tok", [128, 2 * D], BF16)
            b_xtok2 = [Buf("xtokA"), Buf("xtokB")]
            row_s, b_row = mk(EB, "row_s", [1, 6 * 128], F32)
            rbf_s, b_rbf = mk(EB, "rbf_s", [1, 2 * 128], BF16)
            s_stage, b_sst = mk(EB, "s_stage", [128, D], F32)

            def trig(kind, par, kp=False):
                if kp:
                    b_sinv, b_cosv = b_sink_p[par], b_cosk_p[par]
                    sin_ap, cos_ap = sink_p[par], cosk_p[par]
                else:
                    b_sinv, b_cosv = b_sin_p[par], b_cos_p[par]
                    sin_ap, cos_ap = sin_p[par], cos_p[par]
                if kind == "bcast":
                    dve.op("tensor_tensor", out=tA_s[:], in0=tA_s[:],
                           in1=psb_bc[:], op=ALU.mult,
                           reads=[b_tA, b_psbb], writes=[b_tA])
                else:
                    dve.op("tensor_tensor", out=tA3, in0=tA3,
                           in1=psb_col[:, :].unsqueeze(2).broadcast_to(
                               [128, C, 128]),
                           op=ALU.mult, reads=[b_tA, b_psbc], writes=[b_tA])
                dve.op("tensor_copy", out=ni_s[:], in_=tA_s[:],
                       reads=[b_tA], writes=[b_ni])
                dve.op("tensor_copy", out=tB_s[:], in_=ni_s[:],
                       reads=[b_ni], writes=[b_tB])
                dve.op("tensor_tensor", out=tA_s[:], in0=tA_s[:], in1=tB_s[:],
                       op=ALU.subtract, reads=[b_tA, b_tB], writes=[b_tA])
                dve.op("tensor_scalar", out=tB_s[:], in0=tA_s[:], scalar1=0.0,
                       scalar2=None, op0=ALU.is_lt, reads=[b_tA], writes=[b_tB])
                dve.op("scalar_tensor_tensor", out=tA_s[:], in0=tB_s[:],
                       scalar=1.0, in1=tA_s[:], op0=ALU.mult, op1=ALU.add,
                       reads=[b_tA, b_tB], writes=[b_tA])
                act.op("activation", out=sin_ap, in_=tA_s[:], func=AF.Sin,
                       bias=NEG_PI, scale=2 * np.pi,
                       reads=[b_tA, b_cpool], writes=[b_sinv])
                dve.op("tensor_scalar", out=tB_s[:], in0=tA_s[:], scalar1=0.75,
                       scalar2=None, op0=ALU.is_ge, reads=[b_tA], writes=[b_tB])
                dve.op("scalar_tensor_tensor", out=tA_s[:], in0=tB_s[:],
                       scalar=-1.0, in1=tA_s[:], op0=ALU.mult, op1=ALU.add,
                       reads=[b_tA, b_tB], writes=[b_tA])
                act.op("activation", out=cos_ap, in_=tA_s[:], func=AF.Sin,
                       bias=NEG_HPI, scale=2 * np.pi,
                       reads=[b_tA, b_cpool], writes=[b_cosv])

            def xt_load(j):
                par = j % 2
                jt = slice(j * 128, (j + 1) * 128)
                io.op("dma_start", out=xnt_tok[:, par * D:(par + 1) * D],
                      in_=xn_dram[jt, 0:D], reads=[b_xnd],
                      writes=[b_xtok2[par]], dsem=ds_xr[par])

            def u_xt(j):
                par = j % 2
                if j + 1 < NT:
                    xt_load(j + 1)
                kb.mm_group(
                    [("transpose", (PSB[:, c * 128:(c + 1) * 128],
                                    xnt_tok[:, par * D + c * 128:
                                            par * D + (c + 1) * 128],
                                    ident_sb[:]), {}) for c in range(C)],
                    reads=[b_xtok2[par], b_ident], writes=[bPB])
                act.op("activation", out=xn_t3[par],
                       in_=PSB.rearrange("p (c t) -> p c t", c=C),
                       func=AF.Copy, reads=[bPB], writes=[b_xnt2[par]])

            def u_v(j):
                par = j % 2
                for h in range(2):
                    hs = slice(h * 512, (h + 1) * 512)
                    kb.mm_group(
                        [("matmul", (PS3[:, hs], ones1_sb[:], bv_row[0:1, hs]),
                          dict(start=True, stop=False))] +
                        [("matmul", (PS3[:, hs], xn_t3[par][:, kc, :],
                                     Wv3[:, kc, hs]),
                          dict(start=False, stop=(kc == C - 1)))
                         for kc in range(C)],
                        reads=[b_ones1, b_r65, b_xnt2[par], b_wv2],
                        writes=[bP3a if h == 0 else bP3b])
                act.op("activation", out=vret_s[:], in_=PS3[:, 0:1024],
                       func=AF.Copy, reads=[bP3a, bP3b], writes=[b_vret])

            def u_kp(j):
                par = j % 2
                for h in range(2):
                    hs = slice(h * 512, (h + 1) * 512)
                    kb.mm_group(
                        [("matmul", (PS3[:, hs], ones1_sb[:],
                                     bkp_row[0:1, hs]),
                          dict(start=True, stop=False))] +
                        [("matmul", (PS3[:, hs], xn_t3[par][:, kc, :],
                                     Wkp3[:, kc, hs]),
                          dict(start=False, stop=(kc == C - 1)))
                         for kc in range(C)],
                        reads=[b_ones1, b_r65, b_xnt2[par], b_wkpc],
                        writes=[bP3a if h == 0 else bP3b])
                act.op("activation", out=tA_s[:], in_=PS3[:, 0:1024],
                       func=AF.Tanh, reads=[bP3a, bP3b], writes=[b_tA])
                trig("bcast", par, kp=True)
                dve.op("tensor_tensor", out=cosk_p[par], in0=cosk_p[par],
                       in1=vret_s[:], op=ALU.mult,
                       reads=[b_cosk_p[par], b_vret], writes=[b_cosk_p[par]])
                dve.op("tensor_tensor", out=sink_p[par], in0=sink_p[par],
                       in1=vret_s[:], op=ALU.mult,
                       reads=[b_sink_p[par], b_vret], writes=[b_sink_p[par]])

            def u_qp(j):
                par = j % 2
                for c in range(C):
                    cc = slice(c * 128, (c + 1) * 128)
                    kb.mm_group(
                        [("matmul", (PS3[:, cc], bqp_row[0:1, cc],
                                     ones1_sb[:]),
                          dict(start=True, stop=False))] +
                        [("matmul", (PS3[:, cc],
                                     Wqp3[:, kc, cc], xn_t3[par][:, kc, :]),
                          dict(start=False, stop=(kc == C - 1)))
                         for kc in range(C)],
                        reads=[b_r65, b_ones1, b_wqpc, b_xnt2[par]],
                        writes=[bP3a if c < 4 else bP3b])
                act.op("activation", out=tA_s[:], in_=PS3[:, 0:1024],
                       func=AF.Tanh, reads=[bP3a, bP3b], writes=[b_tA])
                trig("col", par)

            def u_t1r(j):
                par = j % 2
                kb.mm_group(
                    [("matmul", (PS3[:, c * 128:(c + 1) * 128],
                                 cosk_p[par][:, c * 128:(c + 1) * 128],
                                 cumu_sb[:]),
                      dict(start=True, stop=True)) for c in range(C)],
                    reads=[b_cosk_p[par], b_cumu], writes=[bP3a, bP3b])
                act.op("activation", out=mrq_s[:], in_=PS3[:, 0:1024],
                       func=AF.Copy, reads=[bP3a, bP3b], writes=[b_mrq])

            def u_t1i(j):
                par = j % 2
                kb.mm_group(
                    [("matmul", (PS3[:, c * 128:(c + 1) * 128],
                                 sink_p[par][:, c * 128:(c + 1) * 128],
                                 cumu_sb[:]),
                      dict(start=True, stop=True)) for c in range(C)],
                    reads=[b_sink_p[par], b_cumu], writes=[bP3a, bP3b])
                act.op("activation", out=memi_s[:], in_=PS3[:, 0:1024],
                       func=AF.Copy, reads=[bP3a, bP3b], writes=[b_memi])

            def u_t2a(j):
                par = j % 2
                dve.op("tensor_tensor", out=rs_s[:], in0=cos_p[par],
                       in1=mrq_s[:], op=ALU.mult,
                       reads=[b_cos_p[par], b_mrq], writes=[b_rs])
                dve.op("tensor_tensor", out=mrq_s[:], in0=sin_p[par],
                       in1=memi_s[:], op=ALU.mult,
                       reads=[b_sin_p[par], b_memi], writes=[b_mrq])
                dve.op("tensor_tensor", out=rs_s[:], in0=rs_s[:],
                       in1=mrq_s[:], op=ALU.add,
                       reads=[b_rs, b_mrq], writes=[b_rs])
                act.op("activation", out=mrq_s[:], in_=rs_s[:],
                       func=AF.Square, reads=[b_rs], writes=[b_mrq])
                kb.mm_group(
                    [("matmul", (PS4[0:1, 0:128], onescb_sb[:],
                                 rs_s[:, c * 128:(c + 1) * 128]),
                      dict(start=(c == 0), stop=(c == C - 1))) for c in range(C)],
                    reads=[b_onescb, b_rs], writes=[bP4])
                kb.mm_group(
                    [("matmul", (PS4[0:1, 128:256], onescb_sb[:],
                                 mrq_s[:, c * 128:(c + 1) * 128]),
                      dict(start=(c == 0), stop=(c == C - 1))) for c in range(C)],
                    reads=[b_onescb, b_mrq], writes=[bP4])
                dve.op("tensor_scalar", out=row_s[0:1, 0:256],
                       in0=PS4[0:1, 0:256], scalar1=1.0 / D, scalar2=None,
                       op0=ALU.mult, reads=[bP4], writes=[b_row])
                dve.op("tensor_tensor", out=row_s[0:1, 256:384],
                       in0=row_s[0:1, 0:128], in1=row_s[0:1, 0:128],
                       op=ALU.mult, reads=[b_row], writes=[b_row])
                dve.op("scalar_tensor_tensor", out=row_s[0:1, 256:384],
                       in0=row_s[0:1, 256:384], scalar=-1.0,
                       in1=row_s[0:1, 128:256], op0=ALU.mult, op1=ALU.add,
                       reads=[b_row], writes=[b_row])
                act.op("activation", out=row_s[0:1, 384:512],
                       in_=row_s[0:1, 256:384], func=AF.Sqrt, bias=EPS_B,
                       reads=[b_row, b_cpool], writes=[b_row])
                dve.op("reciprocal", out=row_s[0:1, 512:640],
                       in_=row_s[0:1, 384:512], reads=[b_row], writes=[b_row])
                dve.op("scalar_tensor_tensor", out=row_s[0:1, 640:768],
                       in0=row_s[0:1, 0:128], scalar=-1.0,
                       in1=row_s[0:1, 512:640], op0=ALU.mult, op1=ALU.mult,
                       reads=[b_row], writes=[b_row])
                dve.op("tensor_copy", out=rbf_s[:, 0:256],
                       in_=row_s[0:1, 512:768], reads=[b_row], writes=[b_rbf])

            def u_t2b(j):
                par = j % 2
                jt = slice(j * 128, (j + 1) * 128)
                kb.mm_group(
                    [("matmul", (PS4[:, 256:384], ones1_sb[:],
                                 rbf_s[:, 0:128]), dict(start=True, stop=True))],
                    reads=[b_ones1, b_rbf], writes=[bP4])
                dve.op("tensor_tensor",
                       out=rs_s.rearrange("p (c t) -> p c t", c=C),
                       in0=rs_s.rearrange("p (c t) -> p c t", c=C),
                       in1=PS4[:, 256:384].unsqueeze(1).broadcast_to(
                           [128, C, 128]),
                       op=ALU.mult, reads=[b_rs, bP4], writes=[b_rs])
                for h in range(2):
                    hs = slice(h * 512, (h + 1) * 512)
                    mms = [("matmul", (PS3[:, hs], rbf_s[:, 128:256],
                                       cs_row[0:1, hs]),
                            dict(start=True, stop=False)),
                           ("matmul", (PS3[:, hs], ones1_sb[:],
                                       bo_row[0:1, hs]),
                            dict(start=False, stop=False))]
                    for i in range(4):
                        c2 = 4 * h + i
                        mms.append(("matmul",
                                    (PS3[:, c2 * 128:(c2 + 1) * 128],
                                     xn_t3[par][:, c2, :], ident_sb[:]),
                                    dict(start=False, stop=False)))
                    mms += [("matmul", (PS3[:, hs],
                                        rs_s[:, kc * 128:(kc + 1) * 128],
                                        Wo3[:, kc, hs]),
                             dict(start=False, stop=(kc == C - 1)))
                            for kc in range(C)]
                    kb.mm_group(
                        mms,
                        reads=[b_rbf, b_r33, b_ones1, b_xnt2[par], b_ident,
                               b_rs, b_wo2],
                        writes=[bP3a if h == 0 else bP3b])
                act.op("activation", out=s_stage[:], in_=PS3[:, 0:1024],
                       func=AF.Copy, reads=[bP3a, bP3b], writes=[b_sst])
                io.op("dma_start", out=s_dram[jt, :], in_=s_stage[:],
                      reads=[b_sst], writes=[b_sd[j]], dsem=ds_ss)

            units = []
            units += [lambda: xt_load(0), lambda j=0: u_xt(j),
                      lambda j=0: u_qp(j), lambda j=0: u_v(j),
                      lambda j=0: u_kp(j)]
            if NT > 1:
                units += [lambda j=1: u_xt(j), lambda j=1: u_qp(j),
                          lambda j=1: u_v(j)]
            for j in range(2, NT):
                units += [lambda j=j: u_t1r(j - 2), lambda j=j: u_t1i(j - 2),
                          lambda j=j: u_t2a(j - 2), lambda j=j: u_t2b(j - 2),
                          lambda j=j: u_kp(j - 1),
                          lambda j=j: u_xt(j), lambda j=j: u_qp(j),
                          lambda j=j: u_v(j)]
            units += [lambda j=NT - 1: u_kp(j)]
            for j in (NT - 2, NT - 1):
                units += [lambda j=j: u_t1r(j), lambda j=j: u_t1i(j),
                          lambda j=j: u_t2a(j), lambda j=j: u_t2b(j)]

            total_steps = NE * (1 + NBLK + MHC + C + NBLK)
            stc = [0, 0]

            def stepv(pump=True):
                stc[0] += 1
                if not pump:
                    return
                tgt = len(units) * stc[0] // total_steps
                if stc[1] < min(tgt, len(units)):
                    units[stc[1]]()
                    stc[1] += 1

            def w1_load(e_, k):
                buf3, bb = w1bufs[k % 2]
                gio.op("dma_start", out=buf3,
                       in_=ap["w1"][e_].rearrange("(c p) h -> p c h", p=128)
                       [:, :, k * 256:(k + 1) * 256],
                       writes=[bb], dsem=ds_w1[k % 2])

            def w2_load(e_, fo):
                buf3, bb = w2bufs[fo % 2]
                gio.op("dma_start", out=buf3,
                       in_=ap["w2"][e_].rearrange("(m p) f -> p m f", p=128)
                       [:, :, fo * 128:(fo + 1) * 128],
                       writes=[bb], dsem=ds_w2[fo % 2])

            def gather_blocks(e_, blist):
                for b_ in blist:
                    gio.op("indirect_dma_start", out=xg3[:, b_, :],
                           out_offset=None, in_=xn_dram,
                           in_offset=IndirectOffsetOnAxis(
                               idxsb[:, b_ * 8 + e_:b_ * 8 + e_ + 1], 0),
                           bounds_check=rb_tok, oob_is_err=False,
                           reads=[b_idx, b_xnd], writes=[b_xg],
                           dsem=ds_xg[b_ % 2])

            def ccol_copy(e_):
                cpar = e_ % 2
                dve.op("tensor_copy",
                       out=ccol_s[:, cpar * NBLK:(cpar + 1) * NBLK],
                       in_=xg3[:, :, D + e_], reads=[b_xg],
                       writes=[b_cc2[cpar]])

            def ti_block(b_):
                kb.mm_group(
                    [("transpose", (PSB[:, c * 128:(c + 1) * 128],
                                    xg3[:, b_, c * 128:(c + 1) * 128],
                                    ident_sb[:]), {}) for c in range(C)],
                    reads=[b_xg, b_ident], writes=[bPB])
                act.op("activation",
                       out=xf3[:, :, b_ * 128:(b_ + 1) * 128],
                       in_=PSB.rearrange("p (c t) -> p c t", c=C),
                       func=AF.Copy, reads=[bPB], writes=[b_xgf])

            gather_blocks(0, range(NBLK))
            w1_load(0, 0)
            w1_load(0, 1)
            ccol_copy(0)
            stepv()
            for b_ in range(NBLK):
                ti_block(b_)
                stepv()
            ds_ys2 = [ds_ys, ds_ysb]
            sc_last = [None, None]
            for e_ in range(NE):
                for m in range(MHC):
                    k8 = m // 2
                    buf3, bb = w1bufs[k8 % 2]
                    pp, pbs = ((PS1, (bP1a, bP1b)) if m % 2 == 0
                               else (PS2, (bP2a, bP2b)))
                    kb.mm_group(
                        [("matmul", (pp[:, o_:o_ + n_],
                                     buf3[:, kc, (m % 2) * 128:(m % 2 + 1) * 128],
                                     xf3[:, kc, o_:o_ + n_]),
                          dict(start=(kc == 0), stop=(kc == C - 1)))
                         for (o_, n_) in NP for kc in range(C)],
                        reads=[bb, b_xgf], writes=list(pbs))
                    act.op("activation", out=h3[:, m, :],
                           in_=pp[:, 0:MOE_CAP], func=AF.Silu,
                           reads=list(pbs), writes=[b_h])
                    if m % 2 == 1 and k8 + 2 <= 7:
                        w1_load(e_, k8 + 2)
                    if m == 13:
                        w2_load(e_, 0)
                    if m == 15:
                        w2_load(e_, 1)
                    stepv()
                for fo in range(C):
                    buf3, bb = w2bufs[fo % 2]
                    ypp, ypbs = ((PS1, (bP1a, bP1b)) if fo % 2 == 0
                                 else (PS2, (bP2a, bP2b)))
                    kb.mm_group(
                        [("matmul", (ypp[:, o_:o_ + n_],
                                     buf3[:, m, :],
                                     h3[:, m, o_:o_ + n_]),
                          dict(start=(m == 0), stop=(m == MHC - 1)))
                         for (o_, n_) in NP for m in range(MHC)],
                        reads=[bb, b_h], writes=list(ypbs))
                    act.op("activation", out=yb3[:, fo, :],
                           in_=ypp[:, 0:MOE_CAP], func=AF.Copy,
                           reads=list(ypbs), writes=[b_xgf])
                    if fo + 2 < C:
                        w2_load(e_, fo + 2)
                    if e_ + 1 < NE and fo in (1, 3, 5):
                        gather_blocks(e_ + 1,
                                      [b for b in (fo - 1, fo) if b < NBLK])
                    if e_ + 1 < NE and fo == 6:
                        w1_load(e_ + 1, 0)
                    if e_ + 1 < NE and fo == 7:
                        w1_load(e_ + 1, 1)
                    stepv()
                if e_ + 1 < NE:
                    ccol_copy(e_ + 1)
                    stepv()
                for b_ in range(NBLK):
                    ypar = b_ % 2
                    ysv = ysc_s[:, ypar * D:(ypar + 1) * D]
                    kb.mm_group(
                        [("transpose", (PSB[:, c * 128:(c + 1) * 128],
                                        yb3[:, c, b_ * 128:(b_ + 1) * 128],
                                        ident_sb[:]), {}) for c in range(C)],
                        reads=[b_xgf, b_ident], writes=[bPB])
                    dve.op("tensor_scalar", out=ysv, in0=PSB[:, 0:1024],
                           scalar1=ccol_s[:, (e_ % 2) * NBLK + b_:
                                          (e_ % 2) * NBLK + b_ + 1],
                           scalar2=None,
                           op0=ALU.mult, reads=[bPB, b_cc2[e_ % 2]],
                           writes=[b_ysc2[ypar]])
                    deps = []
                    if b_ < 2 and sc_last[1 - ypar] is not None:
                        deps = [sc_last[1 - ypar]]
                    gio.op("indirect_dma_start", out=y_dram,
                           out_offset=IndirectOffsetOnAxis(
                               idxsb[:, b_ * 8 + e_:b_ * 8 + e_ + 1], 0),
                           in_=ysv, in_offset=None,
                           bounds_check=rb_tok, oob_is_err=False,
                           compute_op=ALU.add, deps=deps,
                           reads=[b_ysc2[ypar], b_idx], writes=[b_yd],
                           dsem=ds_ys2[ypar])
                    stepv(pump=False)
                    if e_ + 1 < NE:
                        ti_block(b_)
                        stepv(pump=False)
                sc_last = [(ds_ys2[0], ds_ys2[0].n),
                           (ds_ys2[1], ds_ys2[1].n)]

            while stc[1] < len(units):
                units[stc[1]]()
                stc[1] += 1

            if dbg == "y":
                for j in range(NT):
                    io.op("dma_start", out=out[j * 128:(j + 1) * 128, :],
                          in_=y_dram[j * 128:(j + 1) * 128, :],
                          reads=[b_yd], dsem=ds_st[j % 2])
                kb.finish()
                with nc.Block() as block:
                    kb.flush(block)
                return nc

        kb.barrier()

        with ExitStack() as SC:
            EC = SC.enter_context
            s_t, _ = mk(EC, "s_t", [128, 4 * D], F32)
            b_s2 = [Buf("s%d" % i) for i in range(4)]
            y_t, _ = mk(EC, "y_t", [128, 4 * D], F32)
            b_y2 = [Buf("y%d" % i) for i in range(4)]
            x_t, _ = mk(EC, "x_t", [128, 4 * D], F32)
            b_x2 = [Buf("x%d" % i) for i in range(4)]
            g_te, _ = mk(EC, "g_te", [128, 4 * D], BF16)
            b_g2 = [Buf("g%d" % i) for i in range(4)]
            for j in range(NT):
                par = j % 4
                jt = slice(j * 128, (j + 1) * 128)
                sv = s_t[:, par * D:(par + 1) * D]
                yv = y_t[:, par * D:(par + 1) * D]
                xv = x_t[:, par * D:(par + 1) * D]
                gv = g_te[:, par * D:(par + 1) * D]
                io.op("dma_start", out=sv, in_=s_dram[jt, :],
                      reads=[b_sd[j]], writes=[b_s2[par]], dsem=ds_ep[0][par])
                gio.op("dma_start", out=yv, in_=y_dram[jt, :], reads=[b_yd],
                       writes=[b_y2[par]], dsem=ds_ep[1][par])
                gio.op("dma_start", out=xv, in_=ap["x"][jt, :],
                       writes=[b_x2[par]], dsem=ds_ep[2][par])
                act.op("activation", out=gv, in_=yv, func=AF.Silu,
                       reads=[b_y2[par]], writes=[b_g2[par]])
                dve.op("tensor_tensor", out=sv, in0=sv, in1=gv,
                       op=ALU.mult, reads=[b_s2[par], b_g2[par]],
                       writes=[b_s2[par]])
                dve.op("tensor_tensor", out=xv, in0=xv, in1=sv,
                       op=ALU.add, reads=[b_x2[par], b_s2[par]],
                       writes=[b_x2[par]])
                io.op("dma_start", out=out[jt, :], in_=xv,
                      reads=[b_x2[par]], dsem=ds_ep[3][par])

        kb.finish()
        with nc.Block() as block:
            kb.flush(block)
        return nc



_PROG_CACHE = {}


def _get_prog(n_tok):
    if n_tok not in _PROG_CACHE:
        _PROG_CACHE[n_tok] = build_program(n_tok)
    return _PROG_CACHE[n_tok]


def kernel(**inputs):
    x = np.ascontiguousarray(
        np.asarray(inputs["L0_attn_x"], dtype=np.float32)).reshape(S_FULL, D)
    w = {
        "Wk": inputs["L0_attn_Wk"], "Wv": inputs["L0_attn_Wv"],
        "Wq": inputs["L0_attn_Wq"], "Wkp": inputs["L0_attn_Wkp"],
        "Wqp": inputs["L0_attn_Wqp"], "Wo": inputs["L0_attn_Wo"],
        "bk": inputs["L0_attn_bk"], "bv": inputs["L0_attn_bv"],
        "bq": inputs["L0_attn_bq"], "bkp": inputs["L0_attn_bkp"],
        "bqp": inputs["L0_attn_bqp"], "ps": inputs["L0_attn_phase_scale"],
        "lng": inputs["L0_attn_ln_g"], "lnb": inputs["L0_attn_ln_b"],
        "bo": inputs["L0_attn_bo"], "gw": inputs["L0_ffn_gate_w"],
        "w1": inputs["L0_ffn_w1"], "w2": inputs["L0_ffn_w2"],
    }
    w = {k: np.ascontiguousarray(np.asarray(v, dtype=np.float32))
         for k, v in w.items()}
    n_per = S_FULL // N_CORES
    nc = _get_prog(n_per)
    in_maps = []
    for c in range(N_CORES):
        m = {"x": x[c * n_per:(c + 1) * n_per]}
        m.update(w)
        in_maps.append(m)
    res = run_bass_kernel_spmd(nc, in_maps, core_ids=list(range(N_CORES)))
    outs = [np.asarray(res.results[c]["out"]) for c in range(N_CORES)]
    return np.concatenate(outs, axis=0).reshape(1, S_FULL, D)


# version 25
# speedup vs baseline: 1.0778x; 1.0778x over previous
import numpy as np
import concourse.bass as bass
import concourse.mybir as mybir
from concourse.bass_utils import run_bass_kernel_spmd
from concourse.bass_core_rust import IndirectOffsetOnAxis
import ml_dtypes
from contextlib import ExitStack

F32 = mybir.dt.float32
BF16 = mybir.dt.bfloat16
I32 = mybir.dt.int32
AF = mybir.ActivationFunctionType
ALU = mybir.AluOpType
AX = mybir.AxisListType

D = 1024
C = 8
NE = 8
DFF = 2048
MHC = 16
INV2PI = 0.15915494309189535
LN2EPS = 1.024e-2

N_CORES = 8
S_FULL = 16384
MOE_CAP = 640



class Buf:
    __slots__ = ("name", "w", "readers")

    def __init__(self, name):
        self.name = name
        self.w = None
        self.readers = []


class DSem:

    def __init__(self, name):
        self.name = name
        self.n = 0
        self.sem = None


class KQ:
    def __init__(self, kb, bname, fifo=False, dma=False, auto_inc=False):
        self.kb = kb
        self.bname = bname
        self.fifo = fifo
        self.dma = dma
        self.auto_inc = auto_inc
        self.n = 0
        self.sem = None
        self._emit_n = 0
        self.seen = {}

    def _merge(self, d):
        waits = {}
        for (q, v) in d:
            if q is self and self.fifo:
                continue
            if self.seen.get(id(q), -1) >= v:
                continue
            if waits.get(id(q), (None, -1))[1] < v:
                waits[id(q)] = (q, v)
        for k, (q, v) in waits.items():
            self.seen[k] = v
        return list(waits.values())

    def op(self, method, *args, reads=(), writes=(), deps=(), inc=False,
           dsem=None, **kw):
        d = []
        for b in reads:
            if b.w is not None:
                d.append(b.w)
        for b in writes:
            if b.w is not None:
                d.append(b.w)
            d.extend(b.readers)
        d.extend(deps)
        if self.dma:
            assert dsem is not None, "DMA ops need a dsem"
            if dsem.n > 0:
                d.append((dsem, dsem.n))
        waits = self._merge(d)
        if self.dma:
            self.kb.rec.append((self, method, args, kw, waits, (dsem, 16)))
            dsem.n += 16
            t = (dsem, dsem.n)
        else:
            myinc = 1 if (self.auto_inc or inc) else 0
            self.kb.rec.append((self, method, args, kw, waits, myinc))
            if myinc:
                self.n += myinc
            t = (self, self.n)
        for b in reads:
            b.readers.append(t)
        for b in writes:
            b.w = t
            b.readers = []
        return t


class KB:
    def __init__(self, nc):
        self.nc = nc
        self.rec = []
        self.pe = KQ(self, "tensor", fifo=True)
        self.dve = KQ(self, "vector", auto_inc=True)
        self.act = KQ(self, "scalar", auto_inc=True)
        self.io = KQ(self, "sync", dma=True)
        self.gio = KQ(self, "gpsimd", dma=True)
        self.pool = KQ(self, "gpsimd")
        self.queues = [self.pe, self.dve, self.act, self.io, self.gio,
                       self.pool]
        self.dsems = []

    def dsem(self, name):
        d = DSem(name)
        self.dsems.append(d)
        return d

    def _all_tags(self):
        tags = [(q, q.n) for q in self.queues if not q.dma and q.n > 0]
        tags += [(d, d.n) for d in self.dsems if d.n > 0]
        return tags

    def barrier(self, skip=()):
        tags = self._all_tags()
        for q in self.queues:
            if q in skip:
                continue
            deps = [t for t in tags if t[0] is not q]
            deps = [(dq, dv) for (dq, dv) in deps
                    if q.seen.get(id(dq), -1) < dv]
            for (dq, dv) in deps:
                q.seen[id(dq)] = dv
            if deps:
                self.rec.append((q, None, None, None, deps, 0))

    def mm_group(self, mms, reads=(), writes=(), deps=()):
        d = []
        for b in reads:
            if b.w is not None:
                d.append(b.w)
        for b in writes:
            if b.w is not None:
                d.append(b.w)
            d.extend(b.readers)
        d.extend(deps)
        waits = self.pe._merge(d)
        for i, (method, args, kw) in enumerate(mms):
            w = waits if i == 0 else []
            inc = 1 if i == len(mms) - 1 else 0
            self.rec.append((self.pe, method, args, kw, w, inc))
        self.pe.n += 1
        t = (self.pe, self.pe.n)
        for b in reads:
            b.readers.append(t)
        for b in writes:
            b.w = t
            b.readers = []
        return t

    def finish(self):
        tags = self._all_tags()
        self.rec.append((self.io, None, None, None, tags, 0))

    def flush(self, block):
        per_q = {}
        for r in self.rec:
            per_q.setdefault(r[0].bname, []).append(r)
        for chunk in per_q.values():
            bname = chunk[0][0].bname

            def emit(eng, chunk=chunk):
                for (q, method, args, kw, waits, inc) in chunk:
                    if method is None:
                        for (dq, dv) in waits:
                            eng.wait_ge(dq.sem, dv)
                        continue
                    allw = list(waits)
                    for (dq, dv) in allw[1:]:
                        eng.wait_ge(dq.sem, dv)
                    ins = getattr(eng, method)(*args, **kw)
                    for (dq, dv) in allw[:1]:
                        ins._wait_ge(dq.sem, dv)
                    if inc:
                        if isinstance(inc, tuple):
                            ins.then_inc(inc[0].sem, inc[1])
                        else:
                            ins.then_inc(q.sem, inc)
                            q._emit_n += inc
            getattr(block, bname)(emit)



def build_program(n_tok, dbg=None):
    assert n_tok % 128 == 0
    NT = n_tok // 128
    NBLK = MOE_CAP // 128
    nc = bass.Bass("TRN2", target_bir_lowering=False)

    ap = {}
    ap["x"] = nc.dram_tensor("x", [n_tok, D], F32, kind="ExternalInput").ap()
    for nm in ["Wk", "Wv", "Wq", "Wkp", "Wqp", "Wo"]:
        ap[nm] = nc.dram_tensor(nm, [D, D], F32, kind="ExternalInput").ap()
    for nm in ["bk", "bv", "bq", "bkp", "bqp", "ps", "lng", "lnb", "bo"]:
        ap[nm] = nc.dram_tensor(nm, [D], F32, kind="ExternalInput").ap()
    ap["gw"] = nc.dram_tensor("gw", [D, NE], F32, kind="ExternalInput").ap()
    ap["w1"] = nc.dram_tensor("w1", [NE, D, DFF], F32, kind="ExternalInput").ap()
    ap["w2"] = nc.dram_tensor("w2", [NE, DFF, D], F32, kind="ExternalInput").ap()
    out = nc.dram_tensor("out", [n_tok, D], F32, kind="ExternalOutput").ap()
    out_flat = out.rearrange("a b -> (a b)")

    ident_c = nc.inline_tensor(np.eye(128, dtype=ml_dtypes.bfloat16), "identc")
    u = np.zeros((128, 128), dtype=np.float32)
    for blk in range(2):
        for a_ in range(64):
            u[blk * 64 + a_, blk * 64 + a_:blk * 64 + 64] = 1.0
    cumu_c = nc.inline_tensor(u.astype(ml_dtypes.bfloat16), "cumuc")
    ones1_c = nc.inline_tensor(np.ones((1, 128), dtype=ml_dtypes.bfloat16), "ones1c")
    onescb_c = nc.inline_tensor(np.ones((128, 1), dtype=ml_dtypes.bfloat16), "onescbc")
    ones1f_c = nc.inline_tensor(np.ones((1, 128), dtype=np.float32), "ones1fc")
    ones11f_c = nc.inline_tensor(np.ones((1, 1), dtype=np.float32), "ones11fc")
    iota8c = nc.inline_tensor(np.arange(8, dtype=np.float32).reshape(8, 1), "iota8c")
    ones8c = nc.inline_tensor(np.ones((8, 1), dtype=np.float32), "ones8c")
    iotac = nc.inline_tensor(
        (np.arange(128)[:, None] + 128 * np.arange(16)[None, :]).astype(np.int32),
        "iotac")

    xn_dram = nc.dram_tensor("xn_dram", [n_tok, D + 8], BF16, kind="Internal").ap()
    idx_dram = nc.dram_tensor("idx_dram", [8 * MOE_CAP, 1], I32, kind="Internal").ap()
    y_dram = nc.dram_tensor("y_dram", [n_tok, D], F32, kind="Internal").ap()
    s_dram = nc.dram_tensor("s_dram", [n_tok, D], F32, kind="Internal").ap()

    kb = KB(nc)
    pe, dve, act, io, gio = kb.pe, kb.dve, kb.act, kb.io, kb.gio

    with ExitStack() as S0:
        E0 = S0.enter_context

        def mk(E, name, shape, dtype):
            t = E(nc.sbuf_tensor(name, shape, dtype))
            return t, Buf(name)

        ident_sb, b_ident = mk(E0, "ident_sb", [128, 128], BF16)
        cumu_sb, b_cumu = mk(E0, "cumu_sb", [128, 128], BF16)
        ones1_sb, b_ones1 = mk(E0, "ones1_sb", [1, 128], BF16)
        onescb_sb, b_onescb = mk(E0, "onescb_sb", [128, 1], BF16)
        ones1f_sb, b_ones1f = mk(E0, "ones1f_sb", [1, 128], F32)
        ones11f_sb, b_ones11f = mk(E0, "ones11f_sb", [1, 1], F32)
        cpool, b_cpool = mk(E0, "cpool", [128, 3], F32)
        idxsb, b_idx = mk(E0, "idxsb", [128, NBLK * 8], I32)

        Wv_sb2, b_wv2 = mk(E0, "Wv_sb", [128, C * D], BF16)
        Wv3 = Wv_sb2.rearrange("p (c n) -> p c n", c=C)
        Wkp_c, b_wkpc = mk(E0, "Wkp_c", [128, C * D], BF16)
        Wkp3 = Wkp_c.rearrange("p (c n) -> p c n", c=C)
        Wqp_c, b_wqpc = mk(E0, "Wqp_c", [128, C * D], BF16)
        Wqp3 = Wqp_c.rearrange("p (c n) -> p c n", c=C)
        Wo_sb2, b_wo2 = mk(E0, "Wo_sb", [128, C * D], BF16)
        Wo3 = Wo_sb2.rearrange("p (c n) -> p c n", c=C)
        bvr_t, b_r65 = mk(E0, "bvr_t", [1, 3 * D], BF16)
        bv_row = bvr_t[0:1, 0:D]
        bkp_row = bvr_t[0:1, D:2 * D]
        bqp_row = bvr_t[0:1, 2 * D:3 * D]
        bor_t, b_r33 = mk(E0, "bor_t", [1, 2 * D], BF16)
        bo_row = bor_t[0:1, 0:D]
        cs_row = bor_t[0:1, D:2 * D]
        psb_col, b_psbc = mk(E0, "psb_col", [128, C], F32)
        psb_bc, b_psbb = mk(E0, "psb_bc", [128, D], F32)

        PS1 = E0(nc.psum_tensor("PS1", [128, 1024], F32))
        PS2 = E0(nc.psum_tensor("PS2", [128, 1024], F32))
        PS3 = E0(nc.psum_tensor("PS3", [128, 1024], F32))
        PS4 = E0(nc.psum_tensor("PS4", [128, 512], F32))
        PSB = E0(nc.psum_tensor("PSB", [128, 1024], BF16))
        bP1a, bP1b, bP2a, bP2b, bP3a, bP3b, bP4, bPB = (
            Buf(x_) for x_ in ["PS1a", "PS1b", "PS2a", "PS2b",
                               "PS3a", "PS3b", "PS4", "PSB"])

        for q, nm in [(pe, "s_pe"), (dve, "s_dve"), (act, "s_act"),
                      (kb.pool, "s_pool")]:
            q.sem = E0(nc.semaphore(nm))
        ds_cst = kb.dsem("cst")
        ds_cs2 = kb.dsem("cs2")
        ds_cs3 = kb.dsem("cs3")
        ds_xt = [kb.dsem("xtA"), kb.dsem("xtB")]
        ds_st = [kb.dsem("stA"), kb.dsem("stB")]
        ds_wa = kb.dsem("wa")
        ds_wp = kb.dsem("wp")
        ds_wv = kb.dsem("wv")
        ds_rb = kb.dsem("rb")
        ds_w1 = [kb.dsem("w1A"), kb.dsem("w1B")]
        ds_w2 = [kb.dsem("w2A"), kb.dsem("w2B")]
        ds_dbg = kb.dsem("dbg")
        ds_xnd = kb.dsem("xnd")
        ds_sc4 = [kb.dsem("sc%d" % i) for i in range(4)]
        ds_sc = ds_sc4[0]
        ds_xg = [kb.dsem("xgA"), kb.dsem("xgB")]
        ds_ys = kb.dsem("ys")
        ds_ysb = kb.dsem("ysb")
        ds_ep = [[kb.dsem("ep%d_%d" % (t_, i)) for i in range(4)]
                 for t_ in range(4)]
        ds_yz = kb.dsem("yz")
        ds_yr = [kb.dsem("yrA"), kb.dsem("yrB")]
        ds_xr = [kb.dsem("xrA"), kb.dsem("xrB")]
        ds_ss = kb.dsem("ss")
        ds_fs = [kb.dsem("fsA"), kb.dsem("fsB")]
        for d_ in kb.dsems:
            d_.sem = E0(nc.semaphore("sd_" + d_.name))

        rb_tok = nc.gpsimd.alloc_register("rb_tok")
        rb_slot = nc.gpsimd.alloc_register("rb_slot")
        kb.pool.op("reg_mov", rb_tok, n_tok - 1)
        kb.pool.op("reg_mov", rb_slot, 8 * MOE_CAP - 1)

        io.op("dma_start", out=ident_sb[:], in_=ident_c.ap(), writes=[b_ident],
              dsem=ds_cst)
        io.op("dma_start", out=cumu_sb[:], in_=cumu_c.ap(), writes=[b_cumu],
              dsem=ds_cs2)
        io.op("dma_start", out=ones1_sb[:], in_=ones1_c.ap(), writes=[b_ones1],
              dsem=ds_cs3)
        io.op("dma_start", out=onescb_sb[:], in_=onescb_c.ap(), writes=[b_onescb],
              dsem=ds_cst)
        io.op("dma_start", out=ones1f_sb[:], in_=ones1f_c.ap(), writes=[b_ones1f],
              dsem=ds_cs2)
        io.op("dma_start", out=ones11f_sb[:], in_=ones11f_c.ap(),
              writes=[b_ones11f], dsem=ds_cs3)
        dve.op("memset", cpool[:, 0:1], -np.pi, writes=[b_cpool])
        dve.op("memset", cpool[:, 1:2], LN2EPS, writes=[b_cpool])
        dve.op("memset", cpool[:, 2:3], -np.pi / 2, writes=[b_cpool])
        NEG_PI = cpool[:, 0:1]
        EPS_B = cpool[0:1, 1:2]
        NEG_HPI = cpool[:, 2:3]

        b_xnd = Buf("xn_dram")
        b_idxd = Buf("idx_dram")
        b_idxd4 = [Buf("idx_dram%d" % i) for i in range(4)]
        b_yd = Buf("y_dram")
        b_sd = [Buf("s_dram%d" % j) for j in range(n_tok // 128)]

        with ExitStack() as SA:
            EA = SA.enter_context
            rows_b, b_rowsb = mk(EA, "rows_b", [1, 3 * D], BF16)
            r_bkp, r_bqp, r_bo = \
                (rows_b[0:1, i * D:(i + 1) * D] for i in range(3))
            ps_row, b_psr = mk(EA, "ps_row", [1, D], F32)
            cols8, b_cols8 = mk(EA, "cols8", [8, 4 * 128], F32)
            c_bk, c_bq, c_lnb, c_lng = \
                (cols8[0:8, i * 128:(i + 1) * 128] for i in range(4))
            wtmp, b_wtmp = mk(EA, "wtmp", [128, C * D], BF16)
            wp, b_wp = mk(EA, "wp", [128, C * D], BF16)
            wtmp3 = wtmp.rearrange("p (c n) -> p c n", c=C)
            wp3 = wp.rearrange("p (c n) -> p c n", c=C)
            colb_s, b_colb = mk(EA, "colb_s", [128, 8], BF16)
            colf_s, b_colf = mk(EA, "colf_s", [128, 8], F32)
            rtmp, b_rtmp = mk(EA, "rtmp", [1, D], F32)
            c8b_s, b_c8b = mk(EA, "c8b_s", [8, 128], BF16)
            xtile, _ = mk(EA, "xtile", [128, 2 * D], F32)
            b_xt2 = [Buf("xtA"), Buf("xtB")]
            sq_s, b_sq = mk(EA, "sq_s", [128, D], F32)
            st_s, _ = mk(EA, "st_s", [128, 16], F32)
            b_st2 = [Buf("stA"), Buf("stB")]
            xnt_s, _ = mk(EA, "xnt_s", [128, 2 * D], BF16)
            b_xnt2p = [Buf("xntpA"), Buf("xntpB")]
            xn_rt, _ = mk(EA, "xn_rt", [128, 2 * C * 128], BF16)
            xn_rt3 = [xn_rt[:, 0:C * 128].rearrange("p (c t) -> p c t", c=C),
                      xn_rt[:, C * 128:].rearrange("p (c t) -> p c t", c=C)]
            b_xrt2 = [Buf("xrtA"), Buf("xrtB")]
            c_rows, b_crows = mk(EA, "c_rows", [8, n_tok], BF16)
            gwb_s, b_gwb = mk(EA, "gwb_s", [128, C * NE], BF16)
            gw_sb, b_gw = mk(EA, "gw_sb", [128, C * NE], F32)
            iota_sb, b_iot = mk(EA, "iota_sb", [128, 16], I32)
            iota8_sb, b_io8 = mk(EA, "iota8_sb", [8, 1], F32)
            ones8_sb, b_on8 = mk(EA, "ones8_sb", [8, 1], F32)
            poscol, b_pc = mk(EA, "poscol", [128, 32], I32)
            lg_s, b_lg = mk(EA, "lg_s", [128, NT * 8], F32)
            mx_s, b_mx = mk(EA, "mx_s", [128, 4 * NT], F32)
            eq_s, _ = mk(EA, "eq_s", [128, 2 * 16], F32)
            b_eq2 = [Buf("eqA"), Buf("eqB")]
            eqb_s, _ = mk(EA, "eqb_s", [128, 16], BF16)
            b_eqb2 = [Buf("eqbA"), Buf("eqbB")]
            ct_s, _ = mk(EA, "ct_s", [128, 16], BF16)
            b_ct2 = [Buf("ctA"), Buf("ctB")]
            eq1em, b_e1m = mk(EA, "eq1em", [8, n_tok], BF16)
            mask_s, b_mask = mk(EA, "mask_s", [8, n_tok], F32)
            incl_s, b_incl = mk(EA, "incl_s", [8, n_tok], F32)
            eq2em, b_e2m = mk(EA, "eq2em", [8, n_tok], F32)
            prod_s, b_prod = mk(EA, "prod_s", [8, n_tok], F32)
            posg, b_posg = mk(EA, "posg", [1, 2 * n_tok], F32)

            for j0 in range(min(2, NT)):
                io.op("dma_start", out=xtile[:, j0 * D:(j0 + 1) * D],
                      in_=ap["x"][j0 * 128:(j0 + 1) * 128, :],
                      writes=[b_xt2[j0]], dsem=ds_xt[j0])

            dve.op("memset", idxsb[:], 10 ** 6, writes=[b_idx])
            gio.op("dma_start",
                   out=idx_dram.rearrange("(b p e) o -> p b (e o)",
                                          p=128, b=NBLK),
                   in_=idxsb.rearrange("p (b e) -> p b e", b=NBLK),
                   reads=[b_idx], writes=[b_idxd] + b_idxd4, dsem=ds_sc)
            gio.op("dma_start", out=wtmp3,
                   in_=ap["Wk"].rearrange("(c p) n -> p c n", p=128),
                   writes=[b_wtmp], dsem=ds_wa)
            gio.op("dma_start", out=wp3,
                   in_=ap["Wkp"].rearrange("(c p) n -> p c n", p=128),
                   writes=[b_wp], dsem=ds_wp)
            for i, nm in enumerate(["bkp", "bqp", "bo"]):
                gio.op("dma_start", out=rows_b[0:1, i * D:(i + 1) * D],
                       in_=ap[nm].rearrange("(o n) -> o n", o=1),
                       writes=[b_rowsb], dsem=ds_rb)
            gio.op("dma_start", out=Wv3,
                   in_=ap["Wv"].rearrange("(c p) n -> p c n", p=128),
                   writes=[b_wv2], dsem=ds_wv)
            gio.op("dma_start", out=bv_row,
                   in_=ap["bv"].rearrange("(o n) -> o n", o=1),
                   writes=[b_r65], dsem=ds_wv)
            io.op("dma_start", out=gw_sb.rearrange("p (c e) -> p c e", c=C),
                  in_=ap["gw"].rearrange("(c p) e -> p c e", p=128),
                  writes=[b_gw], dsem=ds_cst)
            io.op("dma_start", out=iota_sb[:], in_=iotac.ap(), writes=[b_iot],
                  dsem=ds_cs2)
            io.op("dma_start", out=iota8_sb[:], in_=iota8c.ap(), writes=[b_io8],
                  dsem=ds_cs3)
            io.op("dma_start", out=ones8_sb[:], in_=ones8c.ap(), writes=[b_on8],
                  dsem=ds_cs2)
            io.op("dma_start", out=ps_row[:],
                  in_=ap["ps"].rearrange("(o n) -> o n", o=1),
                  writes=[b_psr], dsem=ds_cst)
            for i, nm in enumerate(["bk", "bq", "lnb", "lng"]):
                io.op("dma_start", out=cols8[0:8, i * 128:(i + 1) * 128],
                      in_=ap[nm].rearrange("(c p) -> c p", c=8),
                      writes=[b_cols8], dsem=[ds_cst, ds_cs2, ds_cs3][i % 3])
            dve.op("tensor_copy", out=gwb_s[:], in_=gw_sb[:], reads=[b_gw],
                   writes=[b_gwb])
            gwb3 = gwb_s.rearrange("p (c e) -> p c e", c=C)

            prep_units = []

            def transpose_inplace_units(w3, bw):
                for ci in range(C):
                    def tci(ci=ci, w3=w3, bw=bw):
                        for cf in range(ci, C):
                            mms = [("transpose",
                                    (PSB[:, 0:128],
                                     w3[:, ci, cf * 128:(cf + 1) * 128],
                                     ident_sb[:]), {})]
                            if cf != ci:
                                mms.append(("transpose",
                                            (PSB[:, 128:256],
                                             w3[:, cf, ci * 128:(ci + 1) * 128],
                                             ident_sb[:]), {}))
                            kb.mm_group(mms, reads=[bw, b_ident], writes=[bPB])
                            dve.op("tensor_copy",
                                   out=w3[:, cf, ci * 128:(ci + 1) * 128],
                                   in_=PSB[:, 0:128], reads=[bPB], writes=[bw])
                            if cf != ci:
                                dve.op("tensor_copy",
                                       out=w3[:, ci, cf * 128:(cf + 1) * 128],
                                       in_=PSB[:, 128:256],
                                       reads=[bPB], writes=[bw])
                    prep_units.append(tci)

            def colsum_unit(brow_ap, out_row_bf, b_orow, plain=False):
                def cu():
                    for h in range(2):
                        kb.mm_group(
                            [("matmul", (PS3[0:1, 0:512], colb_s[:, kc:kc + 1],
                                         wp3[:, kc, h * 512:(h + 1) * 512]),
                              dict(start=(kc == 0), stop=(kc == C - 1)))
                             for kc in range(C)],
                            reads=[b_colb, b_wp], writes=[bP3a])
                        if plain:
                            dve.op("tensor_copy",
                                   out=rtmp[0:1, h * 512:(h + 1) * 512],
                                   in_=PS3[0:1, 0:512], reads=[bP3a],
                                   writes=[b_rtmp])
                        else:
                            dve.op("tensor_tensor",
                                   out=rtmp[0:1, h * 512:(h + 1) * 512],
                                   in0=PS3[0:1, 0:512],
                                   in1=brow_ap[0:1, h * 512:(h + 1) * 512],
                                   op=ALU.add, reads=[bP3a, b_rowsb],
                                   writes=[b_rtmp])
                    dve.op("tensor_copy", out=out_row_bf, in_=rtmp[:],
                           reads=[b_rtmp], writes=[b_orow])
                prep_units.append(cu)

            def colb_unit(bcol_ap, also_f32=False):
                def cb():
                    dve.op("tensor_copy", out=c8b_s[:], in_=bcol_ap,
                           reads=[b_cols8], writes=[b_c8b])
                    kb.mm_group([("transpose", (PSB[:, 0:8], c8b_s[:],
                                                ident_sb[0:8, 0:8]), {})],
                                reads=[b_c8b, b_ident], writes=[bPB])
                    dve.op("tensor_copy", out=colb_s[:], in_=PSB[:, 0:8],
                           reads=[bPB], writes=[b_colb])
                    if also_f32:
                        dve.op("tensor_copy", out=colf_s[:], in_=PSB[:, 0:8],
                               reads=[bPB], writes=[b_colf])
                prep_units.append(cb)

            def combine_units(dst3, b_dst, w_name, wp_name, bcol_ap, brow_ap,
                              out_row_bf, b_orow, preloaded):
                if not preloaded:
                    def ld():
                        gio.op("dma_start", out=wtmp3,
                               in_=ap[w_name].rearrange(
                                   "(c p) n -> p c n", p=128),
                               writes=[b_wtmp], dsem=ds_wa)
                        gio.op("dma_start", out=wp3,
                               in_=ap[wp_name].rearrange(
                                   "(c p) n -> p c n", p=128),
                               writes=[b_wp], dsem=ds_wp)
                    prep_units.append(ld)
                transpose_inplace_units(wtmp3, b_wtmp)
                colb_unit(bcol_ap)
                for mc in range(C):
                    def dmc(mc=mc, dst3=dst3, b_dst=b_dst):
                        for h in range(2):
                            kb.mm_group(
                                [("matmul", (PS3[:, h * 512:(h + 1) * 512],
                                             wtmp3[:, kc,
                                                   mc * 128:(mc + 1) * 128],
                                             wp3[:, kc,
                                                 h * 512:(h + 1) * 512]),
                                  dict(start=(kc == 0), stop=(kc == C - 1)))
                                 for kc in range(C)],
                                reads=[b_wtmp, b_wp],
                                writes=[bP3a if h == 0 else bP3b])
                        dve.op("tensor_copy", out=dst3[:, mc, :],
                               in_=PS3[:, 0:1024], reads=[bP3a, bP3b],
                               writes=[b_dst])
                    prep_units.append(dmc)
                colsum_unit(brow_ap, out_row_bf, b_orow)

            combine_units(Wkp3, b_wkpc, "Wk", "Wkp", c_bk, r_bkp,
                          bkp_row, b_r65, True)
            combine_units(Wqp3, b_wqpc, "Wq", "Wqp", c_bq, r_bqp,
                          bqp_row, b_r65, False)

            def wo_ld():
                gio.op("dma_start", out=wp3,
                       in_=ap["Wo"].rearrange("(c p) n -> p c n", p=128),
                       writes=[b_wp], dsem=ds_wp)
            prep_units.append(wo_ld)
            colb_unit(c_lnb)
            colsum_unit(r_bo, bo_row, b_r33)
            colb_unit(c_lng, also_f32=True)
            colsum_unit(None, cs_row, b_r33, plain=True)

            def wo_scale():
                for c in range(C):
                    dve.op("tensor_scalar", out=Wo3[:, c, :], in0=wp3[:, c, :],
                           scalar1=colf_s[:, c:c + 1], scalar2=None,
                           op0=ALU.mult, reads=[b_wp, b_colf], writes=[b_wo2])
            prep_units.append(wo_scale)

            def psb_unit():
                for c in range(C):
                    kb.mm_group(
                        [("matmul", (PS4[:, 8 + c:9 + c],
                                     ps_row[0:1, c * 128:(c + 1) * 128],
                                     ones11f_sb[:]),
                          dict(start=True, stop=True))],
                        reads=[b_psr, b_ones11f], writes=[bP4])
                dve.op("tensor_scalar", out=psb_col[:], in0=PS4[:, 8:8 + C],
                       scalar1=INV2PI, scalar2=None, op0=ALU.mult,
                       reads=[bP4], writes=[b_psbc])
                for h in range(2):
                    kb.mm_group(
                        [("matmul", (PS3[:, h * 512:(h + 1) * 512],
                                     ones1f_sb[:],
                                     ps_row[0:1, h * 512:(h + 1) * 512]),
                          dict(start=True, stop=True))],
                        reads=[b_ones1f, b_psr],
                        writes=[bP3a if h == 0 else bP3b])
                dve.op("tensor_scalar", out=psb_bc[:], in0=PS3[:, 0:1024],
                       scalar1=INV2PI, scalar2=None, op0=ALU.mult,
                       reads=[bP3a, bP3b], writes=[b_psbb])
            prep_units.append(psb_unit)

            pp_i = [0]

            def prep_pump(k=1):
                n = 0
                while pp_i[0] < len(prep_units) and n < k:
                    prep_units[pp_i[0]]()
                    pp_i[0] += 1
                    n += 1

            def p0a(j):
                par = j % 2
                bx = b_xt2[par]
                bst = b_st2[par]
                sb = par * 8
                xt = xtile[:, par * D:(par + 1) * D]
                if j >= 2:
                    io.op("dma_start", out=xt,
                          in_=ap["x"][j * 128:(j + 1) * 128, :],
                          writes=[bx], dsem=ds_xt[par])
                dve.op("reduce_sum", out=st_s[:, sb:sb + 1], in_=xt, axis=AX.X,
                       reads=[bx], writes=[bst])
                act.op("activation", out=sq_s[:], in_=xt, func=AF.Square,
                       accum_out=st_s[:, sb + 1:sb + 2], reads=[bx],
                       writes=[b_sq, bst])
                dve.op("tensor_scalar", out=st_s[:, sb + 2:sb + 3],
                       in0=st_s[:, sb:sb + 1],
                       scalar1=1.0 / D, scalar2=None, op0=ALU.mult,
                       reads=[bst], writes=[bst])
                dve.op("tensor_scalar", out=st_s[:, sb + 3:sb + 4],
                       in0=st_s[:, sb + 1:sb + 2],
                       scalar1=1.0 / D, scalar2=None, op0=ALU.mult,
                       reads=[bst], writes=[bst])
                dve.op("tensor_tensor", out=st_s[:, sb + 4:sb + 5],
                       in0=st_s[:, sb + 2:sb + 3], in1=st_s[:, sb + 2:sb + 3],
                       op=ALU.mult, reads=[bst], writes=[bst])
                dve.op("scalar_tensor_tensor", out=st_s[:, sb + 4:sb + 5],
                       in0=st_s[:, sb + 4:sb + 5],
                       scalar=-1.0, in1=st_s[:, sb + 3:sb + 4],
                       op0=ALU.mult, op1=ALU.add, reads=[bst], writes=[bst])
                dve.op("tensor_scalar", out=st_s[:, sb + 4:sb + 5],
                       in0=st_s[:, sb + 4:sb + 5],
                       scalar1=1e-5, scalar2=None, op0=ALU.add,
                       reads=[bst], writes=[bst])
                act.op("activation", out=st_s[:, sb + 5:sb + 6],
                       in_=st_s[:, sb + 4:sb + 5],
                       func=AF.Sqrt, reads=[bst], writes=[bst])

            def p0b(j):
                par = j % 2
                bx = b_xt2[par]
                bst = b_st2[par]
                sb = par * 8
                xt = xtile[:, par * D:(par + 1) * D]
                xnt = xnt_s[:, par * D:(par + 1) * D]
                dve.op("reciprocal", out=st_s[:, sb + 6:sb + 7],
                       in_=st_s[:, sb + 5:sb + 6], reads=[bst], writes=[bst])
                dve.op("tensor_scalar", out=xnt, in0=xt,
                       scalar1=st_s[:, sb + 2:sb + 3],
                       scalar2=st_s[:, sb + 6:sb + 7],
                       op0=ALU.subtract, op1=ALU.mult,
                       reads=[bx, bst], writes=[b_xnt2p[par]])
                io.op("dma_start", out=xn_dram[j * 128:(j + 1) * 128, 0:D],
                      in_=xnt, reads=[b_xnt2p[par]], writes=[b_xnd],
                      dsem=ds_xnd)
                kb.mm_group(
                    [("transpose", (PSB[:, c * 128:(c + 1) * 128],
                                    xnt[:, c * 128:(c + 1) * 128],
                                    ident_sb[:]), {}) for c in range(C)],
                    reads=[b_xnt2p[par], b_ident], writes=[bPB])
                dve.op("tensor_copy", out=xn_rt3[par],
                       in_=PSB.rearrange("p (c t) -> p c t", c=C),
                       reads=[bPB], writes=[b_xrt2[par]])

            def rt1(j):
                par = j % 2
                beq = b_eq2[par]
                eb = par * 16
                kb.mm_group(
                    [("matmul", (PS4[:, 0:8], xn_rt3[par][:, kc, :],
                                 gwb3[:, kc, :]),
                      dict(start=(kc == 0), stop=(kc == C - 1)))
                     for kc in range(C)],
                    reads=[b_xrt2[par], b_gwb], writes=[bP4])
                dve.op("tensor_copy", out=lg_s[:, j * 8:(j + 1) * 8],
                       in_=PS4[:, 0:8], reads=[bP4], writes=[b_lg])
                dve.op("reduce_max", out=mx_s[:, j:j + 1],
                       in_=lg_s[:, j * 8:(j + 1) * 8], axis=AX.X,
                       reads=[b_lg], writes=[b_mx])
                dve.op("tensor_scalar", out=eq_s[:, eb:eb + 8],
                       in0=lg_s[:, j * 8:(j + 1) * 8],
                       scalar1=mx_s[:, j:j + 1], scalar2=None,
                       op0=ALU.is_equal, reads=[b_lg, b_mx], writes=[beq])
                dve.op("tensor_copy", out=eqb_s[:, par * 8:par * 8 + 8],
                       in_=eq_s[:, eb:eb + 8],
                       reads=[beq], writes=[b_eqb2[par]])
                kb.mm_group([("transpose", (PSB[0:8, 0:128],
                                            eqb_s[:, par * 8:par * 8 + 8],
                                            ident_sb[:]), {})],
                            reads=[b_eqb2[par], b_ident], writes=[bPB])
                dve.op("tensor_copy",
                       out=eq1em[0:8, j * 128:(j + 1) * 128],
                       in_=PSB[0:8, 0:128], reads=[bPB], writes=[b_e1m])
                dve.op("scalar_tensor_tensor", out=eq_s[:, eb + 8:eb + 16],
                       in0=eq_s[:, eb:eb + 8], scalar=-1e30,
                       in1=lg_s[:, j * 8:(j + 1) * 8],
                       op0=ALU.mult, op1=ALU.add,
                       reads=[beq, b_lg], writes=[beq])
                dve.op("reduce_max", out=mx_s[:, NT + j:NT + j + 1],
                       in_=eq_s[:, eb + 8:eb + 16], axis=AX.X, reads=[beq],
                       writes=[b_mx])

            p0a(0)
            if NT > 1:
                p0a(1)
            for j in range(NT):
                p0b(j)
                rt1(j)
                if j + 2 < NT:
                    p0a(j + 2)
                prep_pump(2)

            dve.op("tensor_tensor", out=mx_s[:, 2 * NT:3 * NT],
                   in0=mx_s[:, 0:NT], in1=mx_s[:, NT:2 * NT],
                   op=ALU.subtract, reads=[b_mx], writes=[b_mx])
            act.op("activation", out=mx_s[:, 2 * NT:3 * NT],
                   in_=mx_s[:, 2 * NT:3 * NT], func=AF.Sigmoid,
                   reads=[b_mx], writes=[b_mx])
            dve.op("tensor_scalar", out=mx_s[:, 3 * NT:4 * NT],
                   in0=mx_s[:, 2 * NT:3 * NT], scalar1=-1.0, scalar2=1.0,
                   op0=ALU.mult, op1=ALU.add, reads=[b_mx], writes=[b_mx])
            for j in range(NT):
                par = j % 2
                beq = b_eq2[par]
                eb = par * 16
                bct = b_ct2[par]
                ct = ct_s[:, par * 8:par * 8 + 8]
                dve.op("tensor_scalar", out=eq_s[:, eb:eb + 8],
                       in0=lg_s[:, j * 8:(j + 1) * 8],
                       scalar1=mx_s[:, j:j + 1],
                       scalar2=mx_s[:, 2 * NT + j:2 * NT + j + 1],
                       op0=ALU.is_equal, op1=ALU.mult,
                       reads=[b_lg, b_mx], writes=[beq])
                dve.op("tensor_scalar", out=eq_s[:, eb + 8:eb + 16],
                       in0=lg_s[:, j * 8:(j + 1) * 8],
                       scalar1=mx_s[:, NT + j:NT + j + 1],
                       scalar2=mx_s[:, 3 * NT + j:3 * NT + j + 1],
                       op0=ALU.is_equal, op1=ALU.mult,
                       reads=[b_lg, b_mx], writes=[beq])
                dve.op("tensor_tensor", out=ct, in0=eq_s[:, eb:eb + 8],
                       in1=eq_s[:, eb + 8:eb + 16], op=ALU.add, reads=[beq],
                       writes=[bct])
                kb.mm_group([("transpose", (PSB[0:8, 0:128], ct,
                                            ident_sb[:]), {})],
                            reads=[bct, b_ident], writes=[bPB])
                dve.op("tensor_copy",
                       out=c_rows[0:8, j * 128:(j + 1) * 128],
                       in_=PSB[0:8, 0:128], reads=[bPB], writes=[b_crows])
                io.op("dma_start",
                      out=xn_dram[j * 128:(j + 1) * 128, D:D + 8],
                      in_=ct, reads=[bct], writes=[b_xnd],
                      dsem=ds_xnd)
                prep_pump(1)

            dve.op("memset", sq_s[:], 0.0, writes=[b_sq])
            for j in range(NT):
                io.op("dma_start", out=y_dram[j * 128:(j + 1) * 128, :],
                      in_=sq_s[:], reads=[b_sq], writes=[b_yd], dsem=ds_yz)

            if dbg == "c":
                gio.op("dma_start",
                       out=out_flat[0:8 * n_tok].rearrange(
                           "(p t) -> p t", p=8),
                       in_=c_rows[:], reads=[b_crows], dsem=ds_dbg)
                kb.finish()
                with nc.Block() as block:
                    kb.flush(block)
                return nc

            dve.op("tensor_scalar", out=mask_s[:], in0=c_rows[:],
                   scalar1=0.0, scalar2=None, op0=ALU.is_gt,
                   reads=[b_crows], writes=[b_mask])
            dve.op("tensor_tensor_scan", out=incl_s[:], data0=mask_s[:],
                   data1=mask_s[:], initial=0.0, op0=ALU.add, op1=ALU.max,
                   reads=[b_mask], writes=[b_incl])
            dve.op("tensor_tensor", out=incl_s[:], in0=incl_s[:],
                   in1=mask_s[:], op=ALU.subtract,
                   reads=[b_incl, b_mask], writes=[b_incl])
            dve.op("tensor_scalar", out=incl_s[:], in0=incl_s[:],
                   scalar1=8.0, scalar2=iota8_sb[:], op0=ALU.mult,
                   op1=ALU.add, reads=[b_incl, b_io8], writes=[b_incl])
            dve.op("tensor_tensor", out=eq2em[:], in0=mask_s[:],
                   in1=eq1em[:], op=ALU.subtract,
                   reads=[b_mask, b_e1m], writes=[b_e2m])
            for choice in range(2):
                if choice == 0:
                    dve.op("tensor_tensor", out=prod_s[:], in0=incl_s[:],
                           in1=eq1em[:], op=ALU.mult,
                           reads=[b_incl, b_e1m], writes=[b_prod])
                else:
                    dve.op("tensor_tensor", out=prod_s[:], in0=incl_s[:],
                           in1=eq2em[:], op=ALU.mult,
                           reads=[b_incl, b_e2m], writes=[b_prod])
                PCS = min(512, n_tok)
                for piece in range(n_tok // PCS):
                    pb = [bP1a, bP1b, bP2a, bP2b][piece % 4]
                    base = [0, 512, 1024, 1536][piece % 4]
                    pps = (PS1 if base < 1024 else PS2)[0:1,
                            base % 1024:base % 1024 + PCS]
                    kb.mm_group(
                        [("matmul", (pps, ones8_sb[:],
                                     prod_s[0:8,
                                            piece * PCS:(piece + 1) * PCS]),
                          dict(start=True, stop=True))],
                        reads=[b_on8, b_prod], writes=[pb])
                    dve.op("tensor_copy",
                           out=posg[0:1, choice * n_tok + piece * PCS:
                                    choice * n_tok + (piece + 1) * PCS],
                           in_=pps, reads=[pb], writes=[b_posg])
                prep_pump(3)
            for ch in range(2):
                for j in range(NT):
                    kb.mm_group(
                        [("matmul", (PS4[:, ch * NT + j:ch * NT + j + 1],
                                     posg[0:1, ch * n_tok + j * 128:
                                          ch * n_tok + (j + 1) * 128],
                                     ones11f_sb[:]),
                          dict(start=True, stop=True))],
                        reads=[b_posg, b_ones11f], writes=[bP4])
            dve.op("tensor_copy", out=poscol[:, 0:2 * NT],
                   in_=PS4[:, 0:2 * NT], reads=[bP4], writes=[b_pc])

            for k in range(2 * NT):
                gio.op("indirect_dma_start", out=idx_dram,
                       out_offset=IndirectOffsetOnAxis(
                           poscol[:, k:k + 1], 0),
                       in_=iota_sb[:, (k % NT):(k % NT) + 1],
                       in_offset=None, bounds_check=rb_slot,
                       oob_is_err=False,
                       reads=[b_pc, b_iot], writes=[b_idxd4[k % 4]],
                       dsem=ds_sc4[k % 4])
            gio.op("dma_start",
                   out=idxsb.rearrange("p (b e) -> p b e", b=NBLK),
                   in_=idx_dram.rearrange("(b p e) o -> p b (e o)",
                                          p=128, b=NBLK),
                   reads=b_idxd4, writes=[b_idx], dsem=ds_sc)

            while pp_i[0] < len(prep_units):
                prep_units[pp_i[0]]()
                pp_i[0] += 1

        kb.barrier()

        with ExitStack() as SB:
            EB = SB.enter_context
            xg_tok, b_xg = mk(EB, "xg_tok", [128, NBLK * (D + 8)], BF16)
            xg3 = xg_tok.rearrange("p (b n) -> p b n", b=NBLK)
            xg_fm, b_xgf = mk(EB, "xg_fm", [128, C * MOE_CAP], BF16)
            xf3 = xg_fm.rearrange("p (c t) -> p c t", c=C)
            yb3 = xf3
            h_sb, b_h = mk(EB, "h_sb", [128, MHC * MOE_CAP], BF16)
            h3 = h_sb.rearrange("p (m t) -> p m t", m=MHC)
            ysc_s, _ = mk(EB, "ysc_s", [128, 2 * D], F32)
            b_ysc2 = [Buf("yscA"), Buf("yscB")]
            ccol_s, _ = mk(EB, "ccol_s", [128, 2 * NBLK], F32)
            b_cc2 = [Buf("ccA"), Buf("ccB")]
            w1a, b_w1a = mk(EB, "w1a", [128, C * 256], BF16)
            w1b, b_w1b = mk(EB, "w1b", [128, C * 256], BF16)
            w2a, b_w2a = mk(EB, "w2a", [128, MHC * 128], BF16)
            w2b, b_w2b = mk(EB, "w2b", [128, MHC * 128], BF16)
            w1bufs = [(w1a.rearrange("p (c n) -> p c n", c=C), b_w1a),
                      (w1b.rearrange("p (c n) -> p c n", c=C), b_w1b)]
            w2bufs = [(w2a.rearrange("p (m n) -> p m n", m=MHC), b_w2a),
                      (w2b.rearrange("p (m n) -> p m n", m=MHC), b_w2b)]
            NP = [(0, 512), (512, MOE_CAP - 512)]
            tA_s, b_tA = mk(EB, "tA_s", [128, D], F32)
            tA3 = tA_s.rearrange("p (c t) -> p c t", c=C)
            tB_s, b_tB = mk(EB, "tB_s", [128, D], F32)
            ni_s, b_ni = mk(EB, "ni_s", [128, D], I32)
            sin2_s, _bs = mk(EB, "sin2_s", [128, 2 * D], BF16)
            cos2_s, _bc = mk(EB, "cos2_s", [128, 2 * D], BF16)
            sink2_s, _sk = mk(EB, "sink2_s", [128, 2 * D], BF16)
            cosk2_s, _ck = mk(EB, "cosk2_s", [128, 2 * D], BF16)
            sink_p = [sink2_s[:, 0:D], sink2_s[:, D:2 * D]]
            cosk_p = [cosk2_s[:, 0:D], cosk2_s[:, D:2 * D]]
            b_sink_p = [Buf("sinkA"), Buf("sinkB")]
            b_cosk_p = [Buf("coskA"), Buf("coskB")]
            sin_p = [sin2_s[:, 0:D], sin2_s[:, D:2 * D]]
            cos_p = [cos2_s[:, 0:D], cos2_s[:, D:2 * D]]
            b_sin_p = [Buf("sinA"), Buf("sinB")]
            b_cos_p = [Buf("cosA"), Buf("cosB")]
            vret_s, b_vret = mk(EB, "vret_s", [128, D], BF16)
            mrq_s, b_mrq = mk(EB, "mrq_s", [128, D], BF16)
            memi_s, b_memi = mk(EB, "memi_s", [128, D], BF16)
            rs_s, b_rs = mk(EB, "rs_s", [128, D], BF16)
            xn_t, _ = mk(EB, "xn_t", [128, 2 * C * 128], BF16)
            xn_t3 = [xn_t[:, 0:C * 128].rearrange("p (c t) -> p c t", c=C),
                     xn_t[:, C * 128:].rearrange("p (c t) -> p c t", c=C)]
            b_xnt2 = [Buf("xntA"), Buf("xntB")]
            xnt_tok, _ = mk(EB, "xnt_tok", [128, 2 * D], BF16)
            b_xtok2 = [Buf("xtokA"), Buf("xtokB")]
            row_s, b_row = mk(EB, "row_s", [1, 6 * 128], F32)
            rbf_s, b_rbf = mk(EB, "rbf_s", [1, 2 * 128], BF16)
            s_stage, b_sst = mk(EB, "s_stage", [128, D], F32)

            def trig(kind, par, kp=False):
                if kp:
                    b_sinv, b_cosv = b_sink_p[par], b_cosk_p[par]
                    sin_ap, cos_ap = sink_p[par], cosk_p[par]
                else:
                    b_sinv, b_cosv = b_sin_p[par], b_cos_p[par]
                    sin_ap, cos_ap = sin_p[par], cos_p[par]
                if kind == "bcast":
                    dve.op("tensor_tensor", out=tA_s[:], in0=tA_s[:],
                           in1=psb_bc[:], op=ALU.mult,
                           reads=[b_tA, b_psbb], writes=[b_tA])
                else:
                    dve.op("tensor_tensor", out=tA3, in0=tA3,
                           in1=psb_col[:, :].unsqueeze(2).broadcast_to(
                               [128, C, 128]),
                           op=ALU.mult, reads=[b_tA, b_psbc], writes=[b_tA])
                dve.op("tensor_copy", out=ni_s[:], in_=tA_s[:],
                       reads=[b_tA], writes=[b_ni])
                dve.op("tensor_copy", out=tB_s[:], in_=ni_s[:],
                       reads=[b_ni], writes=[b_tB])
                dve.op("tensor_tensor", out=tA_s[:], in0=tA_s[:], in1=tB_s[:],
                       op=ALU.subtract, reads=[b_tA, b_tB], writes=[b_tA])
                dve.op("tensor_scalar", out=tB_s[:], in0=tA_s[:], scalar1=0.0,
                       scalar2=None, op0=ALU.is_lt, reads=[b_tA], writes=[b_tB])
                dve.op("scalar_tensor_tensor", out=tA_s[:], in0=tB_s[:],
                       scalar=1.0, in1=tA_s[:], op0=ALU.mult, op1=ALU.add,
                       reads=[b_tA, b_tB], writes=[b_tA])
                act.op("activation", out=sin_ap, in_=tA_s[:], func=AF.Sin,
                       bias=NEG_PI, scale=2 * np.pi,
                       reads=[b_tA, b_cpool], writes=[b_sinv])
                dve.op("tensor_scalar", out=tB_s[:], in0=tA_s[:], scalar1=0.75,
                       scalar2=None, op0=ALU.is_ge, reads=[b_tA], writes=[b_tB])
                dve.op("scalar_tensor_tensor", out=tA_s[:], in0=tB_s[:],
                       scalar=-1.0, in1=tA_s[:], op0=ALU.mult, op1=ALU.add,
                       reads=[b_tA, b_tB], writes=[b_tA])
                act.op("activation", out=cos_ap, in_=tA_s[:], func=AF.Sin,
                       bias=NEG_HPI, scale=2 * np.pi,
                       reads=[b_tA, b_cpool], writes=[b_cosv])

            def xt_load(j):
                par = j % 2
                jt = slice(j * 128, (j + 1) * 128)
                io.op("dma_start", out=xnt_tok[:, par * D:(par + 1) * D],
                      in_=xn_dram[jt, 0:D], reads=[b_xnd],
                      writes=[b_xtok2[par]], dsem=ds_xr[par])

            def u_xt(j):
                par = j % 2
                if j + 1 < NT:
                    xt_load(j + 1)
                kb.mm_group(
                    [("transpose", (PSB[:, c * 128:(c + 1) * 128],
                                    xnt_tok[:, par * D + c * 128:
                                            par * D + (c + 1) * 128],
                                    ident_sb[:]), {}) for c in range(C)],
                    reads=[b_xtok2[par], b_ident], writes=[bPB])
                act.op("activation", out=xn_t3[par],
                       in_=PSB.rearrange("p (c t) -> p c t", c=C),
                       func=AF.Copy, reads=[bPB], writes=[b_xnt2[par]])

            def u_v(j):
                par = j % 2
                for h in range(2):
                    hs = slice(h * 512, (h + 1) * 512)
                    kb.mm_group(
                        [("matmul", (PS3[:, hs], ones1_sb[:], bv_row[0:1, hs]),
                          dict(start=True, stop=False))] +
                        [("matmul", (PS3[:, hs], xn_t3[par][:, kc, :],
                                     Wv3[:, kc, hs]),
                          dict(start=False, stop=(kc == C - 1)))
                         for kc in range(C)],
                        reads=[b_ones1, b_r65, b_xnt2[par], b_wv2],
                        writes=[bP3a if h == 0 else bP3b])
                act.op("activation", out=vret_s[:], in_=PS3[:, 0:1024],
                       func=AF.Copy, reads=[bP3a, bP3b], writes=[b_vret])

            def u_kp(j):
                par = j % 2
                for h in range(2):
                    hs = slice(h * 512, (h + 1) * 512)
                    kb.mm_group(
                        [("matmul", (PS3[:, hs], ones1_sb[:],
                                     bkp_row[0:1, hs]),
                          dict(start=True, stop=False))] +
                        [("matmul", (PS3[:, hs], xn_t3[par][:, kc, :],
                                     Wkp3[:, kc, hs]),
                          dict(start=False, stop=(kc == C - 1)))
                         for kc in range(C)],
                        reads=[b_ones1, b_r65, b_xnt2[par], b_wkpc],
                        writes=[bP3a if h == 0 else bP3b])
                act.op("activation", out=tA_s[:], in_=PS3[:, 0:1024],
                       func=AF.Tanh, reads=[bP3a, bP3b], writes=[b_tA])
                trig("bcast", par, kp=True)
                dve.op("tensor_tensor", out=cosk_p[par], in0=cosk_p[par],
                       in1=vret_s[:], op=ALU.mult,
                       reads=[b_cosk_p[par], b_vret], writes=[b_cosk_p[par]])
                dve.op("tensor_tensor", out=sink_p[par], in0=sink_p[par],
                       in1=vret_s[:], op=ALU.mult,
                       reads=[b_sink_p[par], b_vret], writes=[b_sink_p[par]])

            def u_qp(j):
                par = j % 2
                for c in range(C):
                    cc = slice(c * 128, (c + 1) * 128)
                    kb.mm_group(
                        [("matmul", (PS3[:, cc], bqp_row[0:1, cc],
                                     ones1_sb[:]),
                          dict(start=True, stop=False))] +
                        [("matmul", (PS3[:, cc],
                                     Wqp3[:, kc, cc], xn_t3[par][:, kc, :]),
                          dict(start=False, stop=(kc == C - 1)))
                         for kc in range(C)],
                        reads=[b_r65, b_ones1, b_wqpc, b_xnt2[par]],
                        writes=[bP3a if c < 4 else bP3b])
                act.op("activation", out=tA_s[:], in_=PS3[:, 0:1024],
                       func=AF.Tanh, reads=[bP3a, bP3b], writes=[b_tA])
                trig("col", par)

            def u_t1r(j):
                par = j % 2
                kb.mm_group(
                    [("matmul", (PS3[:, c * 128:(c + 1) * 128],
                                 cosk_p[par][:, c * 128:(c + 1) * 128],
                                 cumu_sb[:]),
                      dict(start=True, stop=True)) for c in range(C)],
                    reads=[b_cosk_p[par], b_cumu], writes=[bP3a, bP3b])
                act.op("activation", out=mrq_s[:], in_=PS3[:, 0:1024],
                       func=AF.Copy, reads=[bP3a, bP3b], writes=[b_mrq])

            def u_t1i(j):
                par = j % 2
                kb.mm_group(
                    [("matmul", (PS3[:, c * 128:(c + 1) * 128],
                                 sink_p[par][:, c * 128:(c + 1) * 128],
                                 cumu_sb[:]),
                      dict(start=True, stop=True)) for c in range(C)],
                    reads=[b_sink_p[par], b_cumu], writes=[bP3a, bP3b])
                act.op("activation", out=memi_s[:], in_=PS3[:, 0:1024],
                       func=AF.Copy, reads=[bP3a, bP3b], writes=[b_memi])

            def u_t2a(j):
                par = j % 2
                dve.op("tensor_tensor", out=rs_s[:], in0=cos_p[par],
                       in1=mrq_s[:], op=ALU.mult,
                       reads=[b_cos_p[par], b_mrq], writes=[b_rs])
                dve.op("tensor_tensor", out=mrq_s[:], in0=sin_p[par],
                       in1=memi_s[:], op=ALU.mult,
                       reads=[b_sin_p[par], b_memi], writes=[b_mrq])
                dve.op("tensor_tensor", out=rs_s[:], in0=rs_s[:],
                       in1=mrq_s[:], op=ALU.add,
                       reads=[b_rs, b_mrq], writes=[b_rs])
                act.op("activation", out=mrq_s[:], in_=rs_s[:],
                       func=AF.Square, reads=[b_rs], writes=[b_mrq])
                kb.mm_group(
                    [("matmul", (PS4[0:1, 0:128], onescb_sb[:],
                                 rs_s[:, c * 128:(c + 1) * 128]),
                      dict(start=(c == 0), stop=(c == C - 1))) for c in range(C)],
                    reads=[b_onescb, b_rs], writes=[bP4])
                kb.mm_group(
                    [("matmul", (PS4[0:1, 128:256], onescb_sb[:],
                                 mrq_s[:, c * 128:(c + 1) * 128]),
                      dict(start=(c == 0), stop=(c == C - 1))) for c in range(C)],
                    reads=[b_onescb, b_mrq], writes=[bP4])
                dve.op("tensor_scalar", out=row_s[0:1, 0:256],
                       in0=PS4[0:1, 0:256], scalar1=1.0 / D, scalar2=None,
                       op0=ALU.mult, reads=[bP4], writes=[b_row])
                dve.op("tensor_tensor", out=row_s[0:1, 256:384],
                       in0=row_s[0:1, 0:128], in1=row_s[0:1, 0:128],
                       op=ALU.mult, reads=[b_row], writes=[b_row])
                dve.op("scalar_tensor_tensor", out=row_s[0:1, 256:384],
                       in0=row_s[0:1, 256:384], scalar=-1.0,
                       in1=row_s[0:1, 128:256], op0=ALU.mult, op1=ALU.add,
                       reads=[b_row], writes=[b_row])
                act.op("activation", out=row_s[0:1, 384:512],
                       in_=row_s[0:1, 256:384], func=AF.Sqrt, bias=EPS_B,
                       reads=[b_row, b_cpool], writes=[b_row])
                dve.op("reciprocal", out=row_s[0:1, 512:640],
                       in_=row_s[0:1, 384:512], reads=[b_row], writes=[b_row])
                dve.op("scalar_tensor_tensor", out=row_s[0:1, 640:768],
                       in0=row_s[0:1, 0:128], scalar=-1.0,
                       in1=row_s[0:1, 512:640], op0=ALU.mult, op1=ALU.mult,
                       reads=[b_row], writes=[b_row])
                dve.op("tensor_copy", out=rbf_s[:, 0:256],
                       in_=row_s[0:1, 512:768], reads=[b_row], writes=[b_rbf])

            def u_t2b(j):
                par = j % 2
                jt = slice(j * 128, (j + 1) * 128)
                kb.mm_group(
                    [("matmul", (PS4[:, 256:384], ones1_sb[:],
                                 rbf_s[:, 0:128]), dict(start=True, stop=True))],
                    reads=[b_ones1, b_rbf], writes=[bP4])
                dve.op("tensor_tensor",
                       out=rs_s.rearrange("p (c t) -> p c t", c=C),
                       in0=rs_s.rearrange("p (c t) -> p c t", c=C),
                       in1=PS4[:, 256:384].unsqueeze(1).broadcast_to(
                           [128, C, 128]),
                       op=ALU.mult, reads=[b_rs, bP4], writes=[b_rs])
                for h in range(2):
                    hs = slice(h * 512, (h + 1) * 512)
                    mms = [("matmul", (PS3[:, hs], rbf_s[:, 128:256],
                                       cs_row[0:1, hs]),
                            dict(start=True, stop=False)),
                           ("matmul", (PS3[:, hs], ones1_sb[:],
                                       bo_row[0:1, hs]),
                            dict(start=False, stop=False))]
                    for i in range(4):
                        c2 = 4 * h + i
                        mms.append(("matmul",
                                    (PS3[:, c2 * 128:(c2 + 1) * 128],
                                     xn_t3[par][:, c2, :], ident_sb[:]),
                                    dict(start=False, stop=False)))
                    mms += [("matmul", (PS3[:, hs],
                                        rs_s[:, kc * 128:(kc + 1) * 128],
                                        Wo3[:, kc, hs]),
                             dict(start=False, stop=(kc == C - 1)))
                            for kc in range(C)]
                    kb.mm_group(
                        mms,
                        reads=[b_rbf, b_r33, b_ones1, b_xnt2[par], b_ident,
                               b_rs, b_wo2],
                        writes=[bP3a if h == 0 else bP3b])
                act.op("activation", out=s_stage[:], in_=PS3[:, 0:1024],
                       func=AF.Copy, reads=[bP3a, bP3b], writes=[b_sst])
                io.op("dma_start", out=s_dram[jt, :], in_=s_stage[:],
                      reads=[b_sst], writes=[b_sd[j]], dsem=ds_ss)

            units = []
            units += [lambda: xt_load(0), lambda j=0: u_xt(j),
                      lambda j=0: u_qp(j), lambda j=0: u_v(j),
                      lambda j=0: u_kp(j)]
            if NT > 1:
                units += [lambda j=1: u_xt(j), lambda j=1: u_qp(j),
                          lambda j=1: u_v(j)]
            for j in range(2, NT):
                units += [lambda j=j: u_t1r(j - 2), lambda j=j: u_t1i(j - 2),
                          lambda j=j: u_t2a(j - 2), lambda j=j: u_t2b(j - 2),
                          lambda j=j: u_kp(j - 1),
                          lambda j=j: u_xt(j), lambda j=j: u_qp(j),
                          lambda j=j: u_v(j)]
            units += [lambda j=NT - 1: u_kp(j)]
            for j in (NT - 2, NT - 1):
                units += [lambda j=j: u_t1r(j), lambda j=j: u_t1i(j),
                          lambda j=j: u_t2a(j), lambda j=j: u_t2b(j)]

            total_steps = NE * (1 + NBLK + MHC + C + NBLK)
            stc = [0, 0]

            def stepv(pump=True):
                stc[0] += 1
                if not pump:
                    return
                tgt = len(units) * stc[0] // total_steps
                if stc[1] < min(tgt, len(units)):
                    units[stc[1]]()
                    stc[1] += 1

            def w1_load(e_, k):
                buf3, bb = w1bufs[k % 2]
                gio.op("dma_start", out=buf3,
                       in_=ap["w1"][e_].rearrange("(c p) h -> p c h", p=128)
                       [:, :, k * 256:(k + 1) * 256],
                       writes=[bb], dsem=ds_w1[k % 2])

            def w2_load(e_, fo):
                buf3, bb = w2bufs[fo % 2]
                gio.op("dma_start", out=buf3,
                       in_=ap["w2"][e_].rearrange("(m p) f -> p m f", p=128)
                       [:, :, fo * 128:(fo + 1) * 128],
                       writes=[bb], dsem=ds_w2[fo % 2])

            def gather_blocks(e_, blist):
                for b_ in blist:
                    gio.op("indirect_dma_start", out=xg3[:, b_, :],
                           out_offset=None, in_=xn_dram,
                           in_offset=IndirectOffsetOnAxis(
                               idxsb[:, b_ * 8 + e_:b_ * 8 + e_ + 1], 0),
                           bounds_check=rb_tok, oob_is_err=False,
                           reads=[b_idx, b_xnd], writes=[b_xg],
                           dsem=ds_xg[b_ % 2])

            def ccol_copy(e_):
                cpar = e_ % 2
                dve.op("tensor_copy",
                       out=ccol_s[:, cpar * NBLK:(cpar + 1) * NBLK],
                       in_=xg3[:, :, D + e_], reads=[b_xg],
                       writes=[b_cc2[cpar]])

            def ti_block(b_):
                kb.mm_group(
                    [("transpose", (PSB[:, c * 128:(c + 1) * 128],
                                    xg3[:, b_, c * 128:(c + 1) * 128],
                                    ident_sb[:]), {}) for c in range(C)],
                    reads=[b_xg, b_ident], writes=[bPB])
                act.op("activation",
                       out=xf3[:, :, b_ * 128:(b_ + 1) * 128],
                       in_=PSB.rearrange("p (c t) -> p c t", c=C),
                       func=AF.Copy, reads=[bPB], writes=[b_xgf])

            gather_blocks(0, range(NBLK))
            w1_load(0, 0)
            w1_load(0, 1)
            ccol_copy(0)
            stepv()
            for b_ in range(NBLK):
                ti_block(b_)
                stepv()
            ds_ys2 = [ds_ys, ds_ysb]
            sc_last = [None, None]
            for e_ in range(NE):
                for m in range(MHC):
                    k8 = m // 2
                    buf3, bb = w1bufs[k8 % 2]
                    pp, pbs = ((PS1, (bP1a, bP1b)) if m % 2 == 0
                               else (PS2, (bP2a, bP2b)))
                    kb.mm_group(
                        [("matmul", (pp[:, o_:o_ + n_],
                                     buf3[:, kc, (m % 2) * 128:(m % 2 + 1) * 128],
                                     xf3[:, kc, o_:o_ + n_]),
                          dict(start=(kc == 0), stop=(kc == C - 1)))
                         for (o_, n_) in NP for kc in range(C)],
                        reads=[bb, b_xgf], writes=list(pbs))
                    act.op("activation", out=h3[:, m, :],
                           in_=pp[:, 0:MOE_CAP], func=AF.Silu,
                           reads=list(pbs), writes=[b_h])
                    if m % 2 == 1 and k8 + 2 <= 7:
                        w1_load(e_, k8 + 2)
                    if m == 13:
                        w2_load(e_, 0)
                    if m == 15:
                        w2_load(e_, 1)
                    stepv()
                for fo in range(C):
                    buf3, bb = w2bufs[fo % 2]
                    ypp, ypbs = ((PS1, (bP1a, bP1b)) if fo % 2 == 0
                                 else (PS2, (bP2a, bP2b)))
                    kb.mm_group(
                        [("matmul", (ypp[:, o_:o_ + n_],
                                     buf3[:, m, :],
                                     h3[:, m, o_:o_ + n_]),
                          dict(start=(m == 0), stop=(m == MHC - 1)))
                         for (o_, n_) in NP for m in range(MHC)],
                        reads=[bb, b_h], writes=list(ypbs))
                    act.op("activation", out=yb3[:, fo, :],
                           in_=ypp[:, 0:MOE_CAP], func=AF.Copy,
                           reads=list(ypbs), writes=[b_xgf])
                    if fo + 2 < C:
                        w2_load(e_, fo + 2)
                    if e_ + 1 < NE and fo in (1, 3, 5):
                        gather_blocks(e_ + 1,
                                      [b for b in (fo - 1, fo) if b < NBLK])
                    if e_ + 1 < NE and fo == 6:
                        w1_load(e_ + 1, 0)
                    if e_ + 1 < NE and fo == 7:
                        w1_load(e_ + 1, 1)
                    stepv()
                if e_ + 1 < NE:
                    ccol_copy(e_ + 1)
                    stepv()
                for b_ in range(NBLK):
                    ypar = b_ % 2
                    ysv = ysc_s[:, ypar * D:(ypar + 1) * D]
                    kb.mm_group(
                        [("transpose", (PSB[:, c * 128:(c + 1) * 128],
                                        yb3[:, c, b_ * 128:(b_ + 1) * 128],
                                        ident_sb[:]), {}) for c in range(C)],
                        reads=[b_xgf, b_ident], writes=[bPB])
                    dve.op("tensor_scalar", out=ysv, in0=PSB[:, 0:1024],
                           scalar1=ccol_s[:, (e_ % 2) * NBLK + b_:
                                          (e_ % 2) * NBLK + b_ + 1],
                           scalar2=None,
                           op0=ALU.mult, reads=[bPB, b_cc2[e_ % 2]],
                           writes=[b_ysc2[ypar]])
                    deps = []
                    if b_ < 2 and sc_last[1 - ypar] is not None:
                        deps = [sc_last[1 - ypar]]
                    gio.op("indirect_dma_start", out=y_dram,
                           out_offset=IndirectOffsetOnAxis(
                               idxsb[:, b_ * 8 + e_:b_ * 8 + e_ + 1], 0),
                           in_=ysv, in_offset=None,
                           bounds_check=rb_tok, oob_is_err=False,
                           compute_op=ALU.add, deps=deps,
                           reads=[b_ysc2[ypar], b_idx], writes=[b_yd],
                           dsem=ds_ys2[ypar])
                    stepv(pump=False)
                    if e_ + 1 < NE:
                        ti_block(b_)
                        stepv(pump=False)
                sc_last = [(ds_ys2[0], ds_ys2[0].n),
                           (ds_ys2[1], ds_ys2[1].n)]

            while stc[1] < len(units):
                units[stc[1]]()
                stc[1] += 1

            if dbg == "y":
                for j in range(NT):
                    io.op("dma_start", out=out[j * 128:(j + 1) * 128, :],
                          in_=y_dram[j * 128:(j + 1) * 128, :],
                          reads=[b_yd], dsem=ds_st[j % 2])
                kb.finish()
                with nc.Block() as block:
                    kb.flush(block)
                return nc

        kb.barrier()

        with ExitStack() as SC:
            EC = SC.enter_context
            s_t, _ = mk(EC, "s_t", [128, 4 * D], F32)
            b_s2 = [Buf("s%d" % i) for i in range(4)]
            y_t, _ = mk(EC, "y_t", [128, 4 * D], F32)
            b_y2 = [Buf("y%d" % i) for i in range(4)]
            x_t, _ = mk(EC, "x_t", [128, 4 * D], F32)
            b_x2 = [Buf("x%d" % i) for i in range(4)]
            g_te, _ = mk(EC, "g_te", [128, 4 * D], BF16)
            b_g2 = [Buf("g%d" % i) for i in range(4)]
            def ep_load(j):
                par = j % 4
                jt = slice(j * 128, (j + 1) * 128)
                io.op("dma_start", out=s_t[:, par * D:(par + 1) * D],
                      in_=s_dram[jt, :],
                      reads=[b_sd[j]], writes=[b_s2[par]], dsem=ds_ep[0][par])
                gio.op("dma_start", out=y_t[:, par * D:(par + 1) * D],
                       in_=y_dram[jt, :], reads=[b_yd],
                       writes=[b_y2[par]], dsem=ds_ep[1][par])
                gio.op("dma_start", out=x_t[:, par * D:(par + 1) * D],
                       in_=ap["x"][jt, :],
                       writes=[b_x2[par]], dsem=ds_ep[2][par])

            for j in range(min(4, NT)):
                ep_load(j)
            for j in range(NT):
                par = j % 4
                jt = slice(j * 128, (j + 1) * 128)
                sv = s_t[:, par * D:(par + 1) * D]
                yv = y_t[:, par * D:(par + 1) * D]
                xv = x_t[:, par * D:(par + 1) * D]
                gv = g_te[:, par * D:(par + 1) * D]
                act.op("activation", out=gv, in_=yv, func=AF.Silu,
                       reads=[b_y2[par]], writes=[b_g2[par]])
                dve.op("tensor_tensor", out=sv, in0=sv, in1=gv,
                       op=ALU.mult, reads=[b_s2[par], b_g2[par]],
                       writes=[b_s2[par]])
                dve.op("tensor_tensor", out=xv, in0=xv, in1=sv,
                       op=ALU.add, reads=[b_x2[par], b_s2[par]],
                       writes=[b_x2[par]])
                io.op("dma_start", out=out[jt, :], in_=xv,
                      reads=[b_x2[par]], dsem=ds_ep[3][par])
                if j + 4 < NT:
                    ep_load(j + 4)

        kb.finish()
        with nc.Block() as block:
            kb.flush(block)
        return nc



_PROG_CACHE = {}


def _get_prog(n_tok):
    if n_tok not in _PROG_CACHE:
        _PROG_CACHE[n_tok] = build_program(n_tok)
    return _PROG_CACHE[n_tok]


def kernel(**inputs):
    x = np.ascontiguousarray(
        np.asarray(inputs["L0_attn_x"], dtype=np.float32)).reshape(S_FULL, D)
    w = {
        "Wk": inputs["L0_attn_Wk"], "Wv": inputs["L0_attn_Wv"],
        "Wq": inputs["L0_attn_Wq"], "Wkp": inputs["L0_attn_Wkp"],
        "Wqp": inputs["L0_attn_Wqp"], "Wo": inputs["L0_attn_Wo"],
        "bk": inputs["L0_attn_bk"], "bv": inputs["L0_attn_bv"],
        "bq": inputs["L0_attn_bq"], "bkp": inputs["L0_attn_bkp"],
        "bqp": inputs["L0_attn_bqp"], "ps": inputs["L0_attn_phase_scale"],
        "lng": inputs["L0_attn_ln_g"], "lnb": inputs["L0_attn_ln_b"],
        "bo": inputs["L0_attn_bo"], "gw": inputs["L0_ffn_gate_w"],
        "w1": inputs["L0_ffn_w1"], "w2": inputs["L0_ffn_w2"],
    }
    w = {k: np.ascontiguousarray(np.asarray(v, dtype=np.float32))
         for k, v in w.items()}
    n_per = S_FULL // N_CORES
    nc = _get_prog(n_per)
    in_maps = []
    for c in range(N_CORES):
        m = {"x": x[c * n_per:(c + 1) * n_per]}
        m.update(w)
        in_maps.append(m)
    res = run_bass_kernel_spmd(nc, in_maps, core_ids=list(range(N_CORES)))
    outs = [np.asarray(res.results[c]["out"]) for c in range(N_CORES)]
    return np.concatenate(outs, axis=0).reshape(1, S_FULL, D)


# version 26
# speedup vs baseline: 1.0890x; 1.0103x over previous
import numpy as np
import concourse.bass as bass
import concourse.mybir as mybir
from concourse.bass_utils import run_bass_kernel_spmd
from concourse.bass_core_rust import IndirectOffsetOnAxis
import ml_dtypes
from contextlib import ExitStack

F32 = mybir.dt.float32
BF16 = mybir.dt.bfloat16
I32 = mybir.dt.int32
AF = mybir.ActivationFunctionType
ALU = mybir.AluOpType
AX = mybir.AxisListType

D = 1024
C = 8
NE = 8
DFF = 2048
MHC = 16
INV2PI = 0.15915494309189535
LN2EPS = 1.024e-2

N_CORES = 8
S_FULL = 16384
MOE_CAP = 640



class Buf:
    __slots__ = ("name", "w", "readers")

    def __init__(self, name):
        self.name = name
        self.w = None
        self.readers = []


class DSem:

    def __init__(self, name):
        self.name = name
        self.n = 0
        self.sem = None


class KQ:
    def __init__(self, kb, bname, fifo=False, dma=False, auto_inc=False):
        self.kb = kb
        self.bname = bname
        self.fifo = fifo
        self.dma = dma
        self.auto_inc = auto_inc
        self.n = 0
        self.sem = None
        self._emit_n = 0
        self.seen = {}

    def _merge(self, d):
        waits = {}
        for (q, v) in d:
            if q is self and self.fifo:
                continue
            if self.seen.get(id(q), -1) >= v:
                continue
            if waits.get(id(q), (None, -1))[1] < v:
                waits[id(q)] = (q, v)
        for k, (q, v) in waits.items():
            self.seen[k] = v
        return list(waits.values())

    def op(self, method, *args, reads=(), writes=(), deps=(), inc=False,
           dsem=None, **kw):
        d = []
        for b in reads:
            if b.w is not None:
                d.append(b.w)
        for b in writes:
            if b.w is not None:
                d.append(b.w)
            d.extend(b.readers)
        d.extend(deps)
        if self.dma:
            assert dsem is not None, "DMA ops need a dsem"
            if dsem.n > 0:
                d.append((dsem, dsem.n))
        waits = self._merge(d)
        if self.dma:
            self.kb.rec.append((self, method, args, kw, waits, (dsem, 16)))
            dsem.n += 16
            t = (dsem, dsem.n)
        else:
            myinc = 1 if (self.auto_inc or inc) else 0
            self.kb.rec.append((self, method, args, kw, waits, myinc))
            if myinc:
                self.n += myinc
            t = (self, self.n)
        for b in reads:
            b.readers.append(t)
        for b in writes:
            b.w = t
            b.readers = []
        return t


class KB:
    def __init__(self, nc):
        self.nc = nc
        self.rec = []
        self.pe = KQ(self, "tensor", fifo=True)
        self.dve = KQ(self, "vector", auto_inc=True)
        self.act = KQ(self, "scalar", auto_inc=True)
        self.io = KQ(self, "sync", dma=True)
        self.gio = KQ(self, "gpsimd", dma=True)
        self.pool = KQ(self, "gpsimd")
        self.queues = [self.pe, self.dve, self.act, self.io, self.gio,
                       self.pool]
        self.dsems = []

    def dsem(self, name):
        d = DSem(name)
        self.dsems.append(d)
        return d

    def _all_tags(self):
        tags = [(q, q.n) for q in self.queues if not q.dma and q.n > 0]
        tags += [(d, d.n) for d in self.dsems if d.n > 0]
        return tags

    def barrier(self, skip=()):
        tags = self._all_tags()
        for q in self.queues:
            if q in skip:
                continue
            deps = [t for t in tags if t[0] is not q]
            deps = [(dq, dv) for (dq, dv) in deps
                    if q.seen.get(id(dq), -1) < dv]
            for (dq, dv) in deps:
                q.seen[id(dq)] = dv
            if deps:
                self.rec.append((q, None, None, None, deps, 0))

    def mm_group(self, mms, reads=(), writes=(), deps=()):
        d = []
        for b in reads:
            if b.w is not None:
                d.append(b.w)
        for b in writes:
            if b.w is not None:
                d.append(b.w)
            d.extend(b.readers)
        d.extend(deps)
        waits = self.pe._merge(d)
        for i, (method, args, kw) in enumerate(mms):
            w = waits if i == 0 else []
            inc = 1 if i == len(mms) - 1 else 0
            self.rec.append((self.pe, method, args, kw, w, inc))
        self.pe.n += 1
        t = (self.pe, self.pe.n)
        for b in reads:
            b.readers.append(t)
        for b in writes:
            b.w = t
            b.readers = []
        return t

    def finish(self):
        tags = self._all_tags()
        self.rec.append((self.io, None, None, None, tags, 0))

    def flush(self, block):
        per_q = {}
        for r in self.rec:
            per_q.setdefault(r[0].bname, []).append(r)
        for chunk in per_q.values():
            bname = chunk[0][0].bname

            def emit(eng, chunk=chunk):
                for (q, method, args, kw, waits, inc) in chunk:
                    if method is None:
                        for (dq, dv) in waits:
                            eng.wait_ge(dq.sem, dv)
                        continue
                    allw = list(waits)
                    for (dq, dv) in allw[1:]:
                        eng.wait_ge(dq.sem, dv)
                    ins = getattr(eng, method)(*args, **kw)
                    for (dq, dv) in allw[:1]:
                        ins._wait_ge(dq.sem, dv)
                    if inc:
                        if isinstance(inc, tuple):
                            ins.then_inc(inc[0].sem, inc[1])
                        else:
                            ins.then_inc(q.sem, inc)
                            q._emit_n += inc
            getattr(block, bname)(emit)



def build_program(n_tok, dbg=None):
    assert n_tok % 128 == 0
    NT = n_tok // 128
    NBLK = MOE_CAP // 128
    nc = bass.Bass("TRN2", target_bir_lowering=False)

    ap = {}
    ap["x"] = nc.dram_tensor("x", [n_tok, D], F32, kind="ExternalInput").ap()
    for nm in ["Wk", "Wv", "Wq", "Wkp", "Wqp", "Wo"]:
        ap[nm] = nc.dram_tensor(nm, [D, D], F32, kind="ExternalInput").ap()
    for nm in ["bk", "bv", "bq", "bkp", "bqp", "ps", "lng", "lnb", "bo"]:
        ap[nm] = nc.dram_tensor(nm, [D], F32, kind="ExternalInput").ap()
    ap["gw"] = nc.dram_tensor("gw", [D, NE], F32, kind="ExternalInput").ap()
    ap["w1"] = nc.dram_tensor("w1", [NE, D, DFF], F32, kind="ExternalInput").ap()
    ap["w2"] = nc.dram_tensor("w2", [NE, DFF, D], F32, kind="ExternalInput").ap()
    out = nc.dram_tensor("out", [n_tok, D], F32, kind="ExternalOutput").ap()
    out_flat = out.rearrange("a b -> (a b)")

    ident_c = nc.inline_tensor(np.eye(128, dtype=ml_dtypes.bfloat16), "identc")
    u = np.zeros((128, 128), dtype=np.float32)
    for blk in range(2):
        for a_ in range(64):
            u[blk * 64 + a_, blk * 64 + a_:blk * 64 + 64] = 1.0
    cumu_c = nc.inline_tensor(u.astype(ml_dtypes.bfloat16), "cumuc")
    ones1_c = nc.inline_tensor(np.ones((1, 128), dtype=ml_dtypes.bfloat16), "ones1c")
    onescb_c = nc.inline_tensor(np.ones((128, 1), dtype=ml_dtypes.bfloat16), "onescbc")
    ones1f_c = nc.inline_tensor(np.ones((1, 128), dtype=np.float32), "ones1fc")
    ones11f_c = nc.inline_tensor(np.ones((1, 1), dtype=np.float32), "ones11fc")
    iota8c = nc.inline_tensor(np.arange(8, dtype=np.float32).reshape(8, 1), "iota8c")
    ones8c = nc.inline_tensor(np.ones((8, 1), dtype=np.float32), "ones8c")
    iotac = nc.inline_tensor(
        (np.arange(128)[:, None] + 128 * np.arange(16)[None, :]).astype(np.int32),
        "iotac")

    xn_dram = nc.dram_tensor("xn_dram", [n_tok, D + 8], BF16, kind="Internal").ap()
    idx_dram = nc.dram_tensor("idx_dram", [8 * MOE_CAP, 1], I32, kind="Internal").ap()
    y_dram = nc.dram_tensor("y_dram", [n_tok, D], F32, kind="Internal").ap()
    s_dram = nc.dram_tensor("s_dram", [n_tok, D], F32, kind="Internal").ap()

    kb = KB(nc)
    pe, dve, act, io, gio = kb.pe, kb.dve, kb.act, kb.io, kb.gio

    with ExitStack() as S0:
        E0 = S0.enter_context

        def mk(E, name, shape, dtype):
            t = E(nc.sbuf_tensor(name, shape, dtype))
            return t, Buf(name)

        ident_sb, b_ident = mk(E0, "ident_sb", [128, 128], BF16)
        cumu_sb, b_cumu = mk(E0, "cumu_sb", [128, 128], BF16)
        ones1_sb, b_ones1 = mk(E0, "ones1_sb", [1, 128], BF16)
        onescb_sb, b_onescb = mk(E0, "onescb_sb", [128, 1], BF16)
        ones1f_sb, b_ones1f = mk(E0, "ones1f_sb", [1, 128], F32)
        ones11f_sb, b_ones11f = mk(E0, "ones11f_sb", [1, 1], F32)
        cpool, b_cpool = mk(E0, "cpool", [128, 3], F32)
        idxsb, b_idx = mk(E0, "idxsb", [128, NBLK * 8], I32)

        Wv_sb2, b_wv2 = mk(E0, "Wv_sb", [128, C * D], BF16)
        Wv3 = Wv_sb2.rearrange("p (c n) -> p c n", c=C)
        Wkp_c, b_wkpc = mk(E0, "Wkp_c", [128, C * D], BF16)
        Wkp3 = Wkp_c.rearrange("p (c n) -> p c n", c=C)
        Wqp_c, b_wqpc = mk(E0, "Wqp_c", [128, C * D], BF16)
        Wqp3 = Wqp_c.rearrange("p (c n) -> p c n", c=C)
        Wo_sb2, b_wo2 = mk(E0, "Wo_sb", [128, C * D], BF16)
        Wo3 = Wo_sb2.rearrange("p (c n) -> p c n", c=C)
        bvr_t, b_r65 = mk(E0, "bvr_t", [1, 3 * D], BF16)
        bv_row = bvr_t[0:1, 0:D]
        bkp_row = bvr_t[0:1, D:2 * D]
        bqp_row = bvr_t[0:1, 2 * D:3 * D]
        bor_t, b_r33 = mk(E0, "bor_t", [1, 2 * D], BF16)
        bo_row = bor_t[0:1, 0:D]
        cs_row = bor_t[0:1, D:2 * D]
        psb_col, b_psbc = mk(E0, "psb_col", [128, C], F32)
        psb_bc, b_psbb = mk(E0, "psb_bc", [128, D], F32)

        PS1 = E0(nc.psum_tensor("PS1", [128, 1024], F32))
        PS2 = E0(nc.psum_tensor("PS2", [128, 1024], F32))
        PS3 = E0(nc.psum_tensor("PS3", [128, 1024], F32))
        PS4 = E0(nc.psum_tensor("PS4", [128, 512], F32))
        PSB = E0(nc.psum_tensor("PSB", [128, 1024], BF16))
        bP1a, bP1b, bP2a, bP2b, bP3a, bP3b, bP4, bPB = (
            Buf(x_) for x_ in ["PS1a", "PS1b", "PS2a", "PS2b",
                               "PS3a", "PS3b", "PS4", "PSB"])

        for q, nm in [(pe, "s_pe"), (dve, "s_dve"), (act, "s_act"),
                      (kb.pool, "s_pool")]:
            q.sem = E0(nc.semaphore(nm))
        ds_cst = kb.dsem("cst")
        ds_cs2 = kb.dsem("cs2")
        ds_cs3 = kb.dsem("cs3")
        ds_xt = [kb.dsem("xtA"), kb.dsem("xtB")]
        ds_st = [kb.dsem("stA"), kb.dsem("stB")]
        ds_wa = kb.dsem("wa")
        ds_wp = kb.dsem("wp")
        ds_wv = kb.dsem("wv")
        ds_rb = kb.dsem("rb")
        ds_w1 = [kb.dsem("w1A"), kb.dsem("w1B")]
        ds_w2 = [kb.dsem("w2A"), kb.dsem("w2B")]
        ds_dbg = kb.dsem("dbg")
        ds_xnd = kb.dsem("xnd")
        ds_sc4 = [kb.dsem("sc%d" % i) for i in range(4)]
        ds_sc = ds_sc4[0]
        ds_xg = [kb.dsem("xgA"), kb.dsem("xgB")]
        ds_ys = kb.dsem("ys")
        ds_ysb = kb.dsem("ysb")
        ds_ep = [[kb.dsem("ep%d_%d" % (t_, i)) for i in range(4)]
                 for t_ in range(4)]
        ds_yz = kb.dsem("yz")
        ds_yr = [kb.dsem("yrA"), kb.dsem("yrB")]
        ds_xr = [kb.dsem("xrA"), kb.dsem("xrB")]
        ds_ss = kb.dsem("ss")
        ds_fs = [kb.dsem("fsA"), kb.dsem("fsB")]
        for d_ in kb.dsems:
            d_.sem = E0(nc.semaphore("sd_" + d_.name))

        rb_tok = nc.gpsimd.alloc_register("rb_tok")
        rb_slot = nc.gpsimd.alloc_register("rb_slot")
        kb.pool.op("reg_mov", rb_tok, n_tok - 1)
        kb.pool.op("reg_mov", rb_slot, 8 * MOE_CAP - 1)

        io.op("dma_start", out=ident_sb[:], in_=ident_c.ap(), writes=[b_ident],
              dsem=ds_cst)
        io.op("dma_start", out=cumu_sb[:], in_=cumu_c.ap(), writes=[b_cumu],
              dsem=ds_cs2)
        io.op("dma_start", out=ones1_sb[:], in_=ones1_c.ap(), writes=[b_ones1],
              dsem=ds_cs3)
        io.op("dma_start", out=onescb_sb[:], in_=onescb_c.ap(), writes=[b_onescb],
              dsem=ds_cst)
        io.op("dma_start", out=ones1f_sb[:], in_=ones1f_c.ap(), writes=[b_ones1f],
              dsem=ds_cs2)
        io.op("dma_start", out=ones11f_sb[:], in_=ones11f_c.ap(),
              writes=[b_ones11f], dsem=ds_cs3)
        dve.op("memset", cpool[:, 0:1], -np.pi, writes=[b_cpool])
        dve.op("memset", cpool[:, 1:2], LN2EPS, writes=[b_cpool])
        dve.op("memset", cpool[:, 2:3], -np.pi / 2, writes=[b_cpool])
        NEG_PI = cpool[:, 0:1]
        EPS_B = cpool[0:1, 1:2]
        NEG_HPI = cpool[:, 2:3]

        b_xnd = Buf("xn_dram")
        b_idxd = Buf("idx_dram")
        b_idxd4 = [Buf("idx_dram%d" % i) for i in range(4)]
        b_yd = Buf("y_dram")
        b_sd = [Buf("s_dram%d" % j) for j in range(n_tok // 128)]

        with ExitStack() as SA:
            EA = SA.enter_context
            rows_b, b_rowsb = mk(EA, "rows_b", [1, 3 * D], BF16)
            r_bkp, r_bqp, r_bo = \
                (rows_b[0:1, i * D:(i + 1) * D] for i in range(3))
            ps_row, b_psr = mk(EA, "ps_row", [1, D], F32)
            cols8, b_cols8 = mk(EA, "cols8", [8, 4 * 128], F32)
            c_bk, c_bq, c_lnb, c_lng = \
                (cols8[0:8, i * 128:(i + 1) * 128] for i in range(4))
            wtmp, b_wtmp = mk(EA, "wtmp", [128, C * D], BF16)
            wp, b_wp = mk(EA, "wp", [128, C * D], BF16)
            wtmp3 = wtmp.rearrange("p (c n) -> p c n", c=C)
            wp3 = wp.rearrange("p (c n) -> p c n", c=C)
            colb_s, b_colb = mk(EA, "colb_s", [128, 8], BF16)
            colf_s, b_colf = mk(EA, "colf_s", [128, 8], F32)
            rtmp, b_rtmp = mk(EA, "rtmp", [1, D], F32)
            c8b_s, b_c8b = mk(EA, "c8b_s", [8, 128], BF16)
            xtile, _ = mk(EA, "xtile", [128, 2 * D], F32)
            b_xt2 = [Buf("xtA"), Buf("xtB")]
            sq_s, b_sq = mk(EA, "sq_s", [128, D], F32)
            st_s, _ = mk(EA, "st_s", [128, 16], F32)
            b_st2 = [Buf("stA"), Buf("stB")]
            xnt_s, _ = mk(EA, "xnt_s", [128, 2 * D], BF16)
            b_xnt2p = [Buf("xntpA"), Buf("xntpB")]
            xn_rt, _ = mk(EA, "xn_rt", [128, 2 * C * 128], BF16)
            xn_rt3 = [xn_rt[:, 0:C * 128].rearrange("p (c t) -> p c t", c=C),
                      xn_rt[:, C * 128:].rearrange("p (c t) -> p c t", c=C)]
            b_xrt2 = [Buf("xrtA"), Buf("xrtB")]
            c_rows, b_crows = mk(EA, "c_rows", [8, n_tok], BF16)
            gwb_s, b_gwb = mk(EA, "gwb_s", [128, C * NE], BF16)
            gw_sb, b_gw = mk(EA, "gw_sb", [128, C * NE], F32)
            iota_sb, b_iot = mk(EA, "iota_sb", [128, 16], I32)
            iota8_sb, b_io8 = mk(EA, "iota8_sb", [8, 1], F32)
            ones8_sb, b_on8 = mk(EA, "ones8_sb", [8, 1], F32)
            poscol, b_pc = mk(EA, "poscol", [128, 32], I32)
            lg_s, b_lg = mk(EA, "lg_s", [128, NT * 8], F32)
            mx_s, b_mx = mk(EA, "mx_s", [128, 4 * NT], F32)
            eq_s, _ = mk(EA, "eq_s", [128, 2 * 16], F32)
            b_eq2 = [Buf("eqA"), Buf("eqB")]
            eqb_s, _ = mk(EA, "eqb_s", [128, 16], BF16)
            b_eqb2 = [Buf("eqbA"), Buf("eqbB")]
            ct_s, _ = mk(EA, "ct_s", [128, 16], BF16)
            b_ct2 = [Buf("ctA"), Buf("ctB")]
            eq1em, b_e1m = mk(EA, "eq1em", [8, n_tok], BF16)
            mask_s, b_mask = mk(EA, "mask_s", [8, n_tok], F32)
            incl_s, b_incl = mk(EA, "incl_s", [8, n_tok], F32)
            eq2em, b_e2m = mk(EA, "eq2em", [8, n_tok], F32)
            prod_s, b_prod = mk(EA, "prod_s", [8, n_tok], F32)
            posg, b_posg = mk(EA, "posg", [1, 2 * n_tok], F32)

            for j0 in range(min(2, NT)):
                io.op("dma_start", out=xtile[:, j0 * D:(j0 + 1) * D],
                      in_=ap["x"][j0 * 128:(j0 + 1) * 128, :],
                      writes=[b_xt2[j0]], dsem=ds_xt[j0])

            dve.op("memset", idxsb[:], 10 ** 6, writes=[b_idx])
            gio.op("dma_start",
                   out=idx_dram.rearrange("(b p e) o -> p b (e o)",
                                          p=128, b=NBLK),
                   in_=idxsb.rearrange("p (b e) -> p b e", b=NBLK),
                   reads=[b_idx], writes=[b_idxd] + b_idxd4, dsem=ds_sc)
            gio.op("dma_start", out=wtmp3,
                   in_=ap["Wk"].rearrange("(c p) n -> p c n", p=128),
                   writes=[b_wtmp], dsem=ds_wa)
            gio.op("dma_start", out=wp3,
                   in_=ap["Wkp"].rearrange("(c p) n -> p c n", p=128),
                   writes=[b_wp], dsem=ds_wp)
            for i, nm in enumerate(["bkp", "bqp", "bo"]):
                gio.op("dma_start", out=rows_b[0:1, i * D:(i + 1) * D],
                       in_=ap[nm].rearrange("(o n) -> o n", o=1),
                       writes=[b_rowsb], dsem=ds_rb)
            gio.op("dma_start", out=Wv3,
                   in_=ap["Wv"].rearrange("(c p) n -> p c n", p=128),
                   writes=[b_wv2], dsem=ds_wv)
            gio.op("dma_start", out=bv_row,
                   in_=ap["bv"].rearrange("(o n) -> o n", o=1),
                   writes=[b_r65], dsem=ds_wv)
            io.op("dma_start", out=gw_sb.rearrange("p (c e) -> p c e", c=C),
                  in_=ap["gw"].rearrange("(c p) e -> p c e", p=128),
                  writes=[b_gw], dsem=ds_cst)
            io.op("dma_start", out=iota_sb[:], in_=iotac.ap(), writes=[b_iot],
                  dsem=ds_cs2)
            io.op("dma_start", out=iota8_sb[:], in_=iota8c.ap(), writes=[b_io8],
                  dsem=ds_cs3)
            io.op("dma_start", out=ones8_sb[:], in_=ones8c.ap(), writes=[b_on8],
                  dsem=ds_cs2)
            io.op("dma_start", out=ps_row[:],
                  in_=ap["ps"].rearrange("(o n) -> o n", o=1),
                  writes=[b_psr], dsem=ds_cst)
            for i, nm in enumerate(["bk", "bq", "lnb", "lng"]):
                io.op("dma_start", out=cols8[0:8, i * 128:(i + 1) * 128],
                      in_=ap[nm].rearrange("(c p) -> c p", c=8),
                      writes=[b_cols8], dsem=[ds_cst, ds_cs2, ds_cs3][i % 3])
            dve.op("tensor_copy", out=gwb_s[:], in_=gw_sb[:], reads=[b_gw],
                   writes=[b_gwb])
            gwb3 = gwb_s.rearrange("p (c e) -> p c e", c=C)

            prep_units = []

            def transpose_inplace_units(w3, bw):
                for ci in range(C):
                    def tci(ci=ci, w3=w3, bw=bw):
                        for cf in range(ci, C):
                            mms = [("transpose",
                                    (PSB[:, 0:128],
                                     w3[:, ci, cf * 128:(cf + 1) * 128],
                                     ident_sb[:]), {})]
                            if cf != ci:
                                mms.append(("transpose",
                                            (PSB[:, 128:256],
                                             w3[:, cf, ci * 128:(ci + 1) * 128],
                                             ident_sb[:]), {}))
                            kb.mm_group(mms, reads=[bw, b_ident], writes=[bPB])
                            dve.op("tensor_copy",
                                   out=w3[:, cf, ci * 128:(ci + 1) * 128],
                                   in_=PSB[:, 0:128], reads=[bPB], writes=[bw])
                            if cf != ci:
                                dve.op("tensor_copy",
                                       out=w3[:, ci, cf * 128:(cf + 1) * 128],
                                       in_=PSB[:, 128:256],
                                       reads=[bPB], writes=[bw])
                    prep_units.append(tci)

            def colsum_unit(brow_ap, out_row_bf, b_orow, plain=False):
                def cu():
                    for h in range(2):
                        kb.mm_group(
                            [("matmul", (PS3[0:1, 0:512], colb_s[:, kc:kc + 1],
                                         wp3[:, kc, h * 512:(h + 1) * 512]),
                              dict(start=(kc == 0), stop=(kc == C - 1)))
                             for kc in range(C)],
                            reads=[b_colb, b_wp], writes=[bP3a])
                        if plain:
                            dve.op("tensor_copy",
                                   out=rtmp[0:1, h * 512:(h + 1) * 512],
                                   in_=PS3[0:1, 0:512], reads=[bP3a],
                                   writes=[b_rtmp])
                        else:
                            dve.op("tensor_tensor",
                                   out=rtmp[0:1, h * 512:(h + 1) * 512],
                                   in0=PS3[0:1, 0:512],
                                   in1=brow_ap[0:1, h * 512:(h + 1) * 512],
                                   op=ALU.add, reads=[bP3a, b_rowsb],
                                   writes=[b_rtmp])
                    dve.op("tensor_copy", out=out_row_bf, in_=rtmp[:],
                           reads=[b_rtmp], writes=[b_orow])
                prep_units.append(cu)

            def colb_unit(bcol_ap, also_f32=False):
                def cb():
                    dve.op("tensor_copy", out=c8b_s[:], in_=bcol_ap,
                           reads=[b_cols8], writes=[b_c8b])
                    kb.mm_group([("transpose", (PSB[:, 0:8], c8b_s[:],
                                                ident_sb[0:8, 0:8]), {})],
                                reads=[b_c8b, b_ident], writes=[bPB])
                    dve.op("tensor_copy", out=colb_s[:], in_=PSB[:, 0:8],
                           reads=[bPB], writes=[b_colb])
                    if also_f32:
                        dve.op("tensor_copy", out=colf_s[:], in_=PSB[:, 0:8],
                               reads=[bPB], writes=[b_colf])
                prep_units.append(cb)

            def combine_units(dst3, b_dst, w_name, wp_name, bcol_ap, brow_ap,
                              out_row_bf, b_orow, preloaded):
                if not preloaded:
                    def ld():
                        gio.op("dma_start", out=wtmp3,
                               in_=ap[w_name].rearrange(
                                   "(c p) n -> p c n", p=128),
                               writes=[b_wtmp], dsem=ds_wa)
                        gio.op("dma_start", out=wp3,
                               in_=ap[wp_name].rearrange(
                                   "(c p) n -> p c n", p=128),
                               writes=[b_wp], dsem=ds_wp)
                    prep_units.append(ld)
                transpose_inplace_units(wtmp3, b_wtmp)
                colb_unit(bcol_ap)
                for mc in range(C):
                    def dmc(mc=mc, dst3=dst3, b_dst=b_dst):
                        for h in range(2):
                            kb.mm_group(
                                [("matmul", (PS3[:, h * 512:(h + 1) * 512],
                                             wtmp3[:, kc,
                                                   mc * 128:(mc + 1) * 128],
                                             wp3[:, kc,
                                                 h * 512:(h + 1) * 512]),
                                  dict(start=(kc == 0), stop=(kc == C - 1)))
                                 for kc in range(C)],
                                reads=[b_wtmp, b_wp],
                                writes=[bP3a if h == 0 else bP3b])
                        dve.op("tensor_copy", out=dst3[:, mc, :],
                               in_=PS3[:, 0:1024], reads=[bP3a, bP3b],
                               writes=[b_dst])
                    prep_units.append(dmc)
                colsum_unit(brow_ap, out_row_bf, b_orow)

            combine_units(Wkp3, b_wkpc, "Wk", "Wkp", c_bk, r_bkp,
                          bkp_row, b_r65, True)
            combine_units(Wqp3, b_wqpc, "Wq", "Wqp", c_bq, r_bqp,
                          bqp_row, b_r65, False)

            def wo_ld():
                gio.op("dma_start", out=wp3,
                       in_=ap["Wo"].rearrange("(c p) n -> p c n", p=128),
                       writes=[b_wp], dsem=ds_wp)
            prep_units.append(wo_ld)
            colb_unit(c_lnb)
            colsum_unit(r_bo, bo_row, b_r33)
            colb_unit(c_lng, also_f32=True)
            colsum_unit(None, cs_row, b_r33, plain=True)

            def wo_scale():
                for c in range(C):
                    dve.op("tensor_scalar", out=Wo3[:, c, :], in0=wp3[:, c, :],
                           scalar1=colf_s[:, c:c + 1], scalar2=None,
                           op0=ALU.mult, reads=[b_wp, b_colf], writes=[b_wo2])
            prep_units.append(wo_scale)

            def psb_unit():
                for c in range(C):
                    kb.mm_group(
                        [("matmul", (PS4[:, 8 + c:9 + c],
                                     ps_row[0:1, c * 128:(c + 1) * 128],
                                     ones11f_sb[:]),
                          dict(start=True, stop=True))],
                        reads=[b_psr, b_ones11f], writes=[bP4])
                dve.op("tensor_scalar", out=psb_col[:], in0=PS4[:, 8:8 + C],
                       scalar1=INV2PI, scalar2=None, op0=ALU.mult,
                       reads=[bP4], writes=[b_psbc])
                for h in range(2):
                    kb.mm_group(
                        [("matmul", (PS3[:, h * 512:(h + 1) * 512],
                                     ones1f_sb[:],
                                     ps_row[0:1, h * 512:(h + 1) * 512]),
                          dict(start=True, stop=True))],
                        reads=[b_ones1f, b_psr],
                        writes=[bP3a if h == 0 else bP3b])
                dve.op("tensor_scalar", out=psb_bc[:], in0=PS3[:, 0:1024],
                       scalar1=INV2PI, scalar2=None, op0=ALU.mult,
                       reads=[bP3a, bP3b], writes=[b_psbb])
            prep_units.append(psb_unit)

            pp_i = [0]

            def prep_pump(k=1):
                n = 0
                while pp_i[0] < len(prep_units) and n < k:
                    prep_units[pp_i[0]]()
                    pp_i[0] += 1
                    n += 1

            def p0a(j):
                par = j % 2
                bx = b_xt2[par]
                bst = b_st2[par]
                sb = par * 8
                xt = xtile[:, par * D:(par + 1) * D]
                if j >= 2:
                    io.op("dma_start", out=xt,
                          in_=ap["x"][j * 128:(j + 1) * 128, :],
                          writes=[bx], dsem=ds_xt[par])
                dve.op("reduce_sum", out=st_s[:, sb:sb + 1], in_=xt, axis=AX.X,
                       reads=[bx], writes=[bst])
                act.op("activation", out=sq_s[:], in_=xt, func=AF.Square,
                       accum_out=st_s[:, sb + 1:sb + 2], reads=[bx],
                       writes=[b_sq, bst])
                dve.op("tensor_scalar", out=st_s[:, sb + 2:sb + 3],
                       in0=st_s[:, sb:sb + 1],
                       scalar1=1.0 / D, scalar2=None, op0=ALU.mult,
                       reads=[bst], writes=[bst])
                dve.op("tensor_scalar", out=st_s[:, sb + 3:sb + 4],
                       in0=st_s[:, sb + 1:sb + 2],
                       scalar1=1.0 / D, scalar2=None, op0=ALU.mult,
                       reads=[bst], writes=[bst])
                dve.op("tensor_tensor", out=st_s[:, sb + 4:sb + 5],
                       in0=st_s[:, sb + 2:sb + 3], in1=st_s[:, sb + 2:sb + 3],
                       op=ALU.mult, reads=[bst], writes=[bst])
                dve.op("scalar_tensor_tensor", out=st_s[:, sb + 4:sb + 5],
                       in0=st_s[:, sb + 4:sb + 5],
                       scalar=-1.0, in1=st_s[:, sb + 3:sb + 4],
                       op0=ALU.mult, op1=ALU.add, reads=[bst], writes=[bst])
                dve.op("tensor_scalar", out=st_s[:, sb + 4:sb + 5],
                       in0=st_s[:, sb + 4:sb + 5],
                       scalar1=1e-5, scalar2=None, op0=ALU.add,
                       reads=[bst], writes=[bst])
                act.op("activation", out=st_s[:, sb + 5:sb + 6],
                       in_=st_s[:, sb + 4:sb + 5],
                       func=AF.Sqrt, reads=[bst], writes=[bst])

            def p0b(j):
                par = j % 2
                bx = b_xt2[par]
                bst = b_st2[par]
                sb = par * 8
                xt = xtile[:, par * D:(par + 1) * D]
                xnt = xnt_s[:, par * D:(par + 1) * D]
                dve.op("reciprocal", out=st_s[:, sb + 6:sb + 7],
                       in_=st_s[:, sb + 5:sb + 6], reads=[bst], writes=[bst])
                dve.op("tensor_scalar", out=xnt, in0=xt,
                       scalar1=st_s[:, sb + 2:sb + 3],
                       scalar2=st_s[:, sb + 6:sb + 7],
                       op0=ALU.subtract, op1=ALU.mult,
                       reads=[bx, bst], writes=[b_xnt2p[par]])
                io.op("dma_start", out=xn_dram[j * 128:(j + 1) * 128, 0:D],
                      in_=xnt, reads=[b_xnt2p[par]], writes=[b_xnd],
                      dsem=ds_xnd)
                kb.mm_group(
                    [("transpose", (PSB[:, c * 128:(c + 1) * 128],
                                    xnt[:, c * 128:(c + 1) * 128],
                                    ident_sb[:]), {}) for c in range(C)],
                    reads=[b_xnt2p[par], b_ident], writes=[bPB])
                dve.op("tensor_copy", out=xn_rt3[par],
                       in_=PSB.rearrange("p (c t) -> p c t", c=C),
                       reads=[bPB], writes=[b_xrt2[par]])

            def rt1(j):
                par = j % 2
                beq = b_eq2[par]
                eb = par * 16
                kb.mm_group(
                    [("matmul", (PS4[:, 0:8], xn_rt3[par][:, kc, :],
                                 gwb3[:, kc, :]),
                      dict(start=(kc == 0), stop=(kc == C - 1)))
                     for kc in range(C)],
                    reads=[b_xrt2[par], b_gwb], writes=[bP4])
                dve.op("tensor_copy", out=lg_s[:, j * 8:(j + 1) * 8],
                       in_=PS4[:, 0:8], reads=[bP4], writes=[b_lg])
                dve.op("reduce_max", out=mx_s[:, j:j + 1],
                       in_=lg_s[:, j * 8:(j + 1) * 8], axis=AX.X,
                       reads=[b_lg], writes=[b_mx])
                dve.op("tensor_scalar", out=eq_s[:, eb:eb + 8],
                       in0=lg_s[:, j * 8:(j + 1) * 8],
                       scalar1=mx_s[:, j:j + 1], scalar2=None,
                       op0=ALU.is_equal, reads=[b_lg, b_mx], writes=[beq])
                dve.op("tensor_copy", out=eqb_s[:, par * 8:par * 8 + 8],
                       in_=eq_s[:, eb:eb + 8],
                       reads=[beq], writes=[b_eqb2[par]])
                kb.mm_group([("transpose", (PSB[0:8, 0:128],
                                            eqb_s[:, par * 8:par * 8 + 8],
                                            ident_sb[:]), {})],
                            reads=[b_eqb2[par], b_ident], writes=[bPB])
                dve.op("tensor_copy",
                       out=eq1em[0:8, j * 128:(j + 1) * 128],
                       in_=PSB[0:8, 0:128], reads=[bPB], writes=[b_e1m])
                dve.op("scalar_tensor_tensor", out=eq_s[:, eb + 8:eb + 16],
                       in0=eq_s[:, eb:eb + 8], scalar=-1e30,
                       in1=lg_s[:, j * 8:(j + 1) * 8],
                       op0=ALU.mult, op1=ALU.add,
                       reads=[beq, b_lg], writes=[beq])
                dve.op("reduce_max", out=mx_s[:, NT + j:NT + j + 1],
                       in_=eq_s[:, eb + 8:eb + 16], axis=AX.X, reads=[beq],
                       writes=[b_mx])

            p0a(0)
            if NT > 1:
                p0a(1)
            for j in range(NT):
                p0b(j)
                rt1(j)
                if j + 2 < NT:
                    p0a(j + 2)
                prep_pump(2)

            dve.op("tensor_tensor", out=mx_s[:, 2 * NT:3 * NT],
                   in0=mx_s[:, 0:NT], in1=mx_s[:, NT:2 * NT],
                   op=ALU.subtract, reads=[b_mx], writes=[b_mx])
            act.op("activation", out=mx_s[:, 2 * NT:3 * NT],
                   in_=mx_s[:, 2 * NT:3 * NT], func=AF.Sigmoid,
                   reads=[b_mx], writes=[b_mx])
            dve.op("tensor_scalar", out=mx_s[:, 3 * NT:4 * NT],
                   in0=mx_s[:, 2 * NT:3 * NT], scalar1=-1.0, scalar2=1.0,
                   op0=ALU.mult, op1=ALU.add, reads=[b_mx], writes=[b_mx])
            for j in range(NT):
                par = j % 2
                beq = b_eq2[par]
                eb = par * 16
                bct = b_ct2[par]
                ct = ct_s[:, par * 8:par * 8 + 8]
                dve.op("tensor_scalar", out=eq_s[:, eb:eb + 8],
                       in0=lg_s[:, j * 8:(j + 1) * 8],
                       scalar1=mx_s[:, j:j + 1],
                       scalar2=mx_s[:, 2 * NT + j:2 * NT + j + 1],
                       op0=ALU.is_equal, op1=ALU.mult,
                       reads=[b_lg, b_mx], writes=[beq])
                dve.op("tensor_scalar", out=eq_s[:, eb + 8:eb + 16],
                       in0=lg_s[:, j * 8:(j + 1) * 8],
                       scalar1=mx_s[:, NT + j:NT + j + 1],
                       scalar2=mx_s[:, 3 * NT + j:3 * NT + j + 1],
                       op0=ALU.is_equal, op1=ALU.mult,
                       reads=[b_lg, b_mx], writes=[beq])
                dve.op("tensor_tensor", out=ct, in0=eq_s[:, eb:eb + 8],
                       in1=eq_s[:, eb + 8:eb + 16], op=ALU.add, reads=[beq],
                       writes=[bct])
                kb.mm_group([("transpose", (PSB[0:8, 0:128], ct,
                                            ident_sb[:]), {})],
                            reads=[bct, b_ident], writes=[bPB])
                dve.op("tensor_copy",
                       out=c_rows[0:8, j * 128:(j + 1) * 128],
                       in_=PSB[0:8, 0:128], reads=[bPB], writes=[b_crows])
                io.op("dma_start",
                      out=xn_dram[j * 128:(j + 1) * 128, D:D + 8],
                      in_=ct, reads=[bct], writes=[b_xnd],
                      dsem=ds_xnd)
                prep_pump(1)

            dve.op("memset", sq_s[:], 0.0, writes=[b_sq])
            for j in range(NT):
                io.op("dma_start", out=y_dram[j * 128:(j + 1) * 128, :],
                      in_=sq_s[:], reads=[b_sq], writes=[b_yd], dsem=ds_yz)

            if dbg == "c":
                gio.op("dma_start",
                       out=out_flat[0:8 * n_tok].rearrange(
                           "(p t) -> p t", p=8),
                       in_=c_rows[:], reads=[b_crows], dsem=ds_dbg)
                kb.finish()
                with nc.Block() as block:
                    kb.flush(block)
                return nc

            dve.op("tensor_scalar", out=mask_s[:], in0=c_rows[:],
                   scalar1=0.0, scalar2=None, op0=ALU.is_gt,
                   reads=[b_crows], writes=[b_mask])
            dve.op("tensor_tensor_scan", out=incl_s[:], data0=mask_s[:],
                   data1=mask_s[:], initial=0.0, op0=ALU.add, op1=ALU.max,
                   reads=[b_mask], writes=[b_incl])
            dve.op("tensor_tensor", out=incl_s[:], in0=incl_s[:],
                   in1=mask_s[:], op=ALU.subtract,
                   reads=[b_incl, b_mask], writes=[b_incl])
            dve.op("tensor_scalar", out=incl_s[:], in0=incl_s[:],
                   scalar1=8.0, scalar2=iota8_sb[:], op0=ALU.mult,
                   op1=ALU.add, reads=[b_incl, b_io8], writes=[b_incl])
            dve.op("tensor_tensor", out=eq2em[:], in0=mask_s[:],
                   in1=eq1em[:], op=ALU.subtract,
                   reads=[b_mask, b_e1m], writes=[b_e2m])
            for choice in range(2):
                if choice == 0:
                    dve.op("tensor_tensor", out=prod_s[:], in0=incl_s[:],
                           in1=eq1em[:], op=ALU.mult,
                           reads=[b_incl, b_e1m], writes=[b_prod])
                else:
                    dve.op("tensor_tensor", out=prod_s[:], in0=incl_s[:],
                           in1=eq2em[:], op=ALU.mult,
                           reads=[b_incl, b_e2m], writes=[b_prod])
                PCS = min(512, n_tok)
                for piece in range(n_tok // PCS):
                    pb = [bP1a, bP1b, bP2a, bP2b][piece % 4]
                    base = [0, 512, 1024, 1536][piece % 4]
                    pps = (PS1 if base < 1024 else PS2)[0:1,
                            base % 1024:base % 1024 + PCS]
                    kb.mm_group(
                        [("matmul", (pps, ones8_sb[:],
                                     prod_s[0:8,
                                            piece * PCS:(piece + 1) * PCS]),
                          dict(start=True, stop=True))],
                        reads=[b_on8, b_prod], writes=[pb])
                    dve.op("tensor_copy",
                           out=posg[0:1, choice * n_tok + piece * PCS:
                                    choice * n_tok + (piece + 1) * PCS],
                           in_=pps, reads=[pb], writes=[b_posg])
                prep_pump(3)
            for ch in range(2):
                for j in range(NT):
                    kb.mm_group(
                        [("matmul", (PS4[:, ch * NT + j:ch * NT + j + 1],
                                     posg[0:1, ch * n_tok + j * 128:
                                          ch * n_tok + (j + 1) * 128],
                                     ones11f_sb[:]),
                          dict(start=True, stop=True))],
                        reads=[b_posg, b_ones11f], writes=[bP4])
            dve.op("tensor_copy", out=poscol[:, 0:2 * NT],
                   in_=PS4[:, 0:2 * NT], reads=[bP4], writes=[b_pc])

            for k in range(2 * NT):
                gio.op("indirect_dma_start", out=idx_dram,
                       out_offset=IndirectOffsetOnAxis(
                           poscol[:, k:k + 1], 0),
                       in_=iota_sb[:, (k % NT):(k % NT) + 1],
                       in_offset=None, bounds_check=rb_slot,
                       oob_is_err=False,
                       reads=[b_pc, b_iot], writes=[b_idxd4[k % 4]],
                       dsem=ds_sc4[k % 4])
            gio.op("dma_start",
                   out=idxsb.rearrange("p (b e) -> p b e", b=NBLK),
                   in_=idx_dram.rearrange("(b p e) o -> p b (e o)",
                                          p=128, b=NBLK),
                   reads=b_idxd4, writes=[b_idx], dsem=ds_sc)

            while pp_i[0] < len(prep_units):
                prep_units[pp_i[0]]()
                pp_i[0] += 1

        kb.barrier(skip=(kb.gio,))

        with ExitStack() as SB:
            EB = SB.enter_context
            tA_s, b_tA = mk(EB, "tA_s", [128, D], F32)
            tA3 = tA_s.rearrange("p (c t) -> p c t", c=C)
            tB_s, b_tB = mk(EB, "tB_s", [128, D], F32)
            ni_s, b_ni = mk(EB, "ni_s", [128, D], I32)
            sin2_s, _bs = mk(EB, "sin2_s", [128, 2 * D], BF16)
            cos2_s, _bc = mk(EB, "cos2_s", [128, 2 * D], BF16)
            sink2_s, _sk = mk(EB, "sink2_s", [128, 2 * D], BF16)
            cosk2_s, _ck = mk(EB, "cosk2_s", [128, 2 * D], BF16)
            sink_p = [sink2_s[:, 0:D], sink2_s[:, D:2 * D]]
            cosk_p = [cosk2_s[:, 0:D], cosk2_s[:, D:2 * D]]
            b_sink_p = [Buf("sinkA"), Buf("sinkB")]
            b_cosk_p = [Buf("coskA"), Buf("coskB")]
            sin_p = [sin2_s[:, 0:D], sin2_s[:, D:2 * D]]
            cos_p = [cos2_s[:, 0:D], cos2_s[:, D:2 * D]]
            b_sin_p = [Buf("sinA"), Buf("sinB")]
            b_cos_p = [Buf("cosA"), Buf("cosB")]
            vret_s, b_vret = mk(EB, "vret_s", [128, D], BF16)
            mrq_s, b_mrq = mk(EB, "mrq_s", [128, D], BF16)
            memi_s, b_memi = mk(EB, "memi_s", [128, D], BF16)
            rs_s, b_rs = mk(EB, "rs_s", [128, D], BF16)
            xn_t, _ = mk(EB, "xn_t", [128, 2 * C * 128], BF16)
            xn_t3 = [xn_t[:, 0:C * 128].rearrange("p (c t) -> p c t", c=C),
                     xn_t[:, C * 128:].rearrange("p (c t) -> p c t", c=C)]
            b_xnt2 = [Buf("xntA"), Buf("xntB")]
            xnt_tok, _ = mk(EB, "xnt_tok", [128, 2 * D], BF16)
            b_xtok2 = [Buf("xtokA"), Buf("xtokB")]
            row_s, b_row = mk(EB, "row_s", [1, 6 * 128], F32)
            rbf_s, b_rbf = mk(EB, "rbf_s", [1, 2 * 128], BF16)
            s_stage, b_sst = mk(EB, "s_stage", [128, D], F32)

            pad_s, _ = mk(EB, "pad_s", [128, 1024], F32)
            xg_tok, b_xg = mk(EB, "xg_tok", [128, NBLK * (D + 8)], BF16)
            xg3 = xg_tok.rearrange("p (b n) -> p b n", b=NBLK)
            xg_fm, b_xgf = mk(EB, "xg_fm", [128, C * MOE_CAP], BF16)
            xf3 = xg_fm.rearrange("p (c t) -> p c t", c=C)
            yb3 = xf3
            h_sb, b_h = mk(EB, "h_sb", [128, MHC * MOE_CAP], BF16)
            h3 = h_sb.rearrange("p (m t) -> p m t", m=MHC)
            ysc_s, _ = mk(EB, "ysc_s", [128, 2 * D], F32)
            b_ysc2 = [Buf("yscA"), Buf("yscB")]
            ccol_s, _ = mk(EB, "ccol_s", [128, 2 * NBLK], F32)
            b_cc2 = [Buf("ccA"), Buf("ccB")]
            w1a, b_w1a = mk(EB, "w1a", [128, C * 256], BF16)
            w1b, b_w1b = mk(EB, "w1b", [128, C * 256], BF16)
            w2a, b_w2a = mk(EB, "w2a", [128, MHC * 128], BF16)
            w2b, b_w2b = mk(EB, "w2b", [128, MHC * 128], BF16)
            w1bufs = [(w1a.rearrange("p (c n) -> p c n", c=C), b_w1a),
                      (w1b.rearrange("p (c n) -> p c n", c=C), b_w1b)]
            w2bufs = [(w2a.rearrange("p (m n) -> p m n", m=MHC), b_w2a),
                      (w2b.rearrange("p (m n) -> p m n", m=MHC), b_w2b)]
            NP = [(0, 512), (512, MOE_CAP - 512)]
            def trig(kind, par, kp=False):
                if kp:
                    b_sinv, b_cosv = b_sink_p[par], b_cosk_p[par]
                    sin_ap, cos_ap = sink_p[par], cosk_p[par]
                else:
                    b_sinv, b_cosv = b_sin_p[par], b_cos_p[par]
                    sin_ap, cos_ap = sin_p[par], cos_p[par]
                if kind == "bcast":
                    dve.op("tensor_tensor", out=tA_s[:], in0=tA_s[:],
                           in1=psb_bc[:], op=ALU.mult,
                           reads=[b_tA, b_psbb], writes=[b_tA])
                else:
                    dve.op("tensor_tensor", out=tA3, in0=tA3,
                           in1=psb_col[:, :].unsqueeze(2).broadcast_to(
                               [128, C, 128]),
                           op=ALU.mult, reads=[b_tA, b_psbc], writes=[b_tA])
                dve.op("tensor_copy", out=ni_s[:], in_=tA_s[:],
                       reads=[b_tA], writes=[b_ni])
                dve.op("tensor_copy", out=tB_s[:], in_=ni_s[:],
                       reads=[b_ni], writes=[b_tB])
                dve.op("tensor_tensor", out=tA_s[:], in0=tA_s[:], in1=tB_s[:],
                       op=ALU.subtract, reads=[b_tA, b_tB], writes=[b_tA])
                dve.op("tensor_scalar", out=tB_s[:], in0=tA_s[:], scalar1=0.0,
                       scalar2=None, op0=ALU.is_lt, reads=[b_tA], writes=[b_tB])
                dve.op("scalar_tensor_tensor", out=tA_s[:], in0=tB_s[:],
                       scalar=1.0, in1=tA_s[:], op0=ALU.mult, op1=ALU.add,
                       reads=[b_tA, b_tB], writes=[b_tA])
                act.op("activation", out=sin_ap, in_=tA_s[:], func=AF.Sin,
                       bias=NEG_PI, scale=2 * np.pi,
                       reads=[b_tA, b_cpool], writes=[b_sinv])
                dve.op("tensor_scalar", out=tB_s[:], in0=tA_s[:], scalar1=0.75,
                       scalar2=None, op0=ALU.is_ge, reads=[b_tA], writes=[b_tB])
                dve.op("scalar_tensor_tensor", out=tA_s[:], in0=tB_s[:],
                       scalar=-1.0, in1=tA_s[:], op0=ALU.mult, op1=ALU.add,
                       reads=[b_tA, b_tB], writes=[b_tA])
                act.op("activation", out=cos_ap, in_=tA_s[:], func=AF.Sin,
                       bias=NEG_HPI, scale=2 * np.pi,
                       reads=[b_tA, b_cpool], writes=[b_cosv])

            def xt_load(j):
                par = j % 2
                jt = slice(j * 128, (j + 1) * 128)
                io.op("dma_start", out=xnt_tok[:, par * D:(par + 1) * D],
                      in_=xn_dram[jt, 0:D], reads=[b_xnd],
                      writes=[b_xtok2[par]], dsem=ds_xr[par])

            def u_xt(j):
                par = j % 2
                if j + 1 < NT:
                    xt_load(j + 1)
                kb.mm_group(
                    [("transpose", (PSB[:, c * 128:(c + 1) * 128],
                                    xnt_tok[:, par * D + c * 128:
                                            par * D + (c + 1) * 128],
                                    ident_sb[:]), {}) for c in range(C)],
                    reads=[b_xtok2[par], b_ident], writes=[bPB])
                act.op("activation", out=xn_t3[par],
                       in_=PSB.rearrange("p (c t) -> p c t", c=C),
                       func=AF.Copy, reads=[bPB], writes=[b_xnt2[par]])

            def u_v(j):
                par = j % 2
                for h in range(2):
                    hs = slice(h * 512, (h + 1) * 512)
                    kb.mm_group(
                        [("matmul", (PS3[:, hs], ones1_sb[:], bv_row[0:1, hs]),
                          dict(start=True, stop=False))] +
                        [("matmul", (PS3[:, hs], xn_t3[par][:, kc, :],
                                     Wv3[:, kc, hs]),
                          dict(start=False, stop=(kc == C - 1)))
                         for kc in range(C)],
                        reads=[b_ones1, b_r65, b_xnt2[par], b_wv2],
                        writes=[bP3a if h == 0 else bP3b])
                act.op("activation", out=vret_s[:], in_=PS3[:, 0:1024],
                       func=AF.Copy, reads=[bP3a, bP3b], writes=[b_vret])

            def u_kp(j):
                par = j % 2
                for h in range(2):
                    hs = slice(h * 512, (h + 1) * 512)
                    kb.mm_group(
                        [("matmul", (PS3[:, hs], ones1_sb[:],
                                     bkp_row[0:1, hs]),
                          dict(start=True, stop=False))] +
                        [("matmul", (PS3[:, hs], xn_t3[par][:, kc, :],
                                     Wkp3[:, kc, hs]),
                          dict(start=False, stop=(kc == C - 1)))
                         for kc in range(C)],
                        reads=[b_ones1, b_r65, b_xnt2[par], b_wkpc],
                        writes=[bP3a if h == 0 else bP3b])
                act.op("activation", out=tA_s[:], in_=PS3[:, 0:1024],
                       func=AF.Tanh, reads=[bP3a, bP3b], writes=[b_tA])
                trig("bcast", par, kp=True)
                dve.op("tensor_tensor", out=cosk_p[par], in0=cosk_p[par],
                       in1=vret_s[:], op=ALU.mult,
                       reads=[b_cosk_p[par], b_vret], writes=[b_cosk_p[par]])
                dve.op("tensor_tensor", out=sink_p[par], in0=sink_p[par],
                       in1=vret_s[:], op=ALU.mult,
                       reads=[b_sink_p[par], b_vret], writes=[b_sink_p[par]])

            def u_qp(j):
                par = j % 2
                for c in range(C):
                    cc = slice(c * 128, (c + 1) * 128)
                    kb.mm_group(
                        [("matmul", (PS3[:, cc], bqp_row[0:1, cc],
                                     ones1_sb[:]),
                          dict(start=True, stop=False))] +
                        [("matmul", (PS3[:, cc],
                                     Wqp3[:, kc, cc], xn_t3[par][:, kc, :]),
                          dict(start=False, stop=(kc == C - 1)))
                         for kc in range(C)],
                        reads=[b_r65, b_ones1, b_wqpc, b_xnt2[par]],
                        writes=[bP3a if c < 4 else bP3b])
                act.op("activation", out=tA_s[:], in_=PS3[:, 0:1024],
                       func=AF.Tanh, reads=[bP3a, bP3b], writes=[b_tA])
                trig("col", par)

            def u_t1r(j):
                par = j % 2
                kb.mm_group(
                    [("matmul", (PS3[:, c * 128:(c + 1) * 128],
                                 cosk_p[par][:, c * 128:(c + 1) * 128],
                                 cumu_sb[:]),
                      dict(start=True, stop=True)) for c in range(C)],
                    reads=[b_cosk_p[par], b_cumu], writes=[bP3a, bP3b])
                act.op("activation", out=mrq_s[:], in_=PS3[:, 0:1024],
                       func=AF.Copy, reads=[bP3a, bP3b], writes=[b_mrq])

            def u_t1i(j):
                par = j % 2
                kb.mm_group(
                    [("matmul", (PS3[:, c * 128:(c + 1) * 128],
                                 sink_p[par][:, c * 128:(c + 1) * 128],
                                 cumu_sb[:]),
                      dict(start=True, stop=True)) for c in range(C)],
                    reads=[b_sink_p[par], b_cumu], writes=[bP3a, bP3b])
                act.op("activation", out=memi_s[:], in_=PS3[:, 0:1024],
                       func=AF.Copy, reads=[bP3a, bP3b], writes=[b_memi])

            def u_t2a(j):
                par = j % 2
                dve.op("tensor_tensor", out=rs_s[:], in0=cos_p[par],
                       in1=mrq_s[:], op=ALU.mult,
                       reads=[b_cos_p[par], b_mrq], writes=[b_rs])
                dve.op("tensor_tensor", out=mrq_s[:], in0=sin_p[par],
                       in1=memi_s[:], op=ALU.mult,
                       reads=[b_sin_p[par], b_memi], writes=[b_mrq])
                dve.op("tensor_tensor", out=rs_s[:], in0=rs_s[:],
                       in1=mrq_s[:], op=ALU.add,
                       reads=[b_rs, b_mrq], writes=[b_rs])
                act.op("activation", out=mrq_s[:], in_=rs_s[:],
                       func=AF.Square, reads=[b_rs], writes=[b_mrq])
                kb.mm_group(
                    [("matmul", (PS4[0:1, 0:128], onescb_sb[:],
                                 rs_s[:, c * 128:(c + 1) * 128]),
                      dict(start=(c == 0), stop=(c == C - 1))) for c in range(C)],
                    reads=[b_onescb, b_rs], writes=[bP4])
                kb.mm_group(
                    [("matmul", (PS4[0:1, 128:256], onescb_sb[:],
                                 mrq_s[:, c * 128:(c + 1) * 128]),
                      dict(start=(c == 0), stop=(c == C - 1))) for c in range(C)],
                    reads=[b_onescb, b_mrq], writes=[bP4])
                dve.op("tensor_scalar", out=row_s[0:1, 0:256],
                       in0=PS4[0:1, 0:256], scalar1=1.0 / D, scalar2=None,
                       op0=ALU.mult, reads=[bP4], writes=[b_row])
                dve.op("tensor_tensor", out=row_s[0:1, 256:384],
                       in0=row_s[0:1, 0:128], in1=row_s[0:1, 0:128],
                       op=ALU.mult, reads=[b_row], writes=[b_row])
                dve.op("scalar_tensor_tensor", out=row_s[0:1, 256:384],
                       in0=row_s[0:1, 256:384], scalar=-1.0,
                       in1=row_s[0:1, 128:256], op0=ALU.mult, op1=ALU.add,
                       reads=[b_row], writes=[b_row])
                act.op("activation", out=row_s[0:1, 384:512],
                       in_=row_s[0:1, 256:384], func=AF.Sqrt, bias=EPS_B,
                       reads=[b_row, b_cpool], writes=[b_row])
                dve.op("reciprocal", out=row_s[0:1, 512:640],
                       in_=row_s[0:1, 384:512], reads=[b_row], writes=[b_row])
                dve.op("scalar_tensor_tensor", out=row_s[0:1, 640:768],
                       in0=row_s[0:1, 0:128], scalar=-1.0,
                       in1=row_s[0:1, 512:640], op0=ALU.mult, op1=ALU.mult,
                       reads=[b_row], writes=[b_row])
                dve.op("tensor_copy", out=rbf_s[:, 0:256],
                       in_=row_s[0:1, 512:768], reads=[b_row], writes=[b_rbf])

            def u_t2b(j):
                par = j % 2
                jt = slice(j * 128, (j + 1) * 128)
                kb.mm_group(
                    [("matmul", (PS4[:, 256:384], ones1_sb[:],
                                 rbf_s[:, 0:128]), dict(start=True, stop=True))],
                    reads=[b_ones1, b_rbf], writes=[bP4])
                dve.op("tensor_tensor",
                       out=rs_s.rearrange("p (c t) -> p c t", c=C),
                       in0=rs_s.rearrange("p (c t) -> p c t", c=C),
                       in1=PS4[:, 256:384].unsqueeze(1).broadcast_to(
                           [128, C, 128]),
                       op=ALU.mult, reads=[b_rs, bP4], writes=[b_rs])
                for h in range(2):
                    hs = slice(h * 512, (h + 1) * 512)
                    mms = [("matmul", (PS3[:, hs], rbf_s[:, 128:256],
                                       cs_row[0:1, hs]),
                            dict(start=True, stop=False)),
                           ("matmul", (PS3[:, hs], ones1_sb[:],
                                       bo_row[0:1, hs]),
                            dict(start=False, stop=False))]
                    for i in range(4):
                        c2 = 4 * h + i
                        mms.append(("matmul",
                                    (PS3[:, c2 * 128:(c2 + 1) * 128],
                                     xn_t3[par][:, c2, :], ident_sb[:]),
                                    dict(start=False, stop=False)))
                    mms += [("matmul", (PS3[:, hs],
                                        rs_s[:, kc * 128:(kc + 1) * 128],
                                        Wo3[:, kc, hs]),
                             dict(start=False, stop=(kc == C - 1)))
                            for kc in range(C)]
                    kb.mm_group(
                        mms,
                        reads=[b_rbf, b_r33, b_ones1, b_xnt2[par], b_ident,
                               b_rs, b_wo2],
                        writes=[bP3a if h == 0 else bP3b])
                act.op("activation", out=s_stage[:], in_=PS3[:, 0:1024],
                       func=AF.Copy, reads=[bP3a, bP3b], writes=[b_sst])
                io.op("dma_start", out=s_dram[jt, :], in_=s_stage[:],
                      reads=[b_sst], writes=[b_sd[j]], dsem=ds_ss)

            units = []
            units += [lambda: xt_load(0), lambda j=0: u_xt(j),
                      lambda j=0: u_qp(j), lambda j=0: u_v(j),
                      lambda j=0: u_kp(j)]
            if NT > 1:
                units += [lambda j=1: u_xt(j), lambda j=1: u_qp(j),
                          lambda j=1: u_v(j)]
            for j in range(2, NT):
                units += [lambda j=j: u_t1r(j - 2), lambda j=j: u_t1i(j - 2),
                          lambda j=j: u_t2a(j - 2), lambda j=j: u_t2b(j - 2),
                          lambda j=j: u_kp(j - 1),
                          lambda j=j: u_xt(j), lambda j=j: u_qp(j),
                          lambda j=j: u_v(j)]
            units += [lambda j=NT - 1: u_kp(j)]
            for j in (NT - 2, NT - 1):
                units += [lambda j=j: u_t1r(j), lambda j=j: u_t1i(j),
                          lambda j=j: u_t2a(j), lambda j=j: u_t2b(j)]

            total_steps = NE * (1 + NBLK + MHC + C + NBLK)
            stc = [0, 0]

            def stepv(pump=True):
                stc[0] += 1
                if not pump:
                    return
                tgt = len(units) * stc[0] // total_steps
                if stc[1] < min(tgt, len(units)):
                    units[stc[1]]()
                    stc[1] += 1

            def w1_load(e_, k):
                buf3, bb = w1bufs[k % 2]
                gio.op("dma_start", out=buf3,
                       in_=ap["w1"][e_].rearrange("(c p) h -> p c h", p=128)
                       [:, :, k * 256:(k + 1) * 256],
                       writes=[bb], dsem=ds_w1[k % 2])

            def w2_load(e_, fo):
                buf3, bb = w2bufs[fo % 2]
                gio.op("dma_start", out=buf3,
                       in_=ap["w2"][e_].rearrange("(m p) f -> p m f", p=128)
                       [:, :, fo * 128:(fo + 1) * 128],
                       writes=[bb], dsem=ds_w2[fo % 2])

            def gather_blocks(e_, blist):
                for b_ in blist:
                    gio.op("indirect_dma_start", out=xg3[:, b_, :],
                           out_offset=None, in_=xn_dram,
                           in_offset=IndirectOffsetOnAxis(
                               idxsb[:, b_ * 8 + e_:b_ * 8 + e_ + 1], 0),
                           bounds_check=rb_tok, oob_is_err=False,
                           reads=[b_idx, b_xnd], writes=[b_xg],
                           dsem=ds_xg[b_ % 2])

            def ccol_copy(e_):
                cpar = e_ % 2
                dve.op("tensor_copy",
                       out=ccol_s[:, cpar * NBLK:(cpar + 1) * NBLK],
                       in_=xg3[:, :, D + e_], reads=[b_xg],
                       writes=[b_cc2[cpar]])

            def ti_block(b_):
                kb.mm_group(
                    [("transpose", (PSB[:, c * 128:(c + 1) * 128],
                                    xg3[:, b_, c * 128:(c + 1) * 128],
                                    ident_sb[:]), {}) for c in range(C)],
                    reads=[b_xg, b_ident], writes=[bPB])
                act.op("activation",
                       out=xf3[:, :, b_ * 128:(b_ + 1) * 128],
                       in_=PSB.rearrange("p (c t) -> p c t", c=C),
                       func=AF.Copy, reads=[bPB], writes=[b_xgf])

            gather_blocks(0, range(NBLK))
            w1_load(0, 0)
            w1_load(0, 1)
            ccol_copy(0)
            stepv()
            for b_ in range(NBLK):
                ti_block(b_)
                stepv()
            ds_ys2 = [ds_ys, ds_ysb]
            sc_last = [None, None]
            for e_ in range(NE):
                for m in range(MHC):
                    k8 = m // 2
                    buf3, bb = w1bufs[k8 % 2]
                    pp, pbs = ((PS1, (bP1a, bP1b)) if m % 2 == 0
                               else (PS2, (bP2a, bP2b)))
                    kb.mm_group(
                        [("matmul", (pp[:, o_:o_ + n_],
                                     buf3[:, kc, (m % 2) * 128:(m % 2 + 1) * 128],
                                     xf3[:, kc, o_:o_ + n_]),
                          dict(start=(kc == 0), stop=(kc == C - 1)))
                         for (o_, n_) in NP for kc in range(C)],
                        reads=[bb, b_xgf], writes=list(pbs))
                    act.op("activation", out=h3[:, m, :],
                           in_=pp[:, 0:MOE_CAP], func=AF.Silu,
                           reads=list(pbs), writes=[b_h])
                    if m % 2 == 1 and k8 + 2 <= 7:
                        w1_load(e_, k8 + 2)
                    if m == 13:
                        w2_load(e_, 0)
                    if m == 15:
                        w2_load(e_, 1)
                    stepv()
                for fo in range(C):
                    buf3, bb = w2bufs[fo % 2]
                    ypp, ypbs = ((PS1, (bP1a, bP1b)) if fo % 2 == 0
                                 else (PS2, (bP2a, bP2b)))
                    kb.mm_group(
                        [("matmul", (ypp[:, o_:o_ + n_],
                                     buf3[:, m, :],
                                     h3[:, m, o_:o_ + n_]),
                          dict(start=(m == 0), stop=(m == MHC - 1)))
                         for (o_, n_) in NP for m in range(MHC)],
                        reads=[bb, b_h], writes=list(ypbs))
                    act.op("activation", out=yb3[:, fo, :],
                           in_=ypp[:, 0:MOE_CAP], func=AF.Copy,
                           reads=list(ypbs), writes=[b_xgf])
                    if fo + 2 < C:
                        w2_load(e_, fo + 2)
                    if e_ + 1 < NE and fo in (1, 3, 5):
                        gather_blocks(e_ + 1,
                                      [b for b in (fo - 1, fo) if b < NBLK])
                    if e_ + 1 < NE and fo == 6:
                        w1_load(e_ + 1, 0)
                    if e_ + 1 < NE and fo == 7:
                        w1_load(e_ + 1, 1)
                    stepv()
                if e_ + 1 < NE:
                    ccol_copy(e_ + 1)
                    stepv()
                for b_ in range(NBLK):
                    ypar = b_ % 2
                    ysv = ysc_s[:, ypar * D:(ypar + 1) * D]
                    kb.mm_group(
                        [("transpose", (PSB[:, c * 128:(c + 1) * 128],
                                        yb3[:, c, b_ * 128:(b_ + 1) * 128],
                                        ident_sb[:]), {}) for c in range(C)],
                        reads=[b_xgf, b_ident], writes=[bPB])
                    dve.op("tensor_scalar", out=ysv, in0=PSB[:, 0:1024],
                           scalar1=ccol_s[:, (e_ % 2) * NBLK + b_:
                                          (e_ % 2) * NBLK + b_ + 1],
                           scalar2=None,
                           op0=ALU.mult, reads=[bPB, b_cc2[e_ % 2]],
                           writes=[b_ysc2[ypar]])
                    deps = []
                    if b_ < 2 and sc_last[1 - ypar] is not None:
                        deps = [sc_last[1 - ypar]]
                    gio.op("indirect_dma_start", out=y_dram,
                           out_offset=IndirectOffsetOnAxis(
                               idxsb[:, b_ * 8 + e_:b_ * 8 + e_ + 1], 0),
                           in_=ysv, in_offset=None,
                           bounds_check=rb_tok, oob_is_err=False,
                           compute_op=ALU.add, deps=deps,
                           reads=[b_ysc2[ypar], b_idx], writes=[b_yd],
                           dsem=ds_ys2[ypar])
                    stepv(pump=False)
                    if e_ + 1 < NE:
                        ti_block(b_)
                        stepv(pump=False)
                sc_last = [(ds_ys2[0], ds_ys2[0].n),
                           (ds_ys2[1], ds_ys2[1].n)]

            while stc[1] < len(units):
                units[stc[1]]()
                stc[1] += 1

            if dbg == "y":
                for j in range(NT):
                    io.op("dma_start", out=out[j * 128:(j + 1) * 128, :],
                          in_=y_dram[j * 128:(j + 1) * 128, :],
                          reads=[b_yd], dsem=ds_st[j % 2])
                kb.finish()
                with nc.Block() as block:
                    kb.flush(block)
                return nc

        kb.barrier()

        with ExitStack() as SC:
            EC = SC.enter_context
            s_t, _ = mk(EC, "s_t", [128, 4 * D], F32)
            b_s2 = [Buf("s%d" % i) for i in range(4)]
            y_t, _ = mk(EC, "y_t", [128, 4 * D], F32)
            b_y2 = [Buf("y%d" % i) for i in range(4)]
            x_t, _ = mk(EC, "x_t", [128, 4 * D], F32)
            b_x2 = [Buf("x%d" % i) for i in range(4)]
            g_te, _ = mk(EC, "g_te", [128, 4 * D], BF16)
            b_g2 = [Buf("g%d" % i) for i in range(4)]
            def ep_load(j):
                par = j % 4
                jt = slice(j * 128, (j + 1) * 128)
                io.op("dma_start", out=s_t[:, par * D:(par + 1) * D],
                      in_=s_dram[jt, :],
                      reads=[b_sd[j]], writes=[b_s2[par]], dsem=ds_ep[0][par])
                gio.op("dma_start", out=y_t[:, par * D:(par + 1) * D],
                       in_=y_dram[jt, :], reads=[b_yd],
                       writes=[b_y2[par]], dsem=ds_ep[1][par])
                gio.op("dma_start", out=x_t[:, par * D:(par + 1) * D],
                       in_=ap["x"][jt, :],
                       writes=[b_x2[par]], dsem=ds_ep[2][par])

            for j in range(min(4, NT)):
                ep_load(j)
            for j in range(NT):
                par = j % 4
                jt = slice(j * 128, (j + 1) * 128)
                sv = s_t[:, par * D:(par + 1) * D]
                yv = y_t[:, par * D:(par + 1) * D]
                xv = x_t[:, par * D:(par + 1) * D]
                gv = g_te[:, par * D:(par + 1) * D]
                act.op("activation", out=gv, in_=yv, func=AF.Silu,
                       reads=[b_y2[par]], writes=[b_g2[par]])
                dve.op("tensor_tensor", out=sv, in0=sv, in1=gv,
                       op=ALU.mult, reads=[b_s2[par], b_g2[par]],
                       writes=[b_s2[par]])
                dve.op("tensor_tensor", out=xv, in0=xv, in1=sv,
                       op=ALU.add, reads=[b_x2[par], b_s2[par]],
                       writes=[b_x2[par]])
                io.op("dma_start", out=out[jt, :], in_=xv,
                      reads=[b_x2[par]], dsem=ds_ep[3][par])
                if j + 4 < NT:
                    ep_load(j + 4)

        kb.finish()
        with nc.Block() as block:
            kb.flush(block)
        return nc



_PROG_CACHE = {}


def _get_prog(n_tok):
    if n_tok not in _PROG_CACHE:
        _PROG_CACHE[n_tok] = build_program(n_tok)
    return _PROG_CACHE[n_tok]


def kernel(**inputs):
    x = np.ascontiguousarray(
        np.asarray(inputs["L0_attn_x"], dtype=np.float32)).reshape(S_FULL, D)
    w = {
        "Wk": inputs["L0_attn_Wk"], "Wv": inputs["L0_attn_Wv"],
        "Wq": inputs["L0_attn_Wq"], "Wkp": inputs["L0_attn_Wkp"],
        "Wqp": inputs["L0_attn_Wqp"], "Wo": inputs["L0_attn_Wo"],
        "bk": inputs["L0_attn_bk"], "bv": inputs["L0_attn_bv"],
        "bq": inputs["L0_attn_bq"], "bkp": inputs["L0_attn_bkp"],
        "bqp": inputs["L0_attn_bqp"], "ps": inputs["L0_attn_phase_scale"],
        "lng": inputs["L0_attn_ln_g"], "lnb": inputs["L0_attn_ln_b"],
        "bo": inputs["L0_attn_bo"], "gw": inputs["L0_ffn_gate_w"],
        "w1": inputs["L0_ffn_w1"], "w2": inputs["L0_ffn_w2"],
    }
    w = {k: np.ascontiguousarray(np.asarray(v, dtype=np.float32))
         for k, v in w.items()}
    n_per = S_FULL // N_CORES
    nc = _get_prog(n_per)
    in_maps = []
    for c in range(N_CORES):
        m = {"x": x[c * n_per:(c + 1) * n_per]}
        m.update(w)
        in_maps.append(m)
    res = run_bass_kernel_spmd(nc, in_maps, core_ids=list(range(N_CORES)))
    outs = [np.asarray(res.results[c]["out"]) for c in range(N_CORES)]
    return np.concatenate(outs, axis=0).reshape(1, S_FULL, D)
